# version 03
# baseline (speedup 1.0000x reference)
import numpy as np
import ml_dtypes
from contextlib import ExitStack

import concourse.bacc as bacc
import concourse.mybir as mybir
from concourse.bass_utils import run_bass_kernel_spmd

F32 = mybir.dt.float32
BF16 = mybir.dt.bfloat16
I16 = mybir.dt.int16
AF = mybir.ActivationFunctionType
ALU = mybir.AluOpType
AX = mybir.AxisListType

B, S, D = 2, 8192, 1024
H, HD = 16, 64
WIN, G = 1024, 256
SO = 2048
NW = SO // WIN
NT = SO // 128
DC = D // 128
FF = 4096
FC = FF // 128
ONE_MINUS_Q = 255.5 / 8191.0
SCALE = HD ** -0.5


class RInst:
    __slots__ = ("incs",)

    def __init__(self):
        self.incs = []


class Rec:
    def __init__(self):
        self.ops = []

    def op(self, eng, meth, args, kwargs):
        ri = RInst()
        self.ops.append((eng, "op", (meth, args, kwargs, ri)))
        return ri

    def wait(self, eng, sem, val):
        self.ops.append((eng, "wait", (sem, val)))


class REng:
    def __init__(self, rec, name):
        self._rec = rec
        self._name = name

    def wait_ge(self, sem, val):
        self._rec.wait(self._name, sem, val)

    def __getattr__(self, meth):
        if meth.startswith("_"):
            raise AttributeError(meth)

        def call(*args, **kwargs):
            return self._rec.op(self._name, meth, args, kwargs)

        return call


class Ctr:
    def __init__(self, sem):
        self.sem = sem
        self.v = 0

    def inc(self, inst, n=1):
        inst.incs.append((self.sem, n))
        self.v += n
        return self.v

    def wait(self, eng, v=None):
        eng.wait_ge(self.sem, self.v if v is None else v)


DEBUG = False


def build():
    nc = bacc.Bacc("TRN2", target_bir_lowering=False)

    xb = nc.dram_tensor("xb", [S, D], F32, kind="ExternalInput").ap()
    xown_h = nc.dram_tensor("xown", [SO, D], F32, kind="ExternalInput").ap()
    wq_h = nc.dram_tensor("wq", [D, D], F32, kind="ExternalInput").ap()
    wk_h = nc.dram_tensor("wk", [D, D], F32, kind="ExternalInput").ap()
    wv_h = nc.dram_tensor("wv", [D, D], F32, kind="ExternalInput").ap()
    wo_h = nc.dram_tensor("wo", [D, D], F32, kind="ExternalInput").ap()
    w1_h = nc.dram_tensor("w1", [FF, D], F32, kind="ExternalInput").ap()
    w2_h = nc.dram_tensor("w2", [D, FF], F32, kind="ExternalInput").ap()
    b1c_h = nc.dram_tensor("b1c", [128, FC], F32, kind="ExternalInput").ap()
    b2r_h = nc.dram_tensor("b2r", [1, D], F32, kind="ExternalInput").ap()
    nwrep_h = nc.dram_tensor("nwrep", [128, D], F32, kind="ExternalInput").ap()
    nw8_h = nc.dram_tensor("nw8", [128, DC], F32, kind="ExternalInput").ap()
    sw8_h = nc.dram_tensor("sw8", [128, DC], F32, kind="ExternalInput").ap()
    offv_h = nc.dram_tensor("offv", [1, 1], F32, kind="ExternalInput").ap()
    triu_h = nc.dram_tensor("triu", [128, 128], BF16, kind="ExternalInput").ap()
    iota256_h = nc.dram_tensor("iota256", [128, 256], F32, kind="ExternalInput").ap()
    qposr_h = nc.dram_tensor("qposr", [128, 512], F32, kind="ExternalInput").ap()
    cm_h = nc.dram_tensor("cm", [128, 4, 512], BF16, kind="ExternalInput").ap()

    out_h = nc.dram_tensor("out", [SO, D], F32, kind="ExternalOutput").ap()
    if DEBUG:
        dbg_h = nc.dram_tensor("dbg_h", [SO, D], BF16, kind="ExternalOutput").ap()
        dbg_srow = nc.dram_tensor("dbg_srow", [3, S], F32, kind="ExternalOutput").ap()
        dbg_s16 = nc.dram_tensor("dbg_s16", [S], F32, kind="ExternalOutput").ap()
        dbg_idx = nc.dram_tensor("dbg_idx", [256], I16, kind="ExternalOutput").ap()
        dbg_r1 = nc.dram_tensor("dbg_r1", [SO, D], F32, kind="ExternalOutput").ap()
        dbg_x2n = nc.dram_tensor("dbg_x2n", [SO, D], BF16, kind="ExternalOutput").ap()
        dbg_wq1 = nc.dram_tensor("dbg_wq1", [FF, D], BF16, kind="ExternalOutput").ap()
        dbg_wq2 = nc.dram_tensor("dbg_wq2", [D, FF], BF16, kind="ExternalOutput").ap()
        dbg_hg = nc.dram_tensor("dbg_hg", [G, D], BF16, kind="ExternalOutput").ap()
        dbg_sml = nc.dram_tensor("dbg_sml", [128, 96], F32, kind="ExternalOutput").ap()
        dbg_xg = nc.dram_tensor("dbg_xg", [128, 2, D], F32, kind="ExternalOutput").ap()
        dbg_i16 = nc.dram_tensor("dbg_i16", [128, 16], I16, kind="ExternalOutput").ap()
        dbg_wabs = nc.dram_tensor("dbg_wabs", [128, 64], F32, kind="ExternalOutput").ap()

    xh_bf = nc.dram_tensor("xh_bf", [S, D], BF16, kind="Internal").ap()
    h_hbm = nc.dram_tensor("h_hbm", [SO, D], BF16, kind="Internal").ap()
    hg_hbm = nc.dram_tensor("hg_hbm", [G, D], BF16, kind="Internal").ap()
    x2n_hbm = nc.dram_tensor("x2n_hbm", [SO, D], BF16, kind="Internal").ap()
    r1_hbm = nc.dram_tensor("r1_hbm", [SO, D], F32, kind="Internal").ap()
    wq1_hbm = nc.dram_tensor("wq1_hbm", [FF, D], BF16, kind="Internal").ap()
    wq2_hbm = nc.dram_tensor("wq2_hbm", [D, FF], BF16, kind="Internal").ap()
    srow_hbm = nc.dram_tensor("srow_hbm", [3, S], F32, kind="Internal").ap()
    s16_hbm = nc.dram_tensor("s16_hbm", [S], F32, kind="Internal").ap()
    idx_dram = nc.dram_tensor("idx_dram", [256], I16, kind="Internal").ap()
    negc_dram = nc.dram_tensor("negc_dram", [16], F32, kind="Internal").ap()

    st = ExitStack()

    def mk(stack):
        return (lambda n, s, d: stack.enter_context(nc.sbuf_tensor(n + "_s", s, d)))

    sbt = mk(st)
    psf = lambda n, s: st.enter_context(nc.psum_tensor(n + "_p", s, F32))
    mksem = lambda n: Ctr(st.enter_context(nc.semaphore(n)))

    triu = sbt("triu", [128, 128], BF16)
    iota256 = sbt("iota256", [128, 256], F32)
    qposr = sbt("qposr", [128, 512], F32)
    cm = sbt("cm", [128, 4, 512], BF16)
    nwrep = sbt("nwrep", [128, D], F32)
    nw8 = sbt("nw8", [128, DC], F32)
    sw8 = sbt("sw8", [128, DC], F32)
    u8 = sbt("u8", [128, DC], F32)
    u2 = sbt("u2", [128, DC, 2], BF16)
    ones128b = sbt("ones128b", [128, 1], BF16)
    ones128f = sbt("ones128f", [128, 1], F32)
    ones_row = sbt("ones_row", [1, 128], BF16)
    onesA = sbt("onesA", [128, 2], BF16)
    b1c = sbt("b1c", [128, FC], F32)
    b2rb = sbt("b2rb", [1, D], BF16)
    offv = sbt("offv", [1, 1], F32)
    offrep = sbt("offrep", [128, 1], F32)
    stats = sbt("stats", [128, 16], F32)
    sown = sbt("sown", [128, 4, NT], F32)
    lng = sbt("lng", [128, D], F32)
    thr = sbt("thr", [1, 2], F32)
    trep = sbt("trep", [128, 1], F32)
    m_tm = sbt("m_tm", [128, 64], BF16)
    rank3 = sbt("rank3", [128, 64], F32)
    prow = sbt("prow", [1, 64], F32)
    prow_rep = sbt("prow_rep", [128, 64], F32)
    trow = sbt("trow", [1, 64], F32)
    scanb = sbt("scanb", [1, 64], F32)
    ptc = sbt("ptc", [128, 2, 256], BF16)
    rhs2 = sbt("rhs2", [128, 2, 64], BF16)
    idxAB = sbt("idxAB", [128, 4], F32)
    idxval = sbt("idxval", [128, 2], F32)
    idxvi = sbt("idxvi", [128, 2], I16)
    idx16 = sbt("idx16", [128, 16], I16)
    sc_tm = sbt("sc_tm", [128, 64], F32)
    surep = sbt("surep", [16, 1], F32)
    su11 = sbt("su11", [1, 1], F32)
    mqk = sbt("mqk", [2, DC, 4], F32)
    mqs = sbt("mqs", [2, 1], F32)
    negC = sbt("negC", [128, 16], F32)
    negC2 = sbt("negC2", [2, DC], F32)
    negCrow = sbt("negCrow", [1, 16], F32)
    gmsk = sbt("gmsk", [128, 2, 512], BF16)
    ish = sbt("ish", [128, 2], F32)
    wgam = sbt("wgam", [1, 4], F32)
    gacc = sbt("gacc", [1, 64], F32)
    wthr = sbt("wthr", [128, 4], F32)
    wabs = sbt("wabs", [128, 64], F32)
    recrow = sbt("recrow", [65, 512], F32)
    recrow0 = sbt("recrow0", [1, 512], F32)
    recrep = sbt("recrep", [64, 512], F32)

    ps = [psf(f"ps{i}", [128, 512]) for i in range(8)]

    dsy = mksem("dsy")
    dgp = mksem("dgp")
    dsc = mksem("dsc")
    vv = mksem("vv")
    ssm = mksem("ssm")
    tsm = mksem("tsm")
    ggm = mksem("ggm")
    hzv = mksem("hzv")

    rec = Rec()
    SY = REng(rec, "sync")
    V = REng(rec, "vector")
    SC = REng(rec, "scalar")
    TE = REng(rec, "tensor")
    GP = REng(rec, "gpsimd")

    def hbar(inst):
        hzv.inc(inst, 1)
        hzv.wait(V)

    stA = ExitStack()
    sba = mk(stA)
    x_fmh = sba("x_fmh", [128, DC, S // 2], BF16)
    xsq = sba("xsq", [128, DC, 512], BF16)
    xg_sb = sba("xg_sb", [128, 2, D], F32)
    hgt = sba("hgt", [128, 2, D], BF16)
    xow = sba("xow", [128, 2, D], F32)
    how = sba("how", [128, 2, D], BF16)
    srow3 = sba("srow3", [2, 2, 512], F32)
    srow1 = sba("srow1", [1, 2, 512], F32)
    sr_a = sba("sr_a", [16, 512], F32)
    sr_b = sba("sr_b", [16, 512], F32)
    sr_c = sba("sr_c", [16, 512], F32)
    sr_t = sba("sr_t", [16, 6, 512], F32)
    sc16 = sba("sc16", [16, 512], F32)
    wtl = sba("wtl", [128, 4, D], F32)
    wqt = sba("wqt", [128, 2, D], BF16)
    wsc = sba("wsc", [128, 2, D], BF16)
    wsc2 = sba("wsc2", [128, 2, D], BF16)
    wscf = sba("wscf", [128, 2, D], F32)


    for t, src in ((triu, triu_h), (iota256, iota256_h), (qposr, qposr_h),
                   (cm, cm_h), (nwrep, nwrep_h), (nw8, nw8_h), (sw8, sw8_h),
                   (b1c, b1c_h), (offv, offv_h)):
        dsy.inc(SY.dma_start(out=t[:], in_=src), 16)
    LD_CONST = dsy.v
    dgp.inc(GP.dma_start(out=b2rb[:], in_=b2r_h), 16)

    cast_x = {}
    for i_ in range(8):
        dgp.inc(GP.dma_start(out=xh_bf[i_ * 1024:(i_ + 1) * 1024, :],
                             in_=xb[i_ * 1024:(i_ + 1) * 1024, :]), 16)
        cast_x[i_] = dgp.v

    GP.iota(out=rhs2[:, 0, :], pattern=[[1, 64]], base=0, channel_multiplier=0,
            allow_small_or_imprecise_dtypes=True)
    ggm.inc(GP.iota(out=rhs2[:, 1, :], pattern=[[0, 64]], base=0,
                    channel_multiplier=1,
                    allow_small_or_imprecise_dtypes=True), 1)
    IOTA = ggm.v
    GP.wait_ge(dsy.sem, LD_CONST)
    ggm.inc(GP.partition_broadcast(out_ap=offrep[:], in_ap=offv[:],
                                   channels=128), 1)
    OFFREP = ggm.v

    V.memset(ones128b[:], 1.0)
    V.memset(ones128f[:], 1.0)
    V.memset(ones_row[:], 1.0)
    V.memset(onesA[:], 0.0)
    V.memset(wabs[:], 0.0)
    V.memset(onesA[0:64, 0:1], 1.0)
    V.memset(onesA[64:128, 1:2], 1.0)
    V.wait_ge(dsy.sem, LD_CONST)
    hbar(V.tensor_mul(out=u8[:], in0=nw8[:], in1=sw8[:]))
    for dc in range(DC):
        V.memset(u2[:, dc, 0:1], 1.0)
        V.tensor_copy(out=u2[:, dc, 1:2], in_=u8[:, dc:dc + 1])
    vv.inc(V.engine_nop(), 1)
    PREP = vv.v

    TE.wait_ge(vv.sem, PREP)
    i = None
    for dc in range(DC):
        i = TE.matmul(ps[7][0:1, 0:1], u2[:, dc, 1:2], ones128b[:],
                      start=(dc == 0), stop=(dc == DC - 1))
    tsm.inc(i, 1)
    SC.wait_ge(tsm.sem, tsm.v)
    ssm.inc(SC.activation(out=su11[:], in_=ps[7][0:1, 0:1], func=AF.Copy,
                          scale=-1.0), 1)
    SU_EV = ssm.v
    GP.wait_ge(ssm.sem, SU_EV)
    ggm.inc(GP.partition_broadcast(out_ap=surep[:], in_ap=su11[:], channels=16), 1)
    SU_REP = ggm.v

    mm_done = {}
    ev_done = {}
    ev_dma = {}
    xfm_ld = {}
    TE.wait_ge(ssm.sem, SU_EV)
    for ph in range(2):
        SY.wait_ge(dgp.sem, cast_x[4 * ph + 3])
        if ph == 1:
            SY.wait_ge(tsm.sem, mm_done[7])
            SY.wait_ge(ssm.sem, ev_done[7])
        for dc in range(DC):
            dsy.inc(SY.dma_start(out=x_fmh[:, dc, :],
                                 in_=xh_bf[ph * 4096:(ph + 1) * 4096,
                                           dc * 128:(dc + 1) * 128],
                                 transpose=True), 16)
        xfm_ld[ph] = dsy.v
        SC.wait_ge(dsy.sem, xfm_ld[ph])
        TE.wait_ge(dsy.sem, xfm_ld[ph])
        for g8 in range(8):
            gp = ph * 8 + g8
            if gp >= 1:
                SC.wait_ge(tsm.sem, mm_done[gp - 1])
            i = None
            for dc in range(DC):
                i = SC.activation(out=xsq[:, dc, :],
                                  in_=x_fmh[:, dc, g8 * 512:(g8 + 1) * 512],
                                  func=AF.Square)
            ssm.inc(i, 1)
            SQ = ssm.v
            TE.wait_ge(ssm.sem, SQ)
            if gp >= 3:
                TE.wait_ge(ssm.sem, ev_done[gp - 3])
            slot_a = ps[2 * (g8 % 3)]
            slot_b = ps[2 * (g8 % 3) + 1]
            for dc in range(DC):
                s0 = (dc == 0)
                sp = (dc == DC - 1)
                TE.matmul(slot_a[0:2, :], u2[:, dc, :],
                          x_fmh[:, dc, g8 * 512:(g8 + 1) * 512], start=s0, stop=sp)
                i = TE.matmul(slot_b[0:1, :], ones128b[:], xsq[:, dc, :],
                              start=s0, stop=sp)
            tsm.inc(i, 1)
            mm_done[gp] = tsm.v
            SC.wait_ge(tsm.sem, mm_done[gp])
            if gp >= 2:
                SC.wait_ge(dsy.sem, ev_dma[gp - 2])
            SC.copy(out=srow3[0:2, gp % 2, :], in_=slot_a[0:2, :])
            ssm.inc(SC.copy(out=srow1[0:1, gp % 2, :], in_=slot_b[0:1, :]), 1)
            ev_done[gp] = ssm.v
            SY.wait_ge(ssm.sem, ev_done[gp])
            dsy.inc(SY.dma_start(out=srow_hbm[0:2, gp * 512:(gp + 1) * 512],
                                 in_=srow3[0:2, gp % 2, :]), 16)
            dsy.inc(SY.dma_start(out=srow_hbm[2:3, gp * 512:(gp + 1) * 512],
                                 in_=srow1[0:1, gp % 2, :]), 16)
            ev_dma[gp] = dsy.v
    EV_ALL = dsy.v

    SY.wait_ge(dsy.sem, EV_ALL)
    dsy.inc(SY.dma_start(out=sr_b[:], in_=srow_hbm[0].rearrange("(p c) -> p c", p=16)), 16)
    dsy.inc(SY.dma_start(out=sr_a[:], in_=srow_hbm[1].rearrange("(p c) -> p c", p=16)), 16)
    dsy.inc(SY.dma_start(out=sr_c[:], in_=srow_hbm[2].rearrange("(p c) -> p c", p=16)), 16)
    ROWS16 = dsy.v

    V.wait_ge(dsy.sem, ROWS16)
    V.wait_ge(ggm.sem, SU_REP)
    T = sr_t
    hbar(V.tensor_scalar(out=T[:, 0, :], in0=sr_b[:], scalar1=1.0 / D,
                         scalar2=None, op0=ALU.mult))
    hbar(V.tensor_tensor(out=T[:, 1, :], in0=T[:, 0, :], in1=T[:, 0, :],
                         op=ALU.mult))
    hbar(V.scalar_tensor_tensor(out=T[:, 2, :], in0=sr_c[:], scalar=1.0 / D,
                                in1=T[:, 1, :], op0=ALU.mult, op1=ALU.subtract))
    hbar(V.tensor_scalar(out=T[:, 3, :], in0=T[:, 2, :], scalar1=1e-5,
                         scalar2=None, op0=ALU.add))
    vv.inc(V.scalar_tensor_tensor(out=T[:, 4, :], in0=T[:, 3, :], scalar=1e-6,
                                  in1=T[:, 2, :], op0=ALU.mult, op1=ALU.add), 1)
    RM1 = vv.v
    SC.wait_ge(vv.sem, RM1)
    ssm.inc(SC.activation(out=sr_t[:, 5, :], in_=sr_t[:, 4, :], func=AF.Sqrt), 1)
    RM2 = ssm.v
    V.wait_ge(ssm.sem, RM2)
    hbar(V.reciprocal(out=T[:, 4, :], in_=T[:, 5, :]))
    hbar(V.scalar_tensor_tensor(out=T[:, 1, :], in0=T[:, 0, :],
                                scalar=surep[:, 0:1], in1=sr_a[:],
                                op0=ALU.mult, op1=ALU.add))
    vv.inc(V.tensor_mul(out=sc16[:], in0=T[:, 1, :], in1=T[:, 4, :]), 1)
    SC16 = vv.v

    SY.wait_ge(vv.sem, SC16)
    dsy.inc(SY.dma_start(out=s16_hbm.rearrange("(p c) -> p c", p=16), in_=sc16[:]), 16)
    SY.wait_ge(dsy.sem, dsy.v)
    dsy.inc(SY.dma_start(out=sc_tm[:], in_=s16_hbm.rearrange("(p c) -> p c", p=128)), 16)
    SCTM = dsy.v

    GP.wait_ge(dsy.sem, SCTM)
    ggm.inc(GP.kth_largest(out_ap=thr[:], in_ap=sc_tm[:], n_per_lane=64, k=300,
                           quantile=1.0 - ONE_MINUS_Q), 1)
    ggm.inc(GP.partition_broadcast(out_ap=trep[:], in_ap=thr[0:1, 0:1],
                                   channels=128), 1)
    THR = ggm.v
    V.wait_ge(ggm.sem, THR)
    vv.inc(V.tensor_scalar(out=m_tm[:], in0=sc_tm[:], scalar1=trep[:, 0:1],
                           scalar2=None, op0=ALU.is_ge), 1)
    MASK = vv.v
    TE.wait_ge(vv.sem, MASK)
    TE.matmul(ps[0][0:128, 0:64], triu[:], m_tm[:], start=True, stop=True)
    tsm.inc(TE.matmul(ps[1][0:1, 0:64], ones128b[:], m_tm[:], start=True,
                      stop=True), 1)
    CIN = tsm.v
    V.wait_ge(tsm.sem, CIN)
    hbar(V.tensor_copy(out=trow[:], in_=ps[1][0:1, 0:64]))
    hbar(V.tensor_tensor_scan(out=scanb[:], data0=trow[:], data1=trow[:],
                              initial=0.0, op0=ALU.add, op1=ALU.bypass))
    vv.inc(V.tensor_sub(out=prow[:], in0=scanb[:], in1=trow[:]), 1)
    PROW = vv.v
    GP.wait_ge(vv.sem, PROW)
    ggm.inc(GP.partition_broadcast(out_ap=prow_rep[:], in_ap=prow[:],
                                   channels=128), 1)
    PROWREP = ggm.v
    V.wait_ge(ggm.sem, PROWREP)
    hbar(V.tensor_tensor(out=rank3[:], in0=ps[0][0:128, 0:64], in1=prow_rep[:],
                         op=ALU.add))
    hbar(V.tensor_tensor(out=rank3[:], in0=rank3[:], in1=m_tm[:], op=ALU.mult))
    vv.inc(V.tensor_scalar(out=rank3[:], in0=rank3[:], scalar1=-1.0,
                           scalar2=None, op0=ALU.add), 1)
    TE.wait_ge(ggm.sem, IOTA)
    pt_mm = {}
    for c in range(64):
        if c >= 2:
            V.wait_ge(tsm.sem, pt_mm[c - 2])
        vv.inc(V.tensor_scalar(out=ptc[:, c % 2, :], in0=iota256[:],
                               scalar1=rank3[:, c:c + 1], scalar2=None,
                               op0=ALU.is_equal), 1)
        TE.wait_ge(vv.sem, vv.v)
        buf = ptc[:, c % 2, :]
        s0, s1 = (c == 0), (c == 63)
        TE.matmul(ps[2][0:128, 0:2], buf[:, 0:128], rhs2[:, :, c], start=s0, stop=s1)
        tsm.inc(TE.matmul(ps[3][0:128, 0:2], buf[:, 128:256], rhs2[:, :, c],
                          start=s0, stop=s1), 1)
        pt_mm[c] = tsm.v
    PT_MM = tsm.v
    V.wait_ge(tsm.sem, PT_MM)
    V.tensor_copy(out=idxAB[:, 0:2], in_=ps[2][0:128, 0:2])
    hbar(V.tensor_copy(out=idxAB[:, 2:4], in_=ps[3][0:128, 0:2]))
    V.scalar_tensor_tensor(out=idxval[:, 0:1], in0=idxAB[:, 1:2], scalar=64.0,
                           in1=idxAB[:, 0:1], op0=ALU.mult, op1=ALU.add)
    hbar(V.scalar_tensor_tensor(out=idxval[:, 1:2], in0=idxAB[:, 3:4],
                                scalar=64.0, in1=idxAB[:, 2:3], op0=ALU.mult,
                                op1=ALU.add))
    vv.inc(V.tensor_copy(out=idxvi[:], in_=idxval[:]), 1)
    IDX = vv.v
    SY.wait_ge(vv.sem, IDX)
    dsy.inc(SY.dma_start(out=idx_dram.rearrange("(c p) -> p c", p=128),
                         in_=idxvi[:], allow_slow_non_contiguous=True), 16)
    SY.wait_ge(dsy.sem, dsy.v)
    for kk in range(8):
        dsy.inc(SY.dma_start(out=idx16[16 * kk:16 * kk + 16, :],
                             in_=idx_dram.rearrange("(c p) -> p c", p=16),
                             allow_slow_non_contiguous=True), 16)
    IDX16 = dsy.v
    GP.wait_ge(dsy.sem, IDX16)
    dgp.inc(GP.dma_gather(out_ap=xg_sb[:], in_ap=xb, idxs_ap=idx16[:],
                          num_idxs=256, num_idxs_reg=256, elem_size=D), 16)
    GATH = dgp.v
    if DEBUG:
        GP.wait_ge(dgp.sem, GATH)
        dgp.inc(GP.dma_start(out=dbg_xg, in_=xg_sb[:]), 16)
        GATH = dgp.v

    def emit_ln_tile(x_ap, h_out, keep_idx=None, x2n_mode=False, x2n_out=None):
        vv.inc(V.reduce_sum(out=stats[:, 0:1], in_=x_ap, axis=AX.X), 1)
        SC.wait_ge(vv.sem, vv.v)
        ssm.inc(SC.activation(out=lng[:], in_=x_ap, func=AF.Square,
                              accum_out=stats[:, 1:2]), 1)
        V.wait_ge(ssm.sem, ssm.v)
        hbar(V.tensor_scalar(out=stats[:, 2:3], in0=stats[:, 0:1],
                             scalar1=1.0 / D, scalar2=None, op0=ALU.mult))
        hbar(V.tensor_tensor(out=stats[:, 3:4], in0=stats[:, 2:3],
                             in1=stats[:, 2:3], op=ALU.mult))
        hbar(V.scalar_tensor_tensor(out=stats[:, 4:5], in0=stats[:, 1:2],
                                    scalar=1.0 / D, in1=stats[:, 3:4],
                                    op0=ALU.mult, op1=ALU.subtract))
        hbar(V.tensor_scalar(out=stats[:, 5:6], in0=stats[:, 4:5], scalar1=1e-5,
                             scalar2=None, op0=ALU.add))
        vv.inc(V.reciprocal(out=stats[:, 6:7], in_=stats[:, 5:6]), 1)
        SC.wait_ge(vv.sem, vv.v)
        ssm.inc(SC.activation(out=stats[:, 7:8], in_=stats[:, 6:7], func=AF.Sqrt), 1)
        V.wait_ge(ssm.sem, ssm.v)
        hbar(V.tensor_tensor(out=stats[:, 8:9], in0=stats[:, 7:8],
                             in1=stats[:, 7:8], op=ALU.mult))
        hbar(V.tensor_tensor(out=stats[:, 8:9], in0=stats[:, 8:9],
                             in1=stats[:, 5:6], op=ALU.mult))
        hbar(V.tensor_scalar(out=stats[:, 8:9], in0=stats[:, 8:9], scalar1=-0.5,
                             scalar2=1.5, op0=ALU.mult, op1=ALU.add))
        hbar(V.tensor_tensor(out=stats[:, 8:9], in0=stats[:, 8:9],
                             in1=stats[:, 7:8], op=ALU.mult))
        hbar(V.tensor_tensor(out=stats[:, 9:10], in0=stats[:, 2:3],
                             in1=stats[:, 8:9], op=ALU.mult))
        hbar(V.tensor_scalar(out=stats[:, 9:10], in0=stats[:, 9:10],
                             scalar1=-1.0, scalar2=None, op0=ALU.mult))
        if x2n_mode:
            vv.inc(V.tensor_scalar(out=x2n_out, in0=x_ap, scalar1=stats[:, 8:9],
                                   scalar2=stats[:, 9:10], op0=ALU.mult,
                                   op1=ALU.add), 1)
            return
        V.tensor_scalar(out=lng[:], in0=x_ap, scalar1=stats[:, 8:9],
                        scalar2=stats[:, 9:10], op0=ALU.mult, op1=ALU.add)
        hbar(V.tensor_tensor(out=stats[:, 10:11], in0=stats[:, 4:5],
                             in1=stats[:, 6:7], op=ALU.mult))
        vv.inc(V.tensor_scalar(out=stats[:, 10:11], in0=stats[:, 10:11],
                               scalar1=1e-6, scalar2=None, op0=ALU.add), 1)
        SC.wait_ge(vv.sem, vv.v)
        ssm.inc(SC.activation(out=stats[:, 11:12], in_=stats[:, 10:11],
                              func=AF.Sqrt), 1)
        V.wait_ge(ssm.sem, ssm.v)
        hbar(V.reciprocal(out=stats[:, 12:13], in_=stats[:, 11:12]))
        vv.inc(V.scalar_tensor_tensor(out=h_out, in0=lng[:],
                                      scalar=stats[:, 12:13], in1=nwrep[:],
                                      op0=ALU.mult, op1=ALU.mult), 1)
        if keep_idx is not None:
            V.tensor_copy(out=sown[:, 0, keep_idx:keep_idx + 1], in_=stats[:, 8:9])
            V.tensor_copy(out=sown[:, 1, keep_idx:keep_idx + 1], in_=stats[:, 9:10])
            vv.inc(V.tensor_scalar(out=sown[:, 3, keep_idx:keep_idx + 1],
                                   in0=stats[:, 8:9], scalar1=1.0, scalar2=None,
                                   op0=ALU.add), 1)

    own_h_dma = {}
    for tt in range(NT):
        slot = tt % 2
        if tt >= 2:
            SY.wait_ge(dsy.sem, own_h_dma[tt - 2])
        dsy.inc(SY.dma_start(out=xow[:, slot, :],
                             in_=xown_h[tt * 128:(tt + 1) * 128, :]), 16)
        V.wait_ge(dsy.sem, dsy.v)
        emit_ln_tile(xow[:, slot, :], how[:, slot, :], keep_idx=tt)
        SY.wait_ge(vv.sem, vv.v)
        dsy.inc(SY.dma_start(out=h_hbm[tt * 128:(tt + 1) * 128, :],
                             in_=how[:, slot, :]), 16)
        own_h_dma[tt] = dsy.v

    V.wait_ge(dgp.sem, GATH)
    SC.wait_ge(dgp.sem, GATH)
    for t in range(2):
        emit_ln_tile(xg_sb[:, t, :], hgt[:, t, :])
    HG_OK = vv.v
    SY.wait_ge(vv.sem, HG_OK)
    dsy.inc(SY.dma_start(out=hg_hbm.rearrange("(c p) d -> p c d", p=128),
                         in_=hgt[:]), 16)
    HG_DMA = dsy.v

    quant_abs = {}
    quant_hl = {}
    quant_gmm = {}
    quant_red = {}
    for mi in range(2):
        for t in range(32):
            ft = mi * 32 + t
            slot = ft % 4
            if ft >= 4:
                GP.wait_ge(ssm.sem, quant_abs[ft - 4])
            src = (w1_h[t * 128:(t + 1) * 128, :] if mi == 0
                   else w2_h[(t // 4) * 128:(t // 4 + 1) * 128,
                             (t % 4) * 1024:(t % 4 + 1) * 1024])
            dgp.inc(GP.dma_start(out=wtl[:, slot, :], in_=src), 16)
            SC.wait_ge(dgp.sem, dgp.v)
            if ft >= 2:
                SC.wait_ge(vv.sem, quant_hl[ft - 2])
            ssm.inc(SC.activation(out=wscf[:, ft % 2, :], in_=wtl[:, slot, :],
                                  func=AF.Abs), 1)
            quant_abs[ft] = ssm.v
            V.wait_ge(ssm.sem, quant_abs[ft])
            if ft >= 2:
                V.wait_ge(tsm.sem, quant_gmm[ft - 2])
            V.tensor_copy(out=wsc[:, ft % 2, :], in_=wscf[:, ft % 2, :])
            vv.inc(V.tensor_sub(out=wsc2[:, ft % 2, :], in0=wscf[:, ft % 2, :],
                                in1=wsc[:, ft % 2, :]), 1)
            quant_hl[ft] = vv.v
            TE.wait_ge(vv.sem, quant_hl[ft])
            pg = ps[6 + (ft % 2)]
            if ft >= 2:
                TE.wait_ge(vv.sem, quant_red[ft - 2])
            TE.matmul(pg[0:1, 0:512], ones128b[:], wsc[:, ft % 2, 0:512],
                      start=True, stop=False)
            TE.matmul(pg[0:1, 0:512], ones128b[:], wsc[:, ft % 2, 512:1024],
                      start=False, stop=False)
            TE.matmul(pg[0:1, 0:512], ones128b[:], wsc2[:, ft % 2, 0:512],
                      start=False, stop=False)
            tsm.inc(TE.matmul(pg[0:1, 0:512], ones128b[:], wsc2[:, ft % 2, 512:1024],
                              start=False, stop=True), 1)
            quant_gmm[ft] = tsm.v
            V.wait_ge(tsm.sem, quant_gmm[ft])
            vv.inc(V.reduce_sum(out=gacc[0:1, ft:ft + 1], in_=pg[0:1, 0:512],
                                axis=AX.X), 1)
            quant_red[ft] = vv.v
    QABS = ssm.v
    QGMM = tsm.v
    V.wait_ge(tsm.sem, QGMM)
    hbar(V.reduce_sum(out=wgam[0:1, 0:1], in_=gacc[0:1, 0:32], axis=AX.X))
    hbar(V.reduce_sum(out=wgam[0:1, 1:2], in_=gacc[0:1, 32:64], axis=AX.X))
    hbar(V.tensor_scalar(out=wgam[0:1, 0:2], in0=wgam[0:1, 0:2],
                         scalar1=0.5 / (FF * D), scalar2=0.5e-5, op0=ALU.mult,
                         op1=ALU.add))
    vv.inc(V.tensor_scalar(out=wgam[0:1, 2:4], in0=wgam[0:1, 0:2], scalar1=-1.0,
                           scalar2=None, op0=ALU.mult), 1)
    QTHR = vv.v
    GP.wait_ge(vv.sem, QTHR)
    GP.partition_broadcast(out_ap=wthr[:, 0:1], in_ap=wgam[0:1, 0:1], channels=128)
    GP.partition_broadcast(out_ap=wthr[:, 1:2], in_ap=wgam[0:1, 1:2], channels=128)
    GP.partition_broadcast(out_ap=wthr[:, 2:3], in_ap=wgam[0:1, 2:3], channels=128)
    ggm.inc(GP.partition_broadcast(out_ap=wthr[:, 3:4], in_ap=wgam[0:1, 3:4],
                                   channels=128), 1)
    QREP = ggm.v
    GP.wait_ge(ssm.sem, QABS)
    qspill_v = {}
    qspill_d = {}
    for mi in range(2):
        for t in range(32):
            ft = mi * 32 + t
            slot = ft % 4
            GP.wait_ge(ggm.sem, QREP)
            if ft >= 4:
                GP.wait_ge(vv.sem, qspill_v[ft - 4])
            src = (w1_h[t * 128:(t + 1) * 128, :] if mi == 0
                   else w2_h[(t // 4) * 128:(t // 4 + 1) * 128,
                             (t % 4) * 1024:(t % 4 + 1) * 1024])
            dgp.inc(GP.dma_start(out=wtl[:, slot, :], in_=src), 16)
            V.wait_ge(dgp.sem, dgp.v)
            if ft >= 2:
                V.wait_ge(dsc.sem, qspill_d[ft - 2])
            V.tensor_scalar(out=wsc[:, 0, :], in0=wtl[:, slot, :],
                            scalar1=wthr[:, mi:mi + 1], scalar2=None,
                            op0=ALU.is_ge)
            hbar(V.tensor_scalar(out=wsc[:, 1, :], in0=wtl[:, slot, :],
                                 scalar1=wthr[:, 2 + mi:3 + mi], scalar2=None,
                                 op0=ALU.is_le))
            vv.inc(V.tensor_sub(out=wqt[:, ft % 2, :], in0=wsc[:, 0, :],
                                in1=wsc[:, 1, :]), 1)
            qspill_v[ft] = vv.v
            SC.wait_ge(vv.sem, vv.v)
            dst = (wq1_hbm[t * 128:(t + 1) * 128, :] if mi == 0
                   else wq2_hbm[(t // 4) * 128:(t // 4 + 1) * 128,
                                (t % 4) * 1024:(t % 4 + 1) * 1024])
            dsc.inc(SC.dma_start(out=dst, in_=wqt[:, ft % 2, :]), 16)
            qspill_d[ft] = dsc.v
    QSPILL = dsc.v

    A_V = vv.v
    A_S = ssm.v
    A_T = tsm.v
    A_DY = dsy.v
    A_DG = dgp.v
    stA.close()

    stBC = ExitStack()
    sbb = mk(stBC)
    wqs = sbb("wqs", [128, DC, D], BF16)
    wks = sbb("wks", [128, DC, D], BF16)
    wvs = sbb("wvs", [128, DC, D], BF16)
    wos = sbb("wos", [128, DC, D], BF16)
    h_fmw = sbb("h_fmw", [128, DC, WIN], BF16)
    q_w = sbb("q_w", [128, DC, WIN], BF16)
    k_w = sbb("k_w", [128, DC, WIN], BF16)
    v_w = sbb("v_w", [128, 8, 16, 65], BF16)
    kg_fm = sbb("kg_fm", [128, DC, 256], BF16)
    vg_tm = sbb("vg_tm", [128, 2, 16, 65], BF16)
    hg_fm = sbb("hg_fm", [128, DC, 256], BF16)
    qsqw = sbb("qsqw", [128, 2, WIN], BF16)
    pP = sbb("pP", [128, 4, 512], BF16)
    o_fm = sbb("o_fm", [128, DC, 512], BF16)
    xow2 = sbb("xow2", [128, 2, D], F32)
    r1t = sbb("r1t", [128, 2, D], F32)
    otmp = sbb("otmp", [64, 512], BF16)

    GP.wait_ge(vv.sem, A_V)
    GP.wait_ge(ssm.sem, A_S)
    GP.wait_ge(tsm.sem, A_T)
    GP.wait_ge(dsy.sem, A_DY)
    for w_sb, w_src in ((wqs, wq_h), (wks, wk_h), (wvs, wv_h), (wos, wo_h)):
        for dc in range(DC):
            dgp.inc(GP.dma_start(out=w_sb[:, dc, :],
                                 in_=w_src[dc * 128:(dc + 1) * 128, :]), 16)
    WLOAD = dgp.v

    SY.wait_ge(dsy.sem, HG_DMA)
    SY.wait_ge(dgp.sem, A_DG)
    SY.wait_ge(vv.sem, A_V)
    SY.wait_ge(ssm.sem, A_S)
    for dc in range(DC):
        dsy.inc(SY.dma_start(out=hg_fm[:, dc, :],
                             in_=hg_hbm[:, dc * 128:(dc + 1) * 128],
                             transpose=True), 16)
    HGFM = dsy.v

    bank_free = {bk: None for bk in range(8)}

    def te_bank(bk):
        if bank_free[bk] is not None:
            TE.wait_ge(vv.sem, bank_free[bk])
            bank_free[bk] = None

    TE.wait_ge(dgp.sem, WLOAD)
    TE.wait_ge(dsy.sem, HGFM)
    for oc in range(DC):
        bk = oc % 2
        te_bank(bk)
        i = None
        for dc in range(DC):
            i = TE.matmul(ps[bk][:, 0:256], wks[:, dc, oc * 128:(oc + 1) * 128],
                          hg_fm[:, dc, :], start=(dc == 0), stop=(dc == DC - 1))
        tsm.inc(i, 1)
        V.wait_ge(tsm.sem, tsm.v)
        vv.inc(V.tensor_copy(out=kg_fm[:, oc, :], in_=ps[bk][:, 0:256]), 1)
        bank_free[bk] = vv.v
    V.memset(vg_tm[:, :, :, 64:65], 1.0)
    for t4 in range(2):
        te_bank(2)
        te_bank(3)
        i = None
        for dc in range(DC):
            TE.matmul(ps[2][:, 0:512], hg_fm[:, dc, t4 * 128:(t4 + 1) * 128],
                      wvs[:, dc, 0:512], start=(dc == 0), stop=(dc == DC - 1))
            i = TE.matmul(ps[3][:, 0:512], hg_fm[:, dc, t4 * 128:(t4 + 1) * 128],
                          wvs[:, dc, 512:1024], start=(dc == 0), stop=(dc == DC - 1))
        tsm.inc(i, 1)
        V.wait_ge(tsm.sem, tsm.v)
        V.tensor_copy(out=vg_tm[:, t4, 0:8, 0:64],
                      in_=ps[2][:, 0:512].rearrange("p (h e) -> p h e", h=8))
        vv.inc(V.tensor_copy(out=vg_tm[:, t4, 8:16, 0:64],
                             in_=ps[3][:, 0:512].rearrange("p (h e) -> p h e", h=8)), 1)
        bank_free[2] = vv.v
        bank_free[3] = vv.v
    KVG_EV = vv.v

    SC.wait_ge(vv.sem, KVG_EV)
    for dc in range(DC):
        ssm.inc(SC.activation(out=qsqw[:, dc % 2, 0:256], in_=kg_fm[:, dc, :],
                              func=AF.Square), 1)
        TE.wait_ge(ssm.sem, ssm.v)
        bk = 4 + dc % 2
        te_bank(bk)
        tsm.inc(TE.matmul(ps[bk][0:2, 0:256], onesA[:], qsqw[:, dc % 2, 0:256],
                          start=True, stop=True), 1)
        V.wait_ge(tsm.sem, tsm.v)
        hbar(V.reduce_max(out=mqk[:, dc, 3:4], in_=ps[bk][0:2, 0:256], axis=AX.X))
        vv.inc(V.engine_nop(), 1)
        bank_free[bk] = vv.v
    KGMAX = vv.v

    r1_built = {}
    r1_dma = {}
    prev_o_ev = None
    expc = {}
    lastAV = {}
    SC.wait_ge(ggm.sem, OFFREP)

    for w in range(NW):
        if w >= 1:
            SY.wait_ge(vv.sem, r1_built[(2 * w - 1, 3)])
        SY.wait_ge(dsy.sem, own_h_dma[w * 8 + 7])
        for dc in range(DC):
            dsy.inc(SY.dma_start(out=h_fmw[:, dc, :],
                                 in_=h_hbm[w * WIN:(w + 1) * WIN,
                                           dc * 128:(dc + 1) * 128],
                                 transpose=True), 16)
        HFW = dsy.v

        TE.wait_ge(dsy.sem, HFW)
        qk_ev = {}
        for which, wmat, dest in (("q", wqs, q_w), ("k", wks, k_w)):
            for oc in range(DC):
                bk = oc % 2
                for half in range(2):
                    te_bank(bk)
                    i = None
                    for dc in range(DC):
                        i = TE.matmul(ps[bk][:, 0:512],
                                      wmat[:, dc, oc * 128:(oc + 1) * 128],
                                      h_fmw[:, dc, half * 512:(half + 1) * 512],
                                      start=(dc == 0), stop=(dc == DC - 1))
                    tsm.inc(i, 1)
                    V.wait_ge(tsm.sem, tsm.v)
                    vv.inc(V.tensor_copy(
                        out=dest[:, oc, half * 512:(half + 1) * 512],
                        in_=ps[bk][:, 0:512]), 1)
                    bank_free[bk] = vv.v
                qk_ev[(which, oc)] = vv.v
        V.memset(v_w[:, :, :, 64:65], 1.0)
        for t8 in range(8):
            te_bank(2)
            te_bank(3)
            i = None
            for dc in range(DC):
                TE.matmul(ps[2][:, 0:512], h_fmw[:, dc, t8 * 128:(t8 + 1) * 128],
                          wvs[:, dc, 0:512], start=(dc == 0), stop=(dc == DC - 1))
                i = TE.matmul(ps[3][:, 0:512], h_fmw[:, dc, t8 * 128:(t8 + 1) * 128],
                              wvs[:, dc, 512:1024], start=(dc == 0),
                              stop=(dc == DC - 1))
            tsm.inc(i, 1)
            V.wait_ge(tsm.sem, tsm.v)
            V.tensor_copy(out=v_w[:, t8, 0:8, 0:64],
                          in_=ps[2][:, 0:512].rearrange("p (h e) -> p h e", h=8))
            vv.inc(V.tensor_copy(out=v_w[:, t8, 8:16, 0:64],
                                 in_=ps[3][:, 0:512].rearrange("p (h e) -> p h e", h=8)), 1)
            bank_free[2] = vv.v
            bank_free[3] = vv.v

        for which, src, col in (("q", q_w, 0), ("k", k_w, 1)):
            for dc in range(DC):
                SC.wait_ge(vv.sem, qk_ev[(which, dc)])
                ssm.inc(SC.activation(out=qsqw[:, dc % 2, :], in_=src[:, dc, :],
                                      func=AF.Square), 1)
                TE.wait_ge(ssm.sem, ssm.v)
                bk = 4 + dc % 2
                for gq2 in range(2):
                    te_bank(bk)
                    tsm.inc(TE.matmul(ps[bk][0:2, 0:512], onesA[:],
                                      qsqw[:, dc % 2, gq2 * 512:(gq2 + 1) * 512],
                                      start=True, stop=True), 1)
                    V.wait_ge(tsm.sem, tsm.v)
                    if gq2 == 0:
                        hbar(V.reduce_max(out=mqk[:, dc, col:col + 1],
                                          in_=ps[bk][0:2, 0:512], axis=AX.X))
                    else:
                        hbar(V.reduce_max(out=mqs[:], in_=ps[bk][0:2, 0:512],
                                          axis=AX.X))
                        hbar(V.tensor_tensor(out=mqk[:, dc, col:col + 1],
                                             in0=mqk[:, dc, col:col + 1],
                                             in1=mqs[:], op=ALU.max))
                    vv.inc(V.engine_nop(), 1)
                    bank_free[bk] = vv.v
        V.wait_ge(vv.sem, KGMAX)
        hbar(V.tensor_tensor(out=mqk[:, :, 1], in0=mqk[:, :, 1], in1=mqk[:, :, 3],
                             op=ALU.max))
        hbar(V.tensor_tensor(out=mqk[:, :, 2], in0=mqk[:, :, 0], in1=mqk[:, :, 1],
                             op=ALU.mult))
        vv.inc(V.engine_nop(), 1)
        SC.wait_ge(vv.sem, vv.v)
        ssm.inc(SC.activation(out=mqk[:, :, 2], in_=mqk[:, :, 2], func=AF.Sqrt), 1)
        V.wait_ge(ssm.sem, ssm.v)
        vv.inc(V.tensor_scalar(out=negC2[:], in0=mqk[:, :, 2], scalar1=-SCALE,
                               scalar2=None, op0=ALU.mult), 1)
        NEGC2 = vv.v
        SY.wait_ge(vv.sem, NEGC2)
        dsy.inc(SY.dma_start(out=negc_dram.rearrange("(c p) -> p c", p=2),
                             in_=negC2[:], allow_slow_non_contiguous=True), 16)
        SY.wait_ge(dsy.sem, dsy.v)
        dsy.inc(SY.dma_start(out=negCrow[:], in_=negc_dram.rearrange("(a n) -> a n", a=1)), 16)
        GP.wait_ge(dsy.sem, dsy.v)
        i = None
        for h in range(16):
            i = GP.partition_broadcast(out_ap=negC[:, h:h + 1],
                                       in_ap=negCrow[0:1, h:h + 1],
                                       channels=128)
        ggm.inc(i, 1)
        NEGC = ggm.v
        SC.wait_ge(ggm.sem, NEGC)
        prev_o_ev = vv.v
        TE.wait_ge(vv.sem, vv.v)

        for qg in range(2):
            ob_dma = {}
            rec_dma = [None]
            gi = w * 2 + qg
            q0 = w * WIN + qg * 512
            for half in range(2):
                hbar(V.tensor_scalar(out=ish[:, half:half + 1],
                                     in0=idxval[:, half:half + 1],
                                     scalar1=offrep[:, 0:1], scalar2=float(-q0),
                                     op0=ALU.subtract, op1=ALU.add))
                vv.inc(V.tensor_scalar(out=gmsk[:, half, :], in0=qposr[:],
                                       scalar1=ish[:, half:half + 1],
                                       scalar2=None, op0=ALU.is_ge), 1)
            GM = vv.v

            nloc = 4 * (qg + 1)
            kts = [("loc", kt) for kt in range(nloc)] + [("gl", 0), ("gl", 1)]
            for kc in range(DC):
                hA, hB = 2 * kc, 2 * kc + 1
                TE.wait_ge(ssm.sem, ssm.v)
                if prev_o_ev is not None:
                    TE.wait_ge(vv.sem, prev_o_ev)
                for j, (kind, kt) in enumerate(kts):
                    lgA = ps[j % 2]
                    lgB = ps[2 + j % 2]
                    ksl = slice(kt * 128, kt * 128 + 128)
                    if kind == "loc":
                        lhsA = k_w[0:64, kc, ksl]
                        lhsB = k_w[64:128, kc, ksl]
                    else:
                        lhsA = kg_fm[0:64, kc, ksl]
                        lhsB = kg_fm[64:128, kc, ksl]
                    qsl = slice(qg * 512, qg * 512 + 512)
                    if (gi, kc, j - 2) in expc:
                        TE.wait_ge(ssm.sem, expc[(gi, kc, j - 2)])
                    TE.matmul(lgA[:, 0:512], lhsA, q_w[0:64, kc, qsl],
                              start=True, stop=True)
                    tsm.inc(TE.matmul(lgB[:, 0:512], lhsB, q_w[64:128, kc, qsl],
                                      start=True, stop=True,
                                      tile_position=(64, 0)), 1)
                    qk_mm = tsm.v

                    SC.wait_ge(tsm.sem, qk_mm)
                    if (gi, kc, j - 2) in lastAV:
                        SC.wait_ge(tsm.sem, lastAV[(gi, kc, j - 2)])
                    SC.activation(out=pP[:, j % 2, :], in_=lgA[:, 0:512],
                                  func=AF.Exp, bias=negC[:, hA:hA + 1],
                                  scale=SCALE)
                    ssm.inc(SC.activation(out=pP[:, 2 + j % 2, :],
                                          in_=lgB[:, 0:512], func=AF.Exp,
                                          bias=negC[:, hB:hB + 1], scale=SCALE), 1)
                    expc[(gi, kc, j)] = ssm.v

                    masked = (kind == "gl") or (kind == "loc" and kt >= nloc - 4)
                    if masked:
                        mk_ap = gmsk[:, kt, :] if kind == "gl" else cm[:, kt - 4 * qg, :]
                        V.wait_ge(ssm.sem, expc[(gi, kc, j)])
                        V.wait_ge(vv.sem, GM)
                        V.tensor_tensor(out=pP[:, j % 2, :], in0=pP[:, j % 2, :],
                                        in1=mk_ap, op=ALU.mult)
                        vv.inc(V.tensor_tensor(out=pP[:, 2 + j % 2, :],
                                               in0=pP[:, 2 + j % 2, :],
                                               in1=mk_ap, op=ALU.mult), 1)
                        TE.wait_ge(vv.sem, vv.v)
                    else:
                        TE.wait_ge(ssm.sem, expc[(gi, kc, j)])

                    if kind == "loc":
                        vA = v_w[:, kt, hA, 0:65]
                        vB = v_w[:, kt, hB, 0:65]
                    else:
                        vA = vg_tm[:, kt, hA, 0:65]
                        vB = vg_tm[:, kt, hB, 0:65]
                    s0 = (j == 0)
                    TE.matmul(ps[4][0:65, 0:512], vA, pP[:, j % 2, :],
                              start=s0, stop=False)
                    tsm.inc(TE.matmul(ps[5][0:65, 0:512], vB, pP[:, 2 + j % 2, :],
                                      start=s0, stop=False), 1)
                    lastAV[(gi, kc, j)] = tsm.v
                AV_END = tsm.v

                V.wait_ge(tsm.sem, AV_END)
                for half, obank in ((0, 4), (1, 5)):
                    if rec_dma[0] is not None:
                        V.wait_ge(dsy.sem, rec_dma[0])
                    vv.inc(V.reciprocal(out=recrow[64:65, :],
                                        in_=ps[obank][64:65, 0:512]), 1)
                    SY.wait_ge(vv.sem, vv.v)
                    SY.wait_ge(ggm.sem, ggm.v)
                    dsy.inc(SY.dma_start(out=recrow0[0:1, :],
                                         in_=recrow[64:65, :]), 16)
                    rec_dma[0] = dsy.v
                    GP.wait_ge(dsy.sem, dsy.v)
                    GP.wait_ge(vv.sem, vv.v)
                    ggm.inc(GP.partition_broadcast(out_ap=recrep[:],
                                                   in_ap=recrow0[0:1, :],
                                                   channels=64), 1)
                    V.wait_ge(ggm.sem, ggm.v)
                    if half == 0:
                        vv.inc(V.tensor_tensor(
                            out=o_fm[0:64, kc, :],
                            in0=ps[obank][0:64, 0:512], in1=recrep[:],
                            op=ALU.mult), 1)
                    else:
                        if kc >= 1:
                            V.wait_ge(dsy.sem, ob_dma[kc - 1])
                        vv.inc(V.tensor_tensor(
                            out=otmp[0:64, :],
                            in0=ps[obank][0:64, 0:512], in1=recrep[:],
                            op=ALU.mult), 1)
                        SY.wait_ge(vv.sem, vv.v)
                        dsy.inc(SY.dma_start(out=o_fm[64:128, kc, :],
                                             in_=otmp[0:64, :]), 16)
                        ob_dma[kc] = dsy.v
                prev_o_ev = vv.v
            O_DONE = vv.v
            OB_DMA = dsy.v

            for t4 in range(4):
                tt = gi * 4 + t4
                slot = t4 % 2
                if t4 >= 2:
                    SY.wait_ge(vv.sem, r1_built[(gi, t4 - 2)])
                elif gi > 0:
                    SY.wait_ge(vv.sem, r1_built[(gi - 1, t4 + 2)])
                dsy.inc(SY.dma_start(out=xow2[:, slot, :],
                                     in_=xown_h[tt * 128:(tt + 1) * 128, :]), 16)
                X2LD = dsy.v

                TE.wait_ge(vv.sem, O_DONE)
                TE.wait_ge(dsy.sem, OB_DMA)
                if (gi, t4 - 1) in r1_built:
                    TE.wait_ge(vv.sem, r1_built[(gi, t4 - 1)])
                elif gi > 0 and t4 == 0:
                    TE.wait_ge(vv.sem, r1_built[(gi - 1, 3)])
                i = None
                for dh in range(2):
                    for kc2 in range(DC):
                        i = TE.matmul(ps[6 + dh][:, 0:512],
                                      o_fm[:, kc2, t4 * 128:(t4 + 1) * 128],
                                      wos[:, kc2, dh * 512:(dh + 1) * 512],
                                      start=(kc2 == 0), stop=(kc2 == DC - 1))
                tsm.inc(i, 1)
                PRJ = tsm.v
                V.wait_ge(tsm.sem, PRJ)
                V.wait_ge(dsy.sem, X2LD)
                hbar(V.tensor_scalar(out=xow2[:, slot, :], in0=xow2[:, slot, :],
                                     scalar1=sown[:, 3, tt:tt + 1],
                                     scalar2=sown[:, 1, tt:tt + 1],
                                     op0=ALU.mult, op1=ALU.add))
                V.tensor_tensor(out=r1t[:, slot, 0:512], in0=xow2[:, slot, 0:512],
                                in1=ps[6][:, 0:512], op=ALU.add)
                vv.inc(V.tensor_tensor(out=r1t[:, slot, 512:1024],
                                       in0=xow2[:, slot, 512:1024],
                                       in1=ps[7][:, 0:512], op=ALU.add), 1)
                r1_built[(gi, t4)] = vv.v
                SY.wait_ge(vv.sem, vv.v)
                dsy.inc(SY.dma_start(out=r1_hbm[tt * 128:(tt + 1) * 128, :],
                                     in_=r1t[:, slot, :]), 16)
                r1_dma[tt] = dsy.v

    C_V = vv.v
    C_S = ssm.v
    C_T = tsm.v
    C_DY = dsy.v
    stBC.close()

    stD = ExitStack()
    sbd = mk(stD)
    wq1T = sbd("wq1T", [128, DC, FF], BF16)
    wq2T = sbd("wq2T", [128, FC, D], BF16)
    h1g = sbd("h1g", [128, FC, 256], BF16)
    x2nf = sbd("x2nf", [128, DC, 2, 256], BF16)
    x2nb = sbd("x2nb", [128, 2, D], BF16)
    r1ln = sbd("r1ln", [128, 2, D], F32)
    r1g = sbd("r1g", [128, 2, D], F32)
    outt = sbd("outt", [128, 1, D], F32)

    SY.wait_ge(vv.sem, C_V)
    SY.wait_ge(ssm.sem, C_S)
    SY.wait_ge(tsm.sem, C_T)
    SY.wait_ge(dsc.sem, QSPILL)
    for dc in range(DC):
        dsy.inc(SY.dma_start(out=wq1T[:, dc, :],
                             in_=wq1_hbm[:, dc * 128:(dc + 1) * 128],
                             transpose=True), 16)
    for fc in range(FC):
        dsy.inc(SY.dma_start(out=wq2T[:, fc, :],
                             in_=wq2_hbm[:, fc * 128:(fc + 1) * 128],
                             transpose=True), 16)
    WQT = dsy.v

    x2n_dma = {}
    for tt in range(NT):
        slot = tt % 2
        SY.wait_ge(dsy.sem, r1_dma[tt])
        if tt >= 2:
            SY.wait_ge(dsy.sem, x2n_dma[tt - 2])
        dsy.inc(SY.dma_start(out=r1ln[:, slot, :],
                             in_=r1_hbm[tt * 128:(tt + 1) * 128, :]), 16)
        V.wait_ge(dsy.sem, dsy.v)
        emit_ln_tile(r1ln[:, slot, :], None, x2n_mode=True,
                     x2n_out=x2nb[:, slot, :])
        SY.wait_ge(vv.sem, vv.v)
        dsy.inc(SY.dma_start(out=x2n_hbm[tt * 128:(tt + 1) * 128, :],
                             in_=x2nb[:, slot, :]), 16)
        x2n_dma[tt] = dsy.v

    TE.wait_ge(ssm.sem, C_S)
    TE.wait_ge(vv.sem, C_V)
    TE.wait_ge(dsy.sem, WQT)
    h1_ev = {}
    ffn1_done = {}
    ffn2_ev = {}
    out_dma = {}
    r1g_ld = {}
    x2nf_ld = {}
    for gq in range(8):
        gslot = gq % 2
        SY.wait_ge(dsy.sem, x2n_dma[2 * gq + 1])
        if gq >= 2:
            SY.wait_ge(tsm.sem, ffn1_done[gq - 2])
        for dc in range(DC):
            dsy.inc(SY.dma_start(out=x2nf[:, dc, gslot, :],
                                 in_=x2n_hbm[gq * 256:(gq + 1) * 256,
                                             dc * 128:(dc + 1) * 128],
                                 transpose=True), 16)
        x2nf_ld[gq] = dsy.v
        if gq >= 1:
            SY.wait_ge(vv.sem, ffn2_ev.get(2 * gq - 1, 0) or 0)
        dsy.inc(SY.dma_start(out=r1g[:].rearrange("p a d -> p (a d)"),
                             in_=r1_hbm[gq * 256:(gq + 1) * 256, :]), 16)
        r1g_ld[gq] = dsy.v

        TE.wait_ge(dsy.sem, x2nf_ld[gq])
        for fc in range(FC):
            bk = fc % 2
            if (gq, fc - 2) in h1_ev:
                TE.wait_ge(ssm.sem, h1_ev[(gq, fc - 2)])
            elif gq > 0 and fc < 2:
                TE.wait_ge(ssm.sem, h1_ev[(gq - 1, FC - 2 + fc)])
            i = None
            for dc in range(DC):
                i = TE.matmul(ps[bk][:, 0:256], wq1T[:, dc, fc * 128:(fc + 1) * 128],
                              x2nf[:, dc, gslot, :], start=(dc == 0),
                              stop=(dc == DC - 1))
            tsm.inc(i, 1)
            SC.wait_ge(tsm.sem, tsm.v)
            if gq >= 1:
                SC.wait_ge(tsm.sem, ffn2_mm.get((gq - 1, 1, 1), 0) or 0)
            ssm.inc(SC.activation(out=h1g[:, fc, :], in_=ps[bk][:, 0:256],
                                  func=AF.Relu, scale=1.0,
                                  bias=b1c[:, fc:fc + 1]), 1)
            h1_ev[(gq, fc)] = ssm.v
        ffn1_done[gq] = tsm.v
        H1 = ssm.v

        if gq == 0:
            ffn2_mm = {}
        TE.wait_ge(ssm.sem, H1)
        for t2 in range(2):
            tt = gq * 2 + t2
            if tt >= 2:
                TE.wait_ge(vv.sem, ffn2_ev[tt - 2])
            for dh in range(2):
                bk = 2 + 2 * t2 + dh
                i = None
                for fc in range(FC):
                    i = TE.matmul(ps[bk][:, 0:512],
                                  h1g[:, fc, t2 * 128:(t2 + 1) * 128],
                                  wq2T[:, fc, dh * 512:(dh + 1) * 512],
                                  start=(fc == 0), stop=False)
                i = TE.matmul(ps[bk][:, 0:512], ones_row[:, 0:128],
                              b2rb[:, dh * 512:(dh + 1) * 512], start=False,
                              stop=True)
                tsm.inc(i, 1)
                ffn2_mm[(gq, t2, dh)] = tsm.v
            FFN2 = tsm.v
            V.wait_ge(tsm.sem, FFN2)
            V.wait_ge(dsy.sem, r1g_ld[gq])
            if tt >= 1:
                V.wait_ge(dsy.sem, out_dma[tt - 1])
            V.tensor_tensor(out=outt[:, 0, 0:512], in0=r1g[:, t2, 0:512],
                            in1=ps[2 + 2 * t2][:, 0:512], op=ALU.add)
            vv.inc(V.tensor_tensor(out=outt[:, 0, 512:1024],
                                   in0=r1g[:, t2, 512:1024],
                                   in1=ps[2 + 2 * t2 + 1][:, 0:512], op=ALU.add), 1)
            ffn2_ev[tt] = vv.v
            SY.wait_ge(vv.sem, vv.v)
            dsy.inc(SY.dma_start(out=out_h[tt * 128:(tt + 1) * 128, :],
                                 in_=outt[:, 0, :]), 16)
            out_dma[tt] = dsy.v

    FINAL = dsy.v
    SY.wait_ge(dsy.sem, FINAL)
    if DEBUG:
        GP.wait_ge(dsy.sem, FINAL)
        for dst, srcd in ((dbg_h, h_hbm), (dbg_srow, srow_hbm), (dbg_s16, s16_hbm),
                          (dbg_idx, idx_dram), (dbg_r1, r1_hbm), (dbg_x2n, x2n_hbm),
                          (dbg_wq1, wq1_hbm), (dbg_wq2, wq2_hbm), (dbg_hg, hg_hbm)):
            dgp.inc(GP.dma_start(out=dst, in_=srcd), 16)
        SY.wait_ge(dsy.sem, FINAL)
        dsy.inc(SY.dma_start(out=dbg_sml[:, 0:16], in_=negC[:]), 16)
        dsy.inc(SY.dma_start(out=dbg_sml[:, 16:18], in_=idxval[:]), 16)
        dsy.inc(SY.dma_start(out=dbg_sml[:, 18:82], in_=sc_tm[:]), 16)
        dsy.inc(SY.dma_start(out=dbg_sml[0:2, 82:90], in_=mqk[:, :, 2], allow_slow_non_contiguous=True), 16)
        dsy.inc(SY.dma_start(out=dbg_sml[0:1, 90:92], in_=thr[:]), 16)
        dsy.inc(SY.dma_start(out=dbg_sml[0:1, 92:93], in_=su11[:]), 16)
        dsy.inc(SY.dma_start(out=dbg_sml[0:1, 93:94], in_=wgam[0:1, 0:1]), 16)
        dsy.inc(SY.dma_start(out=dbg_sml[0:1, 94:95], in_=wgam[0:1, 1:2]), 16)
        dsy.inc(SY.dma_start(out=dbg_i16, in_=idx16[:]), 16)
        dsy.inc(SY.dma_start(out=dbg_wabs, in_=wabs[:]), 16)
        SY.wait_ge(dsy.sem, dsy.v)
        GP.wait_ge(dgp.sem, dgp.v)
    stD.close()

    with nc.Block() as blk:
        def run_engine(name, obj):
            for e, kind, payload in rec.ops:
                if e != name:
                    continue
                if kind == "wait":
                    sem, val = payload
                    obj.wait_ge(sem, val)
                else:
                    meth, args, kwargs, ri = payload
                    inst = getattr(obj, meth)(*args, **kwargs)
                    for sem, n in ri.incs:
                        inst.then_inc(sem, n)

        @blk.sync
        def _(sy):
            run_engine("sync", sy)

        @blk.vector
        def _(v):
            run_engine("vector", v)

        @blk.scalar
        def _(sc):
            run_engine("scalar", sc)

        @blk.tensor
        def _(te):
            run_engine("tensor", te)

        @blk.gpsimd
        def _(g):
            run_engine("gpsimd", g)

    st.close()
    nc.compile()
    return nc


_NC = None


def kernel(L0_attn_x, L0_attn_norm_w, L0_attn_w_q, L0_attn_w_k, L0_attn_w_v,
           L0_attn_w_out, L0_attn_scorer_w, L0_attn_scorer_b, L0_ffn_W1,
           L0_ffn_b1, L0_ffn_W2, L0_ffn_b2):
    global _NC
    if _NC is None:
        _NC = build()
    nc = _NC

    x = np.ascontiguousarray(np.asarray(L0_attn_x, dtype=np.float32))
    nw = np.asarray(L0_attn_norm_w, dtype=np.float32).reshape(D)
    sw = np.asarray(L0_attn_scorer_w, dtype=np.float32).reshape(D)
    wq = np.ascontiguousarray(np.asarray(L0_attn_w_q, dtype=np.float32))
    wk = np.ascontiguousarray(np.asarray(L0_attn_w_k, dtype=np.float32))
    wv = np.ascontiguousarray(np.asarray(L0_attn_w_v, dtype=np.float32))
    wo = np.ascontiguousarray(np.asarray(L0_attn_w_out, dtype=np.float32))
    w1 = np.ascontiguousarray(np.asarray(L0_ffn_W1, dtype=np.float32))
    w2 = np.ascontiguousarray(np.asarray(L0_ffn_W2, dtype=np.float32))
    b1 = np.asarray(L0_ffn_b1, dtype=np.float32).reshape(FF)
    b2 = np.asarray(L0_ffn_b2, dtype=np.float32).reshape(D)

    triu = np.triu(np.ones((128, 128), dtype=np.float32)).astype(ml_dtypes.bfloat16)
    iota256 = np.tile(np.arange(256, dtype=np.float32)[None, :], (128, 1))
    qposr = np.tile(np.arange(512, dtype=np.float32)[None, :], (128, 1))
    cmm = np.zeros((128, 4, 512), dtype=np.float32)
    pp = np.arange(128)[:, None]
    ff = np.arange(512)[None, :]
    for dd in range(4):
        cmm[:, dd, :] = (ff >= pp + 128 * dd).astype(np.float32)
    cmm = cmm.astype(ml_dtypes.bfloat16)
    nwrep = np.tile(nw[None, :], (128, 1))
    nw8 = nw.reshape(DC, 128).T.copy()
    sw8 = sw.reshape(DC, 128).T.copy()
    b1c = b1.reshape(FC, 128).T.copy()
    b2r = b2.reshape(1, D)

    in_maps = []
    for c in range(8):
        b_ = c // 4
        o_ = (c % 4) * SO
        in_maps.append({
            "xb": x[b_], "xown": np.ascontiguousarray(x[b_, o_:o_ + SO]),
            "wq": wq, "wk": wk, "wv": wv, "wo": wo,
            "w1": w1, "w2": w2, "b1c": b1c, "b2r": b2r,
            "nwrep": nwrep, "nw8": nw8, "sw8": sw8,
            "offv": np.array([[float(o_)]], dtype=np.float32),
            "triu": triu, "iota256": iota256, "qposr": qposr, "cm": cmm,
        })
    res = run_bass_kernel_spmd(nc, in_maps, core_ids=list(range(8)))
    full = np.empty((B, S, D), dtype=np.float32)
    for c in range(8):
        b_ = c // 4
        o_ = (c % 4) * SO
        full[b_, o_:o_ + SO] = res.results[c]["out"]
    return full


# version 04
# speedup vs baseline: 1.0595x; 1.0595x over previous
import numpy as np
import ml_dtypes
from contextlib import ExitStack

import concourse.bacc as bacc
import concourse.mybir as mybir
from concourse.bass_utils import run_bass_kernel_spmd

F32 = mybir.dt.float32
BF16 = mybir.dt.bfloat16
I16 = mybir.dt.int16
AF = mybir.ActivationFunctionType
ALU = mybir.AluOpType
AX = mybir.AxisListType

B, S, D = 2, 8192, 1024
H, HD = 16, 64
WIN, G = 1024, 256
SO = 2048
NW = SO // WIN
NT = SO // 128
DC = D // 128
FF = 4096
FC = FF // 128
ONE_MINUS_Q = 255.5 / 8191.0
SCALE = HD ** -0.5


class RInst:
    __slots__ = ("incs",)

    def __init__(self):
        self.incs = []


class Rec:
    def __init__(self):
        self.ops = []

    def op(self, eng, meth, args, kwargs):
        ri = RInst()
        self.ops.append((eng, "op", (meth, args, kwargs, ri)))
        return ri

    def wait(self, eng, sem, val):
        self.ops.append((eng, "wait", (sem, val)))


class REng:
    def __init__(self, rec, name):
        self._rec = rec
        self._name = name

    def wait_ge(self, sem, val):
        self._rec.wait(self._name, sem, val)

    def __getattr__(self, meth):
        if meth.startswith("_"):
            raise AttributeError(meth)

        def call(*args, **kwargs):
            return self._rec.op(self._name, meth, args, kwargs)

        return call


class Ctr:
    def __init__(self, sem):
        self.sem = sem
        self.v = 0

    def inc(self, inst, n=1):
        inst.incs.append((self.sem, n))
        self.v += n
        return self.v

    def wait(self, eng, v=None):
        eng.wait_ge(self.sem, self.v if v is None else v)


DEBUG = False


def build():
    nc = bacc.Bacc("TRN2", target_bir_lowering=False)

    xb = nc.dram_tensor("xb", [S, D], F32, kind="ExternalInput").ap()
    xown_h = nc.dram_tensor("xown", [SO, D], F32, kind="ExternalInput").ap()
    wq_h = nc.dram_tensor("wq", [D, D], F32, kind="ExternalInput").ap()
    wk_h = nc.dram_tensor("wk", [D, D], F32, kind="ExternalInput").ap()
    wv_h = nc.dram_tensor("wv", [D, D], F32, kind="ExternalInput").ap()
    wo_h = nc.dram_tensor("wo", [D, D], F32, kind="ExternalInput").ap()
    w1_h = nc.dram_tensor("w1", [FF, D], F32, kind="ExternalInput").ap()
    w2_h = nc.dram_tensor("w2", [D, FF], F32, kind="ExternalInput").ap()
    b1c_h = nc.dram_tensor("b1c", [128, FC], F32, kind="ExternalInput").ap()
    b2r_h = nc.dram_tensor("b2r", [1, D], F32, kind="ExternalInput").ap()
    nwrep_h = nc.dram_tensor("nwrep", [128, D], F32, kind="ExternalInput").ap()
    nw8_h = nc.dram_tensor("nw8", [128, DC], F32, kind="ExternalInput").ap()
    sw8_h = nc.dram_tensor("sw8", [128, DC], F32, kind="ExternalInput").ap()
    offv_h = nc.dram_tensor("offv", [1, 1], F32, kind="ExternalInput").ap()
    triu_h = nc.dram_tensor("triu", [128, 128], BF16, kind="ExternalInput").ap()
    iota256_h = nc.dram_tensor("iota256", [128, 256], F32, kind="ExternalInput").ap()
    qposr_h = nc.dram_tensor("qposr", [128, 512], F32, kind="ExternalInput").ap()
    cm_h = nc.dram_tensor("cm", [128, 4, 512], BF16, kind="ExternalInput").ap()

    out_h = nc.dram_tensor("out", [SO, D], F32, kind="ExternalOutput").ap()
    if DEBUG:
        dbg_h = nc.dram_tensor("dbg_h", [SO, D], BF16, kind="ExternalOutput").ap()
        dbg_srow = nc.dram_tensor("dbg_srow", [3, S], F32, kind="ExternalOutput").ap()
        dbg_s16 = nc.dram_tensor("dbg_s16", [S], F32, kind="ExternalOutput").ap()
        dbg_idx = nc.dram_tensor("dbg_idx", [256], I16, kind="ExternalOutput").ap()
        dbg_r1 = nc.dram_tensor("dbg_r1", [SO, D], F32, kind="ExternalOutput").ap()
        dbg_x2n = nc.dram_tensor("dbg_x2n", [SO, D], BF16, kind="ExternalOutput").ap()
        dbg_wq1 = nc.dram_tensor("dbg_wq1", [FF, D], BF16, kind="ExternalOutput").ap()
        dbg_wq2 = nc.dram_tensor("dbg_wq2", [D, FF], BF16, kind="ExternalOutput").ap()
        dbg_hg = nc.dram_tensor("dbg_hg", [G, D], BF16, kind="ExternalOutput").ap()
        dbg_sml = nc.dram_tensor("dbg_sml", [128, 96], F32, kind="ExternalOutput").ap()
        dbg_xg = nc.dram_tensor("dbg_xg", [128, 2, D], F32, kind="ExternalOutput").ap()
        dbg_i16 = nc.dram_tensor("dbg_i16", [128, 16], I16, kind="ExternalOutput").ap()
        dbg_wabs = nc.dram_tensor("dbg_wabs", [128, 64], F32, kind="ExternalOutput").ap()

    xh_bf = nc.dram_tensor("xh_bf", [S, D], BF16, kind="Internal").ap()
    h_hbm = nc.dram_tensor("h_hbm", [SO, D], BF16, kind="Internal").ap()
    hg_hbm = nc.dram_tensor("hg_hbm", [G, D], BF16, kind="Internal").ap()
    x2n_hbm = nc.dram_tensor("x2n_hbm", [SO, D], BF16, kind="Internal").ap()
    r1_hbm = nc.dram_tensor("r1_hbm", [SO, D], F32, kind="Internal").ap()
    wq1_hbm = nc.dram_tensor("wq1_hbm", [FF, D], BF16, kind="Internal").ap()
    wq2_hbm = nc.dram_tensor("wq2_hbm", [D, FF], BF16, kind="Internal").ap()
    srow_hbm = nc.dram_tensor("srow_hbm", [3, S], F32, kind="Internal").ap()
    s16_hbm = nc.dram_tensor("s16_hbm", [S], F32, kind="Internal").ap()
    idx_dram = nc.dram_tensor("idx_dram", [256], I16, kind="Internal").ap()
    negc_dram = nc.dram_tensor("negc_dram", [16], F32, kind="Internal").ap()

    st = ExitStack()

    def mk(stack):
        return (lambda n, s, d: stack.enter_context(nc.sbuf_tensor(n + "_s", s, d)))

    sbt = mk(st)
    psf = lambda n, s: st.enter_context(nc.psum_tensor(n + "_p", s, F32))
    mksem = lambda n: Ctr(st.enter_context(nc.semaphore(n)))

    triu = sbt("triu", [128, 128], BF16)
    iota256 = sbt("iota256", [128, 256], F32)
    qposr = sbt("qposr", [128, 512], F32)
    cm = sbt("cm", [128, 4, 512], BF16)
    nwrep = sbt("nwrep", [128, D], F32)
    nw8 = sbt("nw8", [128, DC], F32)
    sw8 = sbt("sw8", [128, DC], F32)
    u8 = sbt("u8", [128, DC], F32)
    u2 = sbt("u2", [128, DC, 2], BF16)
    ones128b = sbt("ones128b", [128, 1], BF16)
    ones128f = sbt("ones128f", [128, 1], F32)
    ones_row = sbt("ones_row", [1, 128], BF16)
    onesA = sbt("onesA", [128, 2], BF16)
    b1c = sbt("b1c", [128, FC], F32)
    b2rb = sbt("b2rb", [1, D], BF16)
    offv = sbt("offv", [1, 1], F32)
    offrep = sbt("offrep", [128, 1], F32)
    stats = sbt("stats", [128, 16], F32)
    sown = sbt("sown", [128, 4, NT], F32)
    lng = sbt("lng", [128, D], F32)
    thr = sbt("thr", [1, 2], F32)
    trep = sbt("trep", [128, 1], F32)
    m_tm = sbt("m_tm", [128, 64], BF16)
    rank3 = sbt("rank3", [128, 64], F32)
    prow = sbt("prow", [1, 64], F32)
    prow_rep = sbt("prow_rep", [128, 64], F32)
    trow = sbt("trow", [1, 64], F32)
    scanb = sbt("scanb", [1, 64], F32)
    ptc = sbt("ptc", [128, 2, 256], BF16)
    rhs2 = sbt("rhs2", [128, 2, 64], BF16)
    idxAB = sbt("idxAB", [128, 4], F32)
    idxval = sbt("idxval", [128, 2], F32)
    idxvi = sbt("idxvi", [128, 2], I16)
    idx16 = sbt("idx16", [128, 16], I16)
    sc_tm = sbt("sc_tm", [128, 64], F32)
    surep = sbt("surep", [16, 1], F32)
    su11 = sbt("su11", [1, 1], F32)
    mqk = sbt("mqk", [2, DC, 4], F32)
    mqs = sbt("mqs", [2, 1], F32)
    negC = sbt("negC", [128, 16], F32)
    negC2 = sbt("negC2", [2, DC], F32)
    negCrow = sbt("negCrow", [1, 16], F32)
    gmsk = sbt("gmsk", [128, 2, 512], BF16)
    ish = sbt("ish", [128, 2], F32)
    wgam = sbt("wgam", [1, 4], F32)
    gacc = sbt("gacc", [1, 64], F32)
    wthr = sbt("wthr", [128, 4], F32)
    wabs = sbt("wabs", [128, 64], F32)
    recrow = sbt("recrow", [65, 512], F32)
    recrow0 = sbt("recrow0", [1, 512], F32)
    recrep = sbt("recrep", [64, 512], F32)

    ps = [psf(f"ps{i}", [128, 512]) for i in range(8)]

    dsy = mksem("dsy")
    dgp = mksem("dgp")
    dsc = mksem("dsc")
    vv = mksem("vv")
    ssm = mksem("ssm")
    tsm = mksem("tsm")
    ggm = mksem("ggm")
    hzv = mksem("hzv")

    rec = Rec()
    SY = REng(rec, "sync")
    V = REng(rec, "vector")
    SC = REng(rec, "scalar")
    TE = REng(rec, "tensor")
    GP = REng(rec, "gpsimd")

    def hbar(inst):
        hzv.inc(inst, 1)
        hzv.wait(V)

    stA = ExitStack()
    sba = mk(stA)
    x_fmh = sba("x_fmh", [128, DC, S // 2], BF16)
    xsq = sba("xsq", [128, DC, 512], BF16)
    xg_sb = sba("xg_sb", [128, 2, D], F32)
    hgt = sba("hgt", [128, 2, D], BF16)
    xow = sba("xow", [128, 2, D], F32)
    how = sba("how", [128, 2, D], BF16)
    srow3 = sba("srow3", [2, 2, 512], F32)
    srow1 = sba("srow1", [1, 2, 512], F32)
    sr_a = sba("sr_a", [16, 512], F32)
    sr_b = sba("sr_b", [16, 512], F32)
    sr_c = sba("sr_c", [16, 512], F32)
    sr_t = sba("sr_t", [16, 6, 512], F32)
    sc16 = sba("sc16", [16, 512], F32)
    wtl = sba("wtl", [128, 4, D], F32)
    wqt = sba("wqt", [128, 2, D], BF16)
    wsc = sba("wsc", [128, 2, D], BF16)
    wsc2 = sba("wsc2", [128, 2, D], BF16)
    wscf = sba("wscf", [128, 2, D], F32)


    for t, src in ((triu, triu_h), (iota256, iota256_h), (qposr, qposr_h),
                   (cm, cm_h), (nwrep, nwrep_h), (nw8, nw8_h), (sw8, sw8_h),
                   (b1c, b1c_h), (offv, offv_h)):
        dsy.inc(SY.dma_start(out=t[:], in_=src), 16)
    LD_CONST = dsy.v
    dgp.inc(GP.dma_start(out=b2rb[:], in_=b2r_h), 16)

    cast_x = {}
    for i_ in range(8):
        dgp.inc(GP.dma_start(out=xh_bf[i_ * 1024:(i_ + 1) * 1024, :],
                             in_=xb[i_ * 1024:(i_ + 1) * 1024, :]), 16)
        cast_x[i_] = dgp.v

    GP.iota(out=rhs2[:, 0, :], pattern=[[1, 64]], base=0, channel_multiplier=0,
            allow_small_or_imprecise_dtypes=True)
    ggm.inc(GP.iota(out=rhs2[:, 1, :], pattern=[[0, 64]], base=0,
                    channel_multiplier=1,
                    allow_small_or_imprecise_dtypes=True), 1)
    IOTA = ggm.v
    GP.wait_ge(dsy.sem, LD_CONST)
    ggm.inc(GP.partition_broadcast(out_ap=offrep[:], in_ap=offv[:],
                                   channels=128), 1)
    OFFREP = ggm.v

    V.memset(ones128b[:], 1.0)
    V.memset(ones128f[:], 1.0)
    V.memset(ones_row[:], 1.0)
    V.memset(onesA[:], 0.0)
    V.memset(wabs[:], 0.0)
    V.memset(onesA[0:64, 0:1], 1.0)
    V.memset(onesA[64:128, 1:2], 1.0)
    V.wait_ge(dsy.sem, LD_CONST)
    hbar(V.tensor_mul(out=u8[:], in0=nw8[:], in1=sw8[:]))
    for dc in range(DC):
        V.memset(u2[:, dc, 0:1], 1.0)
        V.tensor_copy(out=u2[:, dc, 1:2], in_=u8[:, dc:dc + 1])
    vv.inc(V.engine_nop(), 1)
    PREP = vv.v

    TE.wait_ge(vv.sem, PREP)
    i = None
    for dc in range(DC):
        i = TE.matmul(ps[7][0:1, 0:1], u2[:, dc, 1:2], ones128b[:],
                      start=(dc == 0), stop=(dc == DC - 1))
    tsm.inc(i, 1)
    SC.wait_ge(tsm.sem, tsm.v)
    ssm.inc(SC.activation(out=su11[:], in_=ps[7][0:1, 0:1], func=AF.Copy,
                          scale=-1.0), 1)
    SU_EV = ssm.v
    GP.wait_ge(ssm.sem, SU_EV)
    ggm.inc(GP.partition_broadcast(out_ap=surep[:], in_ap=su11[:], channels=16), 1)
    SU_REP = ggm.v

    mm_done = {}
    ev_done = {}
    ev_dma = {}
    xfm_ld = {}
    TE.wait_ge(ssm.sem, SU_EV)
    for ph in range(2):
        SY.wait_ge(dgp.sem, cast_x[4 * ph + 3])
        if ph == 1:
            SY.wait_ge(tsm.sem, mm_done[7])
            SY.wait_ge(ssm.sem, ev_done[7])
        for dc in range(DC):
            dsy.inc(SY.dma_start(out=x_fmh[:, dc, :],
                                 in_=xh_bf[ph * 4096:(ph + 1) * 4096,
                                           dc * 128:(dc + 1) * 128],
                                 transpose=True), 16)
        xfm_ld[ph] = dsy.v
        SC.wait_ge(dsy.sem, xfm_ld[ph])
        TE.wait_ge(dsy.sem, xfm_ld[ph])
        for g8 in range(8):
            gp = ph * 8 + g8
            if gp >= 1:
                SC.wait_ge(tsm.sem, mm_done[gp - 1])
            i = None
            for dc in range(DC):
                i = SC.activation(out=xsq[:, dc, :],
                                  in_=x_fmh[:, dc, g8 * 512:(g8 + 1) * 512],
                                  func=AF.Square)
            ssm.inc(i, 1)
            SQ = ssm.v
            TE.wait_ge(ssm.sem, SQ)
            if gp >= 3:
                TE.wait_ge(ssm.sem, ev_done[gp - 3])
            slot_a = ps[2 * (g8 % 3)]
            slot_b = ps[2 * (g8 % 3) + 1]
            for dc in range(DC):
                s0 = (dc == 0)
                sp = (dc == DC - 1)
                TE.matmul(slot_a[0:2, :], u2[:, dc, :],
                          x_fmh[:, dc, g8 * 512:(g8 + 1) * 512], start=s0, stop=sp)
                i = TE.matmul(slot_b[0:1, :], ones128b[:], xsq[:, dc, :],
                              start=s0, stop=sp)
            tsm.inc(i, 1)
            mm_done[gp] = tsm.v
            SC.wait_ge(tsm.sem, mm_done[gp])
            if gp >= 2:
                SC.wait_ge(dsy.sem, ev_dma[gp - 2])
            SC.copy(out=srow3[0:2, gp % 2, :], in_=slot_a[0:2, :])
            ssm.inc(SC.copy(out=srow1[0:1, gp % 2, :], in_=slot_b[0:1, :]), 1)
            ev_done[gp] = ssm.v
            SY.wait_ge(ssm.sem, ev_done[gp])
            dsy.inc(SY.dma_start(out=srow_hbm[0:2, gp * 512:(gp + 1) * 512],
                                 in_=srow3[0:2, gp % 2, :]), 16)
            dsy.inc(SY.dma_start(out=srow_hbm[2:3, gp * 512:(gp + 1) * 512],
                                 in_=srow1[0:1, gp % 2, :]), 16)
            ev_dma[gp] = dsy.v
    EV_ALL = dsy.v

    SY.wait_ge(dsy.sem, EV_ALL)
    dsy.inc(SY.dma_start(out=sr_b[:], in_=srow_hbm[0].rearrange("(p c) -> p c", p=16)), 16)
    dsy.inc(SY.dma_start(out=sr_a[:], in_=srow_hbm[1].rearrange("(p c) -> p c", p=16)), 16)
    dsy.inc(SY.dma_start(out=sr_c[:], in_=srow_hbm[2].rearrange("(p c) -> p c", p=16)), 16)
    ROWS16 = dsy.v

    V.wait_ge(dsy.sem, ROWS16)
    V.wait_ge(ggm.sem, SU_REP)
    T = sr_t
    hbar(V.tensor_scalar(out=T[:, 0, :], in0=sr_b[:], scalar1=1.0 / D,
                         scalar2=None, op0=ALU.mult))
    hbar(V.tensor_tensor(out=T[:, 1, :], in0=T[:, 0, :], in1=T[:, 0, :],
                         op=ALU.mult))
    hbar(V.scalar_tensor_tensor(out=T[:, 2, :], in0=sr_c[:], scalar=1.0 / D,
                                in1=T[:, 1, :], op0=ALU.mult, op1=ALU.subtract))
    hbar(V.tensor_scalar(out=T[:, 3, :], in0=T[:, 2, :], scalar1=1e-5,
                         scalar2=None, op0=ALU.add))
    vv.inc(V.scalar_tensor_tensor(out=T[:, 4, :], in0=T[:, 3, :], scalar=1e-6,
                                  in1=T[:, 2, :], op0=ALU.mult, op1=ALU.add), 1)
    RM1 = vv.v
    SC.wait_ge(vv.sem, RM1)
    ssm.inc(SC.activation(out=sr_t[:, 5, :], in_=sr_t[:, 4, :], func=AF.Sqrt), 1)
    RM2 = ssm.v
    V.wait_ge(ssm.sem, RM2)
    hbar(V.reciprocal(out=T[:, 4, :], in_=T[:, 5, :]))
    hbar(V.scalar_tensor_tensor(out=T[:, 1, :], in0=T[:, 0, :],
                                scalar=surep[:, 0:1], in1=sr_a[:],
                                op0=ALU.mult, op1=ALU.add))
    vv.inc(V.tensor_mul(out=sc16[:], in0=T[:, 1, :], in1=T[:, 4, :]), 1)
    SC16 = vv.v

    SY.wait_ge(vv.sem, SC16)
    dsy.inc(SY.dma_start(out=s16_hbm.rearrange("(p c) -> p c", p=16), in_=sc16[:]), 16)
    SY.wait_ge(dsy.sem, dsy.v)
    dsy.inc(SY.dma_start(out=sc_tm[:], in_=s16_hbm.rearrange("(p c) -> p c", p=128)), 16)
    SCTM = dsy.v

    GP.wait_ge(dsy.sem, SCTM)
    ggm.inc(GP.kth_largest(out_ap=thr[:], in_ap=sc_tm[:], n_per_lane=64, k=300,
                           quantile=1.0 - ONE_MINUS_Q), 1)
    ggm.inc(GP.partition_broadcast(out_ap=trep[:], in_ap=thr[0:1, 0:1],
                                   channels=128), 1)
    THR = ggm.v
    V.wait_ge(ggm.sem, THR)
    vv.inc(V.tensor_scalar(out=m_tm[:], in0=sc_tm[:], scalar1=trep[:, 0:1],
                           scalar2=None, op0=ALU.is_ge), 1)
    MASK = vv.v
    TE.wait_ge(vv.sem, MASK)
    TE.matmul(ps[0][0:128, 0:64], triu[:], m_tm[:], start=True, stop=True)
    tsm.inc(TE.matmul(ps[1][0:1, 0:64], ones128b[:], m_tm[:], start=True,
                      stop=True), 1)
    CIN = tsm.v
    V.wait_ge(tsm.sem, CIN)
    hbar(V.tensor_copy(out=trow[:], in_=ps[1][0:1, 0:64]))
    hbar(V.tensor_tensor_scan(out=scanb[:], data0=trow[:], data1=trow[:],
                              initial=0.0, op0=ALU.add, op1=ALU.bypass))
    vv.inc(V.tensor_sub(out=prow[:], in0=scanb[:], in1=trow[:]), 1)
    PROW = vv.v
    GP.wait_ge(vv.sem, PROW)
    ggm.inc(GP.partition_broadcast(out_ap=prow_rep[:], in_ap=prow[:],
                                   channels=128), 1)
    PROWREP = ggm.v
    V.wait_ge(ggm.sem, PROWREP)
    hbar(V.tensor_tensor(out=rank3[:], in0=ps[0][0:128, 0:64], in1=prow_rep[:],
                         op=ALU.add))
    hbar(V.tensor_tensor(out=rank3[:], in0=rank3[:], in1=m_tm[:], op=ALU.mult))
    vv.inc(V.tensor_scalar(out=rank3[:], in0=rank3[:], scalar1=-1.0,
                           scalar2=None, op0=ALU.add), 1)
    TE.wait_ge(ggm.sem, IOTA)
    pt_mm = {}
    for c in range(64):
        if c >= 2:
            V.wait_ge(tsm.sem, pt_mm[c - 2])
        vv.inc(V.tensor_scalar(out=ptc[:, c % 2, :], in0=iota256[:],
                               scalar1=rank3[:, c:c + 1], scalar2=None,
                               op0=ALU.is_equal), 1)
        TE.wait_ge(vv.sem, vv.v)
        buf = ptc[:, c % 2, :]
        s0, s1 = (c == 0), (c == 63)
        TE.matmul(ps[2][0:128, 0:2], buf[:, 0:128], rhs2[:, :, c], start=s0, stop=s1)
        tsm.inc(TE.matmul(ps[3][0:128, 0:2], buf[:, 128:256], rhs2[:, :, c],
                          start=s0, stop=s1), 1)
        pt_mm[c] = tsm.v
    PT_MM = tsm.v
    V.wait_ge(tsm.sem, PT_MM)
    V.tensor_copy(out=idxAB[:, 0:2], in_=ps[2][0:128, 0:2])
    hbar(V.tensor_copy(out=idxAB[:, 2:4], in_=ps[3][0:128, 0:2]))
    V.scalar_tensor_tensor(out=idxval[:, 0:1], in0=idxAB[:, 1:2], scalar=64.0,
                           in1=idxAB[:, 0:1], op0=ALU.mult, op1=ALU.add)
    hbar(V.scalar_tensor_tensor(out=idxval[:, 1:2], in0=idxAB[:, 3:4],
                                scalar=64.0, in1=idxAB[:, 2:3], op0=ALU.mult,
                                op1=ALU.add))
    vv.inc(V.tensor_copy(out=idxvi[:], in_=idxval[:]), 1)
    IDX = vv.v
    SY.wait_ge(vv.sem, IDX)
    dsy.inc(SY.dma_start(out=idx_dram.rearrange("(c p) -> p c", p=128),
                         in_=idxvi[:], allow_slow_non_contiguous=True), 16)
    SY.wait_ge(dsy.sem, dsy.v)
    for kk in range(8):
        dsy.inc(SY.dma_start(out=idx16[16 * kk:16 * kk + 16, :],
                             in_=idx_dram.rearrange("(c p) -> p c", p=16),
                             allow_slow_non_contiguous=True), 16)
    IDX16 = dsy.v
    GP.wait_ge(dsy.sem, IDX16)
    dgp.inc(GP.dma_gather(out_ap=xg_sb[:], in_ap=xb, idxs_ap=idx16[:],
                          num_idxs=256, num_idxs_reg=256, elem_size=D), 16)
    GATH = dgp.v
    if DEBUG:
        GP.wait_ge(dgp.sem, GATH)
        dgp.inc(GP.dma_start(out=dbg_xg, in_=xg_sb[:]), 16)
        GATH = dgp.v

    def emit_ln_tile(x_ap, h_out, keep_idx=None, x2n_mode=False, x2n_out=None):
        vv.inc(V.reduce_sum(out=stats[:, 0:1], in_=x_ap, axis=AX.X), 1)
        SC.wait_ge(vv.sem, vv.v)
        ssm.inc(SC.activation(out=lng[:], in_=x_ap, func=AF.Square,
                              accum_out=stats[:, 1:2]), 1)
        V.wait_ge(ssm.sem, ssm.v)
        hbar(V.tensor_scalar(out=stats[:, 2:3], in0=stats[:, 0:1],
                             scalar1=1.0 / D, scalar2=None, op0=ALU.mult))
        hbar(V.tensor_tensor(out=stats[:, 3:4], in0=stats[:, 2:3],
                             in1=stats[:, 2:3], op=ALU.mult))
        hbar(V.scalar_tensor_tensor(out=stats[:, 4:5], in0=stats[:, 1:2],
                                    scalar=1.0 / D, in1=stats[:, 3:4],
                                    op0=ALU.mult, op1=ALU.subtract))
        hbar(V.tensor_scalar(out=stats[:, 5:6], in0=stats[:, 4:5], scalar1=1e-5,
                             scalar2=None, op0=ALU.add))
        vv.inc(V.reciprocal(out=stats[:, 6:7], in_=stats[:, 5:6]), 1)
        SC.wait_ge(vv.sem, vv.v)
        ssm.inc(SC.activation(out=stats[:, 7:8], in_=stats[:, 6:7], func=AF.Sqrt), 1)
        V.wait_ge(ssm.sem, ssm.v)
        hbar(V.tensor_tensor(out=stats[:, 8:9], in0=stats[:, 7:8],
                             in1=stats[:, 7:8], op=ALU.mult))
        hbar(V.tensor_tensor(out=stats[:, 8:9], in0=stats[:, 8:9],
                             in1=stats[:, 5:6], op=ALU.mult))
        hbar(V.tensor_scalar(out=stats[:, 8:9], in0=stats[:, 8:9], scalar1=-0.5,
                             scalar2=1.5, op0=ALU.mult, op1=ALU.add))
        hbar(V.tensor_tensor(out=stats[:, 8:9], in0=stats[:, 8:9],
                             in1=stats[:, 7:8], op=ALU.mult))
        hbar(V.tensor_tensor(out=stats[:, 9:10], in0=stats[:, 2:3],
                             in1=stats[:, 8:9], op=ALU.mult))
        hbar(V.tensor_scalar(out=stats[:, 9:10], in0=stats[:, 9:10],
                             scalar1=-1.0, scalar2=None, op0=ALU.mult))
        if x2n_mode:
            vv.inc(V.tensor_scalar(out=x2n_out, in0=x_ap, scalar1=stats[:, 8:9],
                                   scalar2=stats[:, 9:10], op0=ALU.mult,
                                   op1=ALU.add), 1)
            return
        V.tensor_scalar(out=lng[:], in0=x_ap, scalar1=stats[:, 8:9],
                        scalar2=stats[:, 9:10], op0=ALU.mult, op1=ALU.add)
        hbar(V.tensor_tensor(out=stats[:, 10:11], in0=stats[:, 4:5],
                             in1=stats[:, 6:7], op=ALU.mult))
        vv.inc(V.tensor_scalar(out=stats[:, 10:11], in0=stats[:, 10:11],
                               scalar1=1e-6, scalar2=None, op0=ALU.add), 1)
        SC.wait_ge(vv.sem, vv.v)
        ssm.inc(SC.activation(out=stats[:, 11:12], in_=stats[:, 10:11],
                              func=AF.Sqrt), 1)
        V.wait_ge(ssm.sem, ssm.v)
        hbar(V.reciprocal(out=stats[:, 12:13], in_=stats[:, 11:12]))
        vv.inc(V.scalar_tensor_tensor(out=h_out, in0=lng[:],
                                      scalar=stats[:, 12:13], in1=nwrep[:],
                                      op0=ALU.mult, op1=ALU.mult), 1)
        if keep_idx is not None:
            V.tensor_copy(out=sown[:, 0, keep_idx:keep_idx + 1], in_=stats[:, 8:9])
            V.tensor_copy(out=sown[:, 1, keep_idx:keep_idx + 1], in_=stats[:, 9:10])
            vv.inc(V.tensor_scalar(out=sown[:, 3, keep_idx:keep_idx + 1],
                                   in0=stats[:, 8:9], scalar1=1.0, scalar2=None,
                                   op0=ALU.add), 1)

    own_h_dma = {}
    for tt in range(NT):
        slot = tt % 2
        if tt >= 2:
            SY.wait_ge(dsy.sem, own_h_dma[tt - 2])
        dsy.inc(SY.dma_start(out=xow[:, slot, :],
                             in_=xown_h[tt * 128:(tt + 1) * 128, :]), 16)
        V.wait_ge(dsy.sem, dsy.v)
        emit_ln_tile(xow[:, slot, :], how[:, slot, :], keep_idx=tt)
        SY.wait_ge(vv.sem, vv.v)
        dsy.inc(SY.dma_start(out=h_hbm[tt * 128:(tt + 1) * 128, :],
                             in_=how[:, slot, :]), 16)
        own_h_dma[tt] = dsy.v

    V.wait_ge(dgp.sem, GATH)
    SC.wait_ge(dgp.sem, GATH)
    for t in range(2):
        emit_ln_tile(xg_sb[:, t, :], hgt[:, t, :])
    HG_OK = vv.v
    SY.wait_ge(vv.sem, HG_OK)
    dsy.inc(SY.dma_start(out=hg_hbm.rearrange("(c p) d -> p c d", p=128),
                         in_=hgt[:]), 16)
    HG_DMA = dsy.v

    quant_abs = {}
    quant_hl = {}
    quant_gmm = {}
    quant_red = {}
    for mi in range(2):
        for t in range(32):
            ft = mi * 32 + t
            slot = ft % 4
            if ft >= 4:
                GP.wait_ge(ssm.sem, quant_abs[ft - 4])
            src = (w1_h[t * 128:(t + 1) * 128, :] if mi == 0
                   else w2_h[(t // 4) * 128:(t // 4 + 1) * 128,
                             (t % 4) * 1024:(t % 4 + 1) * 1024])
            dgp.inc(GP.dma_start(out=wtl[:, slot, :], in_=src), 16)
            SC.wait_ge(dgp.sem, dgp.v)
            if ft >= 2:
                SC.wait_ge(vv.sem, quant_hl[ft - 2])
            ssm.inc(SC.activation(out=wscf[:, ft % 2, :], in_=wtl[:, slot, :],
                                  func=AF.Abs), 1)
            quant_abs[ft] = ssm.v
            V.wait_ge(ssm.sem, quant_abs[ft])
            if ft >= 2:
                V.wait_ge(tsm.sem, quant_gmm[ft - 2])
            V.tensor_copy(out=wsc[:, ft % 2, :], in_=wscf[:, ft % 2, :])
            vv.inc(V.tensor_sub(out=wsc2[:, ft % 2, :], in0=wscf[:, ft % 2, :],
                                in1=wsc[:, ft % 2, :]), 1)
            quant_hl[ft] = vv.v
            TE.wait_ge(vv.sem, quant_hl[ft])
            pg = ps[6 + (ft % 2)]
            if ft >= 2:
                TE.wait_ge(vv.sem, quant_red[ft - 2])
            TE.matmul(pg[0:1, 0:512], ones128b[:], wsc[:, ft % 2, 0:512],
                      start=True, stop=False)
            TE.matmul(pg[0:1, 0:512], ones128b[:], wsc[:, ft % 2, 512:1024],
                      start=False, stop=False)
            TE.matmul(pg[0:1, 0:512], ones128b[:], wsc2[:, ft % 2, 0:512],
                      start=False, stop=False)
            tsm.inc(TE.matmul(pg[0:1, 0:512], ones128b[:], wsc2[:, ft % 2, 512:1024],
                              start=False, stop=True), 1)
            quant_gmm[ft] = tsm.v
            V.wait_ge(tsm.sem, quant_gmm[ft])
            vv.inc(V.reduce_sum(out=gacc[0:1, ft:ft + 1], in_=pg[0:1, 0:512],
                                axis=AX.X), 1)
            quant_red[ft] = vv.v
    QABS = ssm.v
    QGMM = tsm.v
    V.wait_ge(tsm.sem, QGMM)
    hbar(V.reduce_sum(out=wgam[0:1, 0:1], in_=gacc[0:1, 0:32], axis=AX.X))
    hbar(V.reduce_sum(out=wgam[0:1, 1:2], in_=gacc[0:1, 32:64], axis=AX.X))
    hbar(V.tensor_scalar(out=wgam[0:1, 0:2], in0=wgam[0:1, 0:2],
                         scalar1=0.5 / (FF * D), scalar2=0.5e-5, op0=ALU.mult,
                         op1=ALU.add))
    vv.inc(V.tensor_scalar(out=wgam[0:1, 2:4], in0=wgam[0:1, 0:2], scalar1=-1.0,
                           scalar2=None, op0=ALU.mult), 1)
    QTHR = vv.v
    GP.wait_ge(vv.sem, QTHR)
    GP.partition_broadcast(out_ap=wthr[:, 0:1], in_ap=wgam[0:1, 0:1], channels=128)
    GP.partition_broadcast(out_ap=wthr[:, 1:2], in_ap=wgam[0:1, 1:2], channels=128)
    GP.partition_broadcast(out_ap=wthr[:, 2:3], in_ap=wgam[0:1, 2:3], channels=128)
    ggm.inc(GP.partition_broadcast(out_ap=wthr[:, 3:4], in_ap=wgam[0:1, 3:4],
                                   channels=128), 1)
    QREP = ggm.v
    GP.wait_ge(ssm.sem, QABS)
    qspill_v = {}
    qspill_d = {}
    for mi in range(2):
        for t in range(32):
            ft = mi * 32 + t
            slot = ft % 4
            GP.wait_ge(ggm.sem, QREP)
            if ft >= 4:
                GP.wait_ge(vv.sem, qspill_v[ft - 4])
            src = (w1_h[t * 128:(t + 1) * 128, :] if mi == 0
                   else w2_h[(t // 4) * 128:(t // 4 + 1) * 128,
                             (t % 4) * 1024:(t % 4 + 1) * 1024])
            dgp.inc(GP.dma_start(out=wtl[:, slot, :], in_=src), 16)
            V.wait_ge(dgp.sem, dgp.v)
            if ft >= 2:
                V.wait_ge(dsc.sem, qspill_d[ft - 2])
            V.tensor_scalar(out=wsc[:, 0, :], in0=wtl[:, slot, :],
                            scalar1=wthr[:, mi:mi + 1], scalar2=None,
                            op0=ALU.is_ge)
            hbar(V.tensor_scalar(out=wsc[:, 1, :], in0=wtl[:, slot, :],
                                 scalar1=wthr[:, 2 + mi:3 + mi], scalar2=None,
                                 op0=ALU.is_le))
            vv.inc(V.tensor_sub(out=wqt[:, ft % 2, :], in0=wsc[:, 0, :],
                                in1=wsc[:, 1, :]), 1)
            qspill_v[ft] = vv.v
            SC.wait_ge(vv.sem, vv.v)
            dst = (wq1_hbm[t * 128:(t + 1) * 128, :] if mi == 0
                   else wq2_hbm[(t // 4) * 128:(t // 4 + 1) * 128,
                                (t % 4) * 1024:(t % 4 + 1) * 1024])
            dsc.inc(SC.dma_start(out=dst, in_=wqt[:, ft % 2, :]), 16)
            qspill_d[ft] = dsc.v
    QSPILL = dsc.v

    A_V = vv.v
    A_S = ssm.v
    A_T = tsm.v
    A_DY = dsy.v
    A_DG = dgp.v
    stA.close()

    stBC = ExitStack()
    sbb = mk(stBC)
    wqs = sbb("wqs", [128, DC, D], BF16)
    wks = sbb("wks", [128, DC, D], BF16)
    wvs = sbb("wvs", [128, DC, D], BF16)
    wos = sbb("wos", [128, DC, D], BF16)
    h_fmw = sbb("h_fmw", [128, DC, WIN], BF16)
    q_w = sbb("q_w", [128, DC, WIN], BF16)
    k_w = sbb("k_w", [128, DC, WIN], BF16)
    v_w = sbb("v_w", [128, 8, 16, 65], BF16)
    kg_fm = sbb("kg_fm", [128, DC, 256], BF16)
    vg_tm = sbb("vg_tm", [128, 2, 16, 65], BF16)
    hg_fm = sbb("hg_fm", [128, DC, 256], BF16)
    qsqw = sbb("qsqw", [128, 2, WIN], BF16)
    pP = sbb("pP", [128, 4, 512], BF16)
    o_fm = sbb("o_fm", [128, DC, 512], BF16)
    xow2 = sbb("xow2", [128, 2, D], F32)
    r1t = sbb("r1t", [128, 2, D], F32)
    otmp = sbb("otmp", [64, 512], BF16)

    GP.wait_ge(vv.sem, A_V)
    GP.wait_ge(ssm.sem, A_S)
    GP.wait_ge(tsm.sem, A_T)
    GP.wait_ge(dsy.sem, A_DY)
    for w_sb, w_src in ((wqs, wq_h), (wks, wk_h), (wvs, wv_h), (wos, wo_h)):
        for dc in range(DC):
            dgp.inc(GP.dma_start(out=w_sb[:, dc, :],
                                 in_=w_src[dc * 128:(dc + 1) * 128, :]), 16)
    WLOAD = dgp.v

    SY.wait_ge(dsy.sem, HG_DMA)
    SY.wait_ge(dgp.sem, A_DG)
    SY.wait_ge(vv.sem, A_V)
    SY.wait_ge(ssm.sem, A_S)
    for dc in range(DC):
        dsy.inc(SY.dma_start(out=hg_fm[:, dc, :],
                             in_=hg_hbm[:, dc * 128:(dc + 1) * 128],
                             transpose=True), 16)
    HGFM = dsy.v

    bank_free = {bk: None for bk in range(8)}

    def te_bank(bk):
        if bank_free[bk] is not None:
            TE.wait_ge(vv.sem, bank_free[bk])
            bank_free[bk] = None

    TE.wait_ge(dgp.sem, WLOAD)
    TE.wait_ge(dsy.sem, HGFM)
    for oc in range(DC):
        bk = oc % 2
        te_bank(bk)
        i = None
        for dc in range(DC):
            i = TE.matmul(ps[bk][:, 0:256], wks[:, dc, oc * 128:(oc + 1) * 128],
                          hg_fm[:, dc, :], start=(dc == 0), stop=(dc == DC - 1))
        tsm.inc(i, 1)
        V.wait_ge(tsm.sem, tsm.v)
        vv.inc(V.tensor_copy(out=kg_fm[:, oc, :], in_=ps[bk][:, 0:256]), 1)
        bank_free[bk] = vv.v
    V.memset(vg_tm[:, :, :, 64:65], 1.0)
    for t4 in range(2):
        te_bank(2)
        te_bank(3)
        i = None
        for dc in range(DC):
            TE.matmul(ps[2][:, 0:512], hg_fm[:, dc, t4 * 128:(t4 + 1) * 128],
                      wvs[:, dc, 0:512], start=(dc == 0), stop=(dc == DC - 1))
            i = TE.matmul(ps[3][:, 0:512], hg_fm[:, dc, t4 * 128:(t4 + 1) * 128],
                          wvs[:, dc, 512:1024], start=(dc == 0), stop=(dc == DC - 1))
        tsm.inc(i, 1)
        V.wait_ge(tsm.sem, tsm.v)
        V.tensor_copy(out=vg_tm[:, t4, 0:8, 0:64],
                      in_=ps[2][:, 0:512].rearrange("p (h e) -> p h e", h=8))
        vv.inc(V.tensor_copy(out=vg_tm[:, t4, 8:16, 0:64],
                             in_=ps[3][:, 0:512].rearrange("p (h e) -> p h e", h=8)), 1)
        bank_free[2] = vv.v
        bank_free[3] = vv.v
    KVG_EV = vv.v

    SC.wait_ge(vv.sem, KVG_EV)
    for dc in range(DC):
        ssm.inc(SC.activation(out=qsqw[:, dc % 2, 0:256], in_=kg_fm[:, dc, :],
                              func=AF.Square), 1)
        TE.wait_ge(ssm.sem, ssm.v)
        bk = 4 + dc % 2
        te_bank(bk)
        tsm.inc(TE.matmul(ps[bk][0:2, 0:256], onesA[:], qsqw[:, dc % 2, 0:256],
                          start=True, stop=True), 1)
        V.wait_ge(tsm.sem, tsm.v)
        hbar(V.reduce_max(out=mqk[:, dc, 3:4], in_=ps[bk][0:2, 0:256], axis=AX.X))
        vv.inc(V.engine_nop(), 1)
        bank_free[bk] = vv.v
    KGMAX = vv.v

    r1_built = {}
    r1_dma = {}
    prev_o_ev = None
    prev2_o_ev = None
    expc = {}
    lastAV = {}
    SC.wait_ge(ggm.sem, OFFREP)

    for w in range(NW):
        if w >= 1:
            SY.wait_ge(vv.sem, r1_built[(2 * w - 1, 3)])
        SY.wait_ge(dsy.sem, own_h_dma[w * 8 + 7])
        for dc in range(DC):
            dsy.inc(SY.dma_start(out=h_fmw[:, dc, :],
                                 in_=h_hbm[w * WIN:(w + 1) * WIN,
                                           dc * 128:(dc + 1) * 128],
                                 transpose=True), 16)
        HFW = dsy.v

        TE.wait_ge(dsy.sem, HFW)
        qk_ev = {}
        for which, wmat, dest in (("q", wqs, q_w), ("k", wks, k_w)):
            for oc in range(DC):
                bk = oc % 2
                for half in range(2):
                    te_bank(bk)
                    i = None
                    for dc in range(DC):
                        i = TE.matmul(ps[bk][:, 0:512],
                                      wmat[:, dc, oc * 128:(oc + 1) * 128],
                                      h_fmw[:, dc, half * 512:(half + 1) * 512],
                                      start=(dc == 0), stop=(dc == DC - 1))
                    tsm.inc(i, 1)
                    V.wait_ge(tsm.sem, tsm.v)
                    vv.inc(V.tensor_copy(
                        out=dest[:, oc, half * 512:(half + 1) * 512],
                        in_=ps[bk][:, 0:512]), 1)
                    bank_free[bk] = vv.v
                qk_ev[(which, oc)] = vv.v
        V.memset(v_w[:, :, :, 64:65], 1.0)
        for t8 in range(8):
            te_bank(2)
            te_bank(3)
            i = None
            for dc in range(DC):
                TE.matmul(ps[2][:, 0:512], h_fmw[:, dc, t8 * 128:(t8 + 1) * 128],
                          wvs[:, dc, 0:512], start=(dc == 0), stop=(dc == DC - 1))
                i = TE.matmul(ps[3][:, 0:512], h_fmw[:, dc, t8 * 128:(t8 + 1) * 128],
                              wvs[:, dc, 512:1024], start=(dc == 0),
                              stop=(dc == DC - 1))
            tsm.inc(i, 1)
            V.wait_ge(tsm.sem, tsm.v)
            V.tensor_copy(out=v_w[:, t8, 0:8, 0:64],
                          in_=ps[2][:, 0:512].rearrange("p (h e) -> p h e", h=8))
            vv.inc(V.tensor_copy(out=v_w[:, t8, 8:16, 0:64],
                                 in_=ps[3][:, 0:512].rearrange("p (h e) -> p h e", h=8)), 1)
            bank_free[2] = vv.v
            bank_free[3] = vv.v

        for which, src, col in (("q", q_w, 0), ("k", k_w, 1)):
            for dc in range(DC):
                SC.wait_ge(vv.sem, qk_ev[(which, dc)])
                ssm.inc(SC.activation(out=qsqw[:, dc % 2, :], in_=src[:, dc, :],
                                      func=AF.Square), 1)
                TE.wait_ge(ssm.sem, ssm.v)
                bk = 4 + dc % 2
                for gq2 in range(2):
                    te_bank(bk)
                    tsm.inc(TE.matmul(ps[bk][0:2, 0:512], onesA[:],
                                      qsqw[:, dc % 2, gq2 * 512:(gq2 + 1) * 512],
                                      start=True, stop=True), 1)
                    V.wait_ge(tsm.sem, tsm.v)
                    if gq2 == 0:
                        hbar(V.reduce_max(out=mqk[:, dc, col:col + 1],
                                          in_=ps[bk][0:2, 0:512], axis=AX.X))
                    else:
                        hbar(V.reduce_max(out=mqs[:], in_=ps[bk][0:2, 0:512],
                                          axis=AX.X))
                        hbar(V.tensor_tensor(out=mqk[:, dc, col:col + 1],
                                             in0=mqk[:, dc, col:col + 1],
                                             in1=mqs[:], op=ALU.max))
                    vv.inc(V.engine_nop(), 1)
                    bank_free[bk] = vv.v
        V.wait_ge(vv.sem, KGMAX)
        hbar(V.tensor_tensor(out=mqk[:, :, 1], in0=mqk[:, :, 1], in1=mqk[:, :, 3],
                             op=ALU.max))
        hbar(V.tensor_tensor(out=mqk[:, :, 2], in0=mqk[:, :, 0], in1=mqk[:, :, 1],
                             op=ALU.mult))
        vv.inc(V.engine_nop(), 1)
        SC.wait_ge(vv.sem, vv.v)
        ssm.inc(SC.activation(out=mqk[:, :, 2], in_=mqk[:, :, 2], func=AF.Sqrt), 1)
        V.wait_ge(ssm.sem, ssm.v)
        vv.inc(V.tensor_scalar(out=negC2[:], in0=mqk[:, :, 2], scalar1=-SCALE,
                               scalar2=None, op0=ALU.mult), 1)
        NEGC2 = vv.v
        SY.wait_ge(vv.sem, NEGC2)
        dsy.inc(SY.dma_start(out=negc_dram.rearrange("(c p) -> p c", p=2),
                             in_=negC2[:], allow_slow_non_contiguous=True), 16)
        SY.wait_ge(dsy.sem, dsy.v)
        dsy.inc(SY.dma_start(out=negCrow[:], in_=negc_dram.rearrange("(a n) -> a n", a=1)), 16)
        GP.wait_ge(dsy.sem, dsy.v)
        i = None
        for h in range(16):
            i = GP.partition_broadcast(out_ap=negC[:, h:h + 1],
                                       in_ap=negCrow[0:1, h:h + 1],
                                       channels=128)
        ggm.inc(i, 1)
        NEGC = ggm.v
        SC.wait_ge(ggm.sem, NEGC)
        prev_o_ev = vv.v
        prev2_o_ev = vv.v
        TE.wait_ge(vv.sem, vv.v)

        for qg in range(2):
            ob_dma = {}
            rec_dma = [None]
            gi = w * 2 + qg
            q0 = w * WIN + qg * 512
            for half in range(2):
                hbar(V.tensor_scalar(out=ish[:, half:half + 1],
                                     in0=idxval[:, half:half + 1],
                                     scalar1=offrep[:, 0:1], scalar2=float(-q0),
                                     op0=ALU.subtract, op1=ALU.add))
                vv.inc(V.tensor_scalar(out=gmsk[:, half, :], in0=qposr[:],
                                       scalar1=ish[:, half:half + 1],
                                       scalar2=None, op0=ALU.is_ge), 1)
            GM = vv.v

            nloc = 4 * (qg + 1)
            kts = [("loc", kt) for kt in range(nloc)] + [("gl", 0), ("gl", 1)]
            for kc in range(DC):
                hA, hB = 2 * kc, 2 * kc + 1
                TE.wait_ge(ssm.sem, ssm.v)
                if kc == 0 and (gi - 1, 3) in r1_built:
                    TE.wait_ge(vv.sem, r1_built[(gi - 1, 3)])
                if prev2_o_ev is not None:
                    TE.wait_ge(vv.sem, prev2_o_ev)
                for j, (kind, kt) in enumerate(kts):
                    lgA = ps[j % 2]
                    lgB = ps[2 + j % 2]
                    ksl = slice(kt * 128, kt * 128 + 128)
                    if kind == "loc":
                        lhsA = k_w[0:64, kc, ksl]
                        lhsB = k_w[64:128, kc, ksl]
                    else:
                        lhsA = kg_fm[0:64, kc, ksl]
                        lhsB = kg_fm[64:128, kc, ksl]
                    qsl = slice(qg * 512, qg * 512 + 512)
                    if (gi, kc, j - 2) in expc:
                        TE.wait_ge(ssm.sem, expc[(gi, kc, j - 2)])
                    TE.matmul(lgA[:, 0:512], lhsA, q_w[0:64, kc, qsl],
                              start=True, stop=True)
                    tsm.inc(TE.matmul(lgB[:, 0:512], lhsB, q_w[64:128, kc, qsl],
                                      start=True, stop=True,
                                      tile_position=(64, 0)), 1)
                    qk_mm = tsm.v

                    SC.wait_ge(tsm.sem, qk_mm)
                    if (gi, kc, j - 2) in lastAV:
                        SC.wait_ge(tsm.sem, lastAV[(gi, kc, j - 2)])
                    SC.activation(out=pP[:, j % 2, :], in_=lgA[:, 0:512],
                                  func=AF.Exp, bias=negC[:, hA:hA + 1],
                                  scale=SCALE)
                    ssm.inc(SC.activation(out=pP[:, 2 + j % 2, :],
                                          in_=lgB[:, 0:512], func=AF.Exp,
                                          bias=negC[:, hB:hB + 1], scale=SCALE), 1)
                    expc[(gi, kc, j)] = ssm.v

                    masked = (kind == "gl") or (kind == "loc" and kt >= nloc - 4)
                    if masked:
                        mk_ap = gmsk[:, kt, :] if kind == "gl" else cm[:, kt - 4 * qg, :]
                        V.wait_ge(ssm.sem, expc[(gi, kc, j)])
                        V.wait_ge(vv.sem, GM)
                        V.tensor_tensor(out=pP[:, j % 2, :], in0=pP[:, j % 2, :],
                                        in1=mk_ap, op=ALU.mult)
                        vv.inc(V.tensor_tensor(out=pP[:, 2 + j % 2, :],
                                               in0=pP[:, 2 + j % 2, :],
                                               in1=mk_ap, op=ALU.mult), 1)
                        TE.wait_ge(vv.sem, vv.v)
                    else:
                        TE.wait_ge(ssm.sem, expc[(gi, kc, j)])

                    if kind == "loc":
                        vA = v_w[:, kt, hA, 0:65]
                        vB = v_w[:, kt, hB, 0:65]
                    else:
                        vA = vg_tm[:, kt, hA, 0:65]
                        vB = vg_tm[:, kt, hB, 0:65]
                    s0 = (j == 0)
                    obb = 4 + 2 * (kc % 2)
                    TE.matmul(ps[obb][0:65, 0:512], vA, pP[:, j % 2, :],
                              start=s0, stop=False)
                    tsm.inc(TE.matmul(ps[obb + 1][0:65, 0:512], vB,
                                      pP[:, 2 + j % 2, :],
                                      start=s0, stop=False), 1)
                    lastAV[(gi, kc, j)] = tsm.v
                AV_END = tsm.v

                V.wait_ge(tsm.sem, AV_END)
                obb = 4 + 2 * (kc % 2)
                for half, obank in ((0, obb), (1, obb + 1)):
                    if rec_dma[0] is not None:
                        V.wait_ge(dsy.sem, rec_dma[0])
                    vv.inc(V.reciprocal(out=recrow[64:65, :],
                                        in_=ps[obank][64:65, 0:512]), 1)
                    SY.wait_ge(vv.sem, vv.v)
                    SY.wait_ge(ggm.sem, ggm.v)
                    dsy.inc(SY.dma_start(out=recrow0[0:1, :],
                                         in_=recrow[64:65, :]), 16)
                    rec_dma[0] = dsy.v
                    GP.wait_ge(dsy.sem, dsy.v)
                    GP.wait_ge(vv.sem, vv.v)
                    ggm.inc(GP.partition_broadcast(out_ap=recrep[:],
                                                   in_ap=recrow0[0:1, :],
                                                   channels=64), 1)
                    V.wait_ge(ggm.sem, ggm.v)
                    if half == 0:
                        vv.inc(V.tensor_tensor(
                            out=o_fm[0:64, kc, :],
                            in0=ps[obank][0:64, 0:512], in1=recrep[:],
                            op=ALU.mult), 1)
                    else:
                        if kc >= 1:
                            V.wait_ge(dsy.sem, ob_dma[kc - 1])
                        vv.inc(V.tensor_tensor(
                            out=otmp[0:64, :],
                            in0=ps[obank][0:64, 0:512], in1=recrep[:],
                            op=ALU.mult), 1)
                        SY.wait_ge(vv.sem, vv.v)
                        dsy.inc(SY.dma_start(out=o_fm[64:128, kc, :],
                                             in_=otmp[0:64, :]), 16)
                        ob_dma[kc] = dsy.v
                prev2_o_ev = prev_o_ev
                prev_o_ev = vv.v
            O_DONE = vv.v
            OB_DMA = dsy.v

            for t4 in range(4):
                tt = gi * 4 + t4
                slot = t4 % 2
                if t4 >= 2:
                    SY.wait_ge(vv.sem, r1_built[(gi, t4 - 2)])
                elif gi > 0:
                    SY.wait_ge(vv.sem, r1_built[(gi - 1, t4 + 2)])
                dsy.inc(SY.dma_start(out=xow2[:, slot, :],
                                     in_=xown_h[tt * 128:(tt + 1) * 128, :]), 16)
                X2LD = dsy.v

                TE.wait_ge(vv.sem, O_DONE)
                TE.wait_ge(dsy.sem, OB_DMA)
                if (gi, t4 - 1) in r1_built:
                    TE.wait_ge(vv.sem, r1_built[(gi, t4 - 1)])
                elif gi > 0 and t4 == 0:
                    TE.wait_ge(vv.sem, r1_built[(gi - 1, 3)])
                if t4 == 0:
                    TE.wait_ge(ssm.sem, ssm.v)
                i = None
                for dh in range(2):
                    for kc2 in range(DC):
                        i = TE.matmul(ps[0 + dh][:, 0:512],
                                      o_fm[:, kc2, t4 * 128:(t4 + 1) * 128],
                                      wos[:, kc2, dh * 512:(dh + 1) * 512],
                                      start=(kc2 == 0), stop=(kc2 == DC - 1))
                tsm.inc(i, 1)
                PRJ = tsm.v
                V.wait_ge(tsm.sem, PRJ)
                V.wait_ge(dsy.sem, X2LD)
                hbar(V.tensor_scalar(out=xow2[:, slot, :], in0=xow2[:, slot, :],
                                     scalar1=sown[:, 3, tt:tt + 1],
                                     scalar2=sown[:, 1, tt:tt + 1],
                                     op0=ALU.mult, op1=ALU.add))
                V.tensor_tensor(out=r1t[:, slot, 0:512], in0=xow2[:, slot, 0:512],
                                in1=ps[0][:, 0:512], op=ALU.add)
                vv.inc(V.tensor_tensor(out=r1t[:, slot, 512:1024],
                                       in0=xow2[:, slot, 512:1024],
                                       in1=ps[1][:, 0:512], op=ALU.add), 1)
                r1_built[(gi, t4)] = vv.v
                SY.wait_ge(vv.sem, vv.v)
                dsy.inc(SY.dma_start(out=r1_hbm[tt * 128:(tt + 1) * 128, :],
                                     in_=r1t[:, slot, :]), 16)
                r1_dma[tt] = dsy.v

    C_V = vv.v
    C_S = ssm.v
    C_T = tsm.v
    C_DY = dsy.v
    stBC.close()

    stD = ExitStack()
    sbd = mk(stD)
    wq1T = sbd("wq1T", [128, DC, FF], BF16)
    wq2T = sbd("wq2T", [128, FC, D], BF16)
    h1g = sbd("h1g", [128, FC, 256], BF16)
    x2nf = sbd("x2nf", [128, DC, 2, 256], BF16)
    x2nb = sbd("x2nb", [128, 2, D], BF16)
    r1ln = sbd("r1ln", [128, 2, D], F32)
    r1g = sbd("r1g", [128, 2, D], F32)
    outt = sbd("outt", [128, 1, D], F32)

    SY.wait_ge(vv.sem, C_V)
    SY.wait_ge(ssm.sem, C_S)
    SY.wait_ge(tsm.sem, C_T)
    SY.wait_ge(dsc.sem, QSPILL)
    for dc in range(DC):
        dsy.inc(SY.dma_start(out=wq1T[:, dc, :],
                             in_=wq1_hbm[:, dc * 128:(dc + 1) * 128],
                             transpose=True), 16)
    for fc in range(FC):
        dsy.inc(SY.dma_start(out=wq2T[:, fc, :],
                             in_=wq2_hbm[:, fc * 128:(fc + 1) * 128],
                             transpose=True), 16)
    WQT = dsy.v

    x2n_dma = {}
    for tt in range(NT):
        slot = tt % 2
        SY.wait_ge(dsy.sem, r1_dma[tt])
        if tt >= 2:
            SY.wait_ge(dsy.sem, x2n_dma[tt - 2])
        dsy.inc(SY.dma_start(out=r1ln[:, slot, :],
                             in_=r1_hbm[tt * 128:(tt + 1) * 128, :]), 16)
        V.wait_ge(dsy.sem, dsy.v)
        emit_ln_tile(r1ln[:, slot, :], None, x2n_mode=True,
                     x2n_out=x2nb[:, slot, :])
        SY.wait_ge(vv.sem, vv.v)
        dsy.inc(SY.dma_start(out=x2n_hbm[tt * 128:(tt + 1) * 128, :],
                             in_=x2nb[:, slot, :]), 16)
        x2n_dma[tt] = dsy.v

    TE.wait_ge(ssm.sem, C_S)
    TE.wait_ge(vv.sem, C_V)
    TE.wait_ge(dsy.sem, WQT)
    h1_ev = {}
    ffn1_done = {}
    ffn2_ev = {}
    out_dma = {}
    r1g_ld = {}
    x2nf_ld = {}
    for gq in range(8):
        gslot = gq % 2
        SY.wait_ge(dsy.sem, x2n_dma[2 * gq + 1])
        if gq >= 2:
            SY.wait_ge(tsm.sem, ffn1_done[gq - 2])
        for dc in range(DC):
            dsy.inc(SY.dma_start(out=x2nf[:, dc, gslot, :],
                                 in_=x2n_hbm[gq * 256:(gq + 1) * 256,
                                             dc * 128:(dc + 1) * 128],
                                 transpose=True), 16)
        x2nf_ld[gq] = dsy.v
        if gq >= 1:
            SY.wait_ge(vv.sem, ffn2_ev.get(2 * gq - 1, 0) or 0)
        dsy.inc(SY.dma_start(out=r1g[:].rearrange("p a d -> p (a d)"),
                             in_=r1_hbm[gq * 256:(gq + 1) * 256, :]), 16)
        r1g_ld[gq] = dsy.v

        TE.wait_ge(dsy.sem, x2nf_ld[gq])
        for fc in range(FC):
            bk = fc % 2
            if (gq, fc - 2) in h1_ev:
                TE.wait_ge(ssm.sem, h1_ev[(gq, fc - 2)])
            elif gq > 0 and fc < 2:
                TE.wait_ge(ssm.sem, h1_ev[(gq - 1, FC - 2 + fc)])
            i = None
            for dc in range(DC):
                i = TE.matmul(ps[bk][:, 0:256], wq1T[:, dc, fc * 128:(fc + 1) * 128],
                              x2nf[:, dc, gslot, :], start=(dc == 0),
                              stop=(dc == DC - 1))
            tsm.inc(i, 1)
            SC.wait_ge(tsm.sem, tsm.v)
            if gq >= 1:
                SC.wait_ge(tsm.sem, ffn2_mm.get((gq - 1, 1, 1), 0) or 0)
            ssm.inc(SC.activation(out=h1g[:, fc, :], in_=ps[bk][:, 0:256],
                                  func=AF.Relu, scale=1.0,
                                  bias=b1c[:, fc:fc + 1]), 1)
            h1_ev[(gq, fc)] = ssm.v
        ffn1_done[gq] = tsm.v
        H1 = ssm.v

        if gq == 0:
            ffn2_mm = {}
        TE.wait_ge(ssm.sem, H1)
        for t2 in range(2):
            tt = gq * 2 + t2
            if tt >= 2:
                TE.wait_ge(vv.sem, ffn2_ev[tt - 2])
            for dh in range(2):
                bk = 2 + 2 * t2 + dh
                i = None
                for fc in range(FC):
                    i = TE.matmul(ps[bk][:, 0:512],
                                  h1g[:, fc, t2 * 128:(t2 + 1) * 128],
                                  wq2T[:, fc, dh * 512:(dh + 1) * 512],
                                  start=(fc == 0), stop=False)
                i = TE.matmul(ps[bk][:, 0:512], ones_row[:, 0:128],
                              b2rb[:, dh * 512:(dh + 1) * 512], start=False,
                              stop=True)
                tsm.inc(i, 1)
                ffn2_mm[(gq, t2, dh)] = tsm.v
            FFN2 = tsm.v
            V.wait_ge(tsm.sem, FFN2)
            V.wait_ge(dsy.sem, r1g_ld[gq])
            if tt >= 1:
                V.wait_ge(dsy.sem, out_dma[tt - 1])
            V.tensor_tensor(out=outt[:, 0, 0:512], in0=r1g[:, t2, 0:512],
                            in1=ps[2 + 2 * t2][:, 0:512], op=ALU.add)
            vv.inc(V.tensor_tensor(out=outt[:, 0, 512:1024],
                                   in0=r1g[:, t2, 512:1024],
                                   in1=ps[2 + 2 * t2 + 1][:, 0:512], op=ALU.add), 1)
            ffn2_ev[tt] = vv.v
            SY.wait_ge(vv.sem, vv.v)
            dsy.inc(SY.dma_start(out=out_h[tt * 128:(tt + 1) * 128, :],
                                 in_=outt[:, 0, :]), 16)
            out_dma[tt] = dsy.v

    FINAL = dsy.v
    SY.wait_ge(dsy.sem, FINAL)
    if DEBUG:
        GP.wait_ge(dsy.sem, FINAL)
        for dst, srcd in ((dbg_h, h_hbm), (dbg_srow, srow_hbm), (dbg_s16, s16_hbm),
                          (dbg_idx, idx_dram), (dbg_r1, r1_hbm), (dbg_x2n, x2n_hbm),
                          (dbg_wq1, wq1_hbm), (dbg_wq2, wq2_hbm), (dbg_hg, hg_hbm)):
            dgp.inc(GP.dma_start(out=dst, in_=srcd), 16)
        SY.wait_ge(dsy.sem, FINAL)
        dsy.inc(SY.dma_start(out=dbg_sml[:, 0:16], in_=negC[:]), 16)
        dsy.inc(SY.dma_start(out=dbg_sml[:, 16:18], in_=idxval[:]), 16)
        dsy.inc(SY.dma_start(out=dbg_sml[:, 18:82], in_=sc_tm[:]), 16)
        dsy.inc(SY.dma_start(out=dbg_sml[0:2, 82:90], in_=mqk[:, :, 2], allow_slow_non_contiguous=True), 16)
        dsy.inc(SY.dma_start(out=dbg_sml[0:1, 90:92], in_=thr[:]), 16)
        dsy.inc(SY.dma_start(out=dbg_sml[0:1, 92:93], in_=su11[:]), 16)
        dsy.inc(SY.dma_start(out=dbg_sml[0:1, 93:94], in_=wgam[0:1, 0:1]), 16)
        dsy.inc(SY.dma_start(out=dbg_sml[0:1, 94:95], in_=wgam[0:1, 1:2]), 16)
        dsy.inc(SY.dma_start(out=dbg_i16, in_=idx16[:]), 16)
        dsy.inc(SY.dma_start(out=dbg_wabs, in_=wabs[:]), 16)
        SY.wait_ge(dsy.sem, dsy.v)
        GP.wait_ge(dgp.sem, dgp.v)
    stD.close()

    with nc.Block() as blk:
        def run_engine(name, obj):
            for e, kind, payload in rec.ops:
                if e != name:
                    continue
                if kind == "wait":
                    sem, val = payload
                    obj.wait_ge(sem, val)
                else:
                    meth, args, kwargs, ri = payload
                    inst = getattr(obj, meth)(*args, **kwargs)
                    for sem, n in ri.incs:
                        inst.then_inc(sem, n)

        @blk.sync
        def _(sy):
            run_engine("sync", sy)

        @blk.vector
        def _(v):
            run_engine("vector", v)

        @blk.scalar
        def _(sc):
            run_engine("scalar", sc)

        @blk.tensor
        def _(te):
            run_engine("tensor", te)

        @blk.gpsimd
        def _(g):
            run_engine("gpsimd", g)

    st.close()
    nc.compile()
    return nc


_NC = None


def kernel(L0_attn_x, L0_attn_norm_w, L0_attn_w_q, L0_attn_w_k, L0_attn_w_v,
           L0_attn_w_out, L0_attn_scorer_w, L0_attn_scorer_b, L0_ffn_W1,
           L0_ffn_b1, L0_ffn_W2, L0_ffn_b2):
    global _NC
    if _NC is None:
        _NC = build()
    nc = _NC

    x = np.ascontiguousarray(np.asarray(L0_attn_x, dtype=np.float32))
    nw = np.asarray(L0_attn_norm_w, dtype=np.float32).reshape(D)
    sw = np.asarray(L0_attn_scorer_w, dtype=np.float32).reshape(D)
    wq = np.ascontiguousarray(np.asarray(L0_attn_w_q, dtype=np.float32))
    wk = np.ascontiguousarray(np.asarray(L0_attn_w_k, dtype=np.float32))
    wv = np.ascontiguousarray(np.asarray(L0_attn_w_v, dtype=np.float32))
    wo = np.ascontiguousarray(np.asarray(L0_attn_w_out, dtype=np.float32))
    w1 = np.ascontiguousarray(np.asarray(L0_ffn_W1, dtype=np.float32))
    w2 = np.ascontiguousarray(np.asarray(L0_ffn_W2, dtype=np.float32))
    b1 = np.asarray(L0_ffn_b1, dtype=np.float32).reshape(FF)
    b2 = np.asarray(L0_ffn_b2, dtype=np.float32).reshape(D)

    triu = np.triu(np.ones((128, 128), dtype=np.float32)).astype(ml_dtypes.bfloat16)
    iota256 = np.tile(np.arange(256, dtype=np.float32)[None, :], (128, 1))
    qposr = np.tile(np.arange(512, dtype=np.float32)[None, :], (128, 1))
    cmm = np.zeros((128, 4, 512), dtype=np.float32)
    pp = np.arange(128)[:, None]
    ff = np.arange(512)[None, :]
    for dd in range(4):
        cmm[:, dd, :] = (ff >= pp + 128 * dd).astype(np.float32)
    cmm = cmm.astype(ml_dtypes.bfloat16)
    nwrep = np.tile(nw[None, :], (128, 1))
    nw8 = nw.reshape(DC, 128).T.copy()
    sw8 = sw.reshape(DC, 128).T.copy()
    b1c = b1.reshape(FC, 128).T.copy()
    b2r = b2.reshape(1, D)

    in_maps = []
    for c in range(8):
        b_ = c // 4
        o_ = (c % 4) * SO
        in_maps.append({
            "xb": x[b_], "xown": np.ascontiguousarray(x[b_, o_:o_ + SO]),
            "wq": wq, "wk": wk, "wv": wv, "wo": wo,
            "w1": w1, "w2": w2, "b1c": b1c, "b2r": b2r,
            "nwrep": nwrep, "nw8": nw8, "sw8": sw8,
            "offv": np.array([[float(o_)]], dtype=np.float32),
            "triu": triu, "iota256": iota256, "qposr": qposr, "cm": cmm,
        })
    res = run_bass_kernel_spmd(nc, in_maps, core_ids=list(range(8)))
    full = np.empty((B, S, D), dtype=np.float32)
    for c in range(8):
        b_ = c // 4
        o_ = (c % 4) * SO
        full[b_, o_:o_ + SO] = res.results[c]["out"]
    return full


# version 11
# speedup vs baseline: 1.1644x; 1.0990x over previous
import numpy as np
import ml_dtypes
from contextlib import ExitStack

import concourse.bacc as bacc
import concourse.mybir as mybir
from concourse.bass_utils import run_bass_kernel_spmd

F32 = mybir.dt.float32
BF16 = mybir.dt.bfloat16
I16 = mybir.dt.int16
AF = mybir.ActivationFunctionType
ALU = mybir.AluOpType
AX = mybir.AxisListType

B, S, D = 2, 8192, 1024
H, HD = 16, 64
WIN, G = 1024, 256
SO = 2048
NW = SO // WIN
NT = SO // 128
DC = D // 128
FF = 4096
FC = FF // 128
ONE_MINUS_Q = 255.5 / 8191.0
SCALE = HD ** -0.5


class RInst:
    __slots__ = ("incs",)

    def __init__(self):
        self.incs = []


class Rec:
    def __init__(self):
        self.ops = []

    def op(self, eng, meth, args, kwargs):
        ri = RInst()
        self.ops.append((eng, "op", (meth, args, kwargs, ri)))
        return ri

    def wait(self, eng, sem, val):
        self.ops.append((eng, "wait", (sem, val)))


class REng:
    def __init__(self, rec, name):
        self._rec = rec
        self._name = name

    def wait_ge(self, sem, val):
        self._rec.wait(self._name, sem, val)

    def __getattr__(self, meth):
        if meth.startswith("_"):
            raise AttributeError(meth)

        def call(*args, **kwargs):
            return self._rec.op(self._name, meth, args, kwargs)

        return call


class Ctr:
    def __init__(self, sem):
        self.sem = sem
        self.v = 0

    def inc(self, inst, n=1):
        inst.incs.append((self.sem, n))
        self.v += n
        return self.v

    def wait(self, eng, v=None):
        eng.wait_ge(self.sem, self.v if v is None else v)


DEBUG = False


def build():
    nc = bacc.Bacc("TRN2", target_bir_lowering=False)

    xb = nc.dram_tensor("xb", [S, D], F32, kind="ExternalInput").ap()
    xown_h = nc.dram_tensor("xown", [SO, D], F32, kind="ExternalInput").ap()
    wq_h = nc.dram_tensor("wq", [D, D], F32, kind="ExternalInput").ap()
    wk_h = nc.dram_tensor("wk", [D, D], F32, kind="ExternalInput").ap()
    wv_h = nc.dram_tensor("wv", [D, D], F32, kind="ExternalInput").ap()
    wo_h = nc.dram_tensor("wo", [D, D], F32, kind="ExternalInput").ap()
    w1_h = nc.dram_tensor("w1", [FF, D], F32, kind="ExternalInput").ap()
    w2_h = nc.dram_tensor("w2", [D, FF], F32, kind="ExternalInput").ap()
    b1c_h = nc.dram_tensor("b1c", [128, FC], F32, kind="ExternalInput").ap()
    b2r_h = nc.dram_tensor("b2r", [1, D], F32, kind="ExternalInput").ap()
    nwrep_h = nc.dram_tensor("nwrep", [128, D], F32, kind="ExternalInput").ap()
    nw8_h = nc.dram_tensor("nw8", [128, DC], F32, kind="ExternalInput").ap()
    sw8_h = nc.dram_tensor("sw8", [128, DC], F32, kind="ExternalInput").ap()
    offv_h = nc.dram_tensor("offv", [1, 1], F32, kind="ExternalInput").ap()
    triu_h = nc.dram_tensor("triu", [128, 128], BF16, kind="ExternalInput").ap()
    iota256_h = nc.dram_tensor("iota256", [128, 256], F32, kind="ExternalInput").ap()
    qposr_h = nc.dram_tensor("qposr", [128, 512], F32, kind="ExternalInput").ap()
    cm_h = nc.dram_tensor("cm", [128, 4, 512], BF16, kind="ExternalInput").ap()

    out_h = nc.dram_tensor("out", [SO, D], F32, kind="ExternalOutput").ap()
    if DEBUG:
        dbg_h = nc.dram_tensor("dbg_h", [SO, D], BF16, kind="ExternalOutput").ap()
        dbg_srow = nc.dram_tensor("dbg_srow", [3, S], F32, kind="ExternalOutput").ap()
        dbg_s16 = nc.dram_tensor("dbg_s16", [S], F32, kind="ExternalOutput").ap()
        dbg_idx = nc.dram_tensor("dbg_idx", [256], I16, kind="ExternalOutput").ap()
        dbg_r1 = nc.dram_tensor("dbg_r1", [SO, D], F32, kind="ExternalOutput").ap()
        dbg_x2n = nc.dram_tensor("dbg_x2n", [SO, D], BF16, kind="ExternalOutput").ap()
        dbg_wq1 = nc.dram_tensor("dbg_wq1", [FF, D], BF16, kind="ExternalOutput").ap()
        dbg_wq2 = nc.dram_tensor("dbg_wq2", [D, FF], BF16, kind="ExternalOutput").ap()
        dbg_hg = nc.dram_tensor("dbg_hg", [G, D], BF16, kind="ExternalOutput").ap()
        dbg_sml = nc.dram_tensor("dbg_sml", [128, 96], F32, kind="ExternalOutput").ap()
        dbg_xg = nc.dram_tensor("dbg_xg", [128, 2, D], F32, kind="ExternalOutput").ap()
        dbg_i16 = nc.dram_tensor("dbg_i16", [128, 16], I16, kind="ExternalOutput").ap()
        dbg_wabs = nc.dram_tensor("dbg_wabs", [128, 64], F32, kind="ExternalOutput").ap()

    xh_bf = nc.dram_tensor("xh_bf", [S, D], BF16, kind="Internal").ap()
    h_hbm = nc.dram_tensor("h_hbm", [SO, D], BF16, kind="Internal").ap()
    hg_hbm = nc.dram_tensor("hg_hbm", [G, D], BF16, kind="Internal").ap()
    x2n_hbm = nc.dram_tensor("x2n_hbm", [SO, D], BF16, kind="Internal").ap()
    r1_hbm = nc.dram_tensor("r1_hbm", [SO, D], F32, kind="Internal").ap()
    wq1_hbm = nc.dram_tensor("wq1_hbm", [FF, D], BF16, kind="Internal").ap()
    wq2_hbm = nc.dram_tensor("wq2_hbm", [D, FF], BF16, kind="Internal").ap()
    srow_hbm = nc.dram_tensor("srow_hbm", [3, S], F32, kind="Internal").ap()
    s16_hbm = nc.dram_tensor("s16_hbm", [S], F32, kind="Internal").ap()
    idx_dram = nc.dram_tensor("idx_dram", [256], I16, kind="Internal").ap()
    negc_dram = nc.dram_tensor("negc_dram", [16], F32, kind="Internal").ap()

    st = ExitStack()

    def mk(stack):
        return (lambda n, s, d: stack.enter_context(nc.sbuf_tensor(n + "_s", s, d)))

    sbt = mk(st)
    psf = lambda n, s: st.enter_context(nc.psum_tensor(n + "_p", s, F32))
    mksem = lambda n: Ctr(st.enter_context(nc.semaphore(n)))

    triu = sbt("triu", [128, 128], BF16)
    iota256 = sbt("iota256", [128, 256], F32)
    qposr = sbt("qposr", [128, 512], F32)
    cm = sbt("cm", [128, 4, 512], BF16)
    nwrep = sbt("nwrep", [128, D], F32)
    nw8 = sbt("nw8", [128, DC], F32)
    sw8 = sbt("sw8", [128, DC], F32)
    u8 = sbt("u8", [128, DC], F32)
    u2 = sbt("u2", [128, DC, 2], BF16)
    ones128b = sbt("ones128b", [128, 1], BF16)
    ones128f = sbt("ones128f", [128, 1], F32)
    ones_row = sbt("ones_row", [1, 128], BF16)
    onesA = sbt("onesA", [128, 2], BF16)
    b1c = sbt("b1c", [128, FC], F32)
    b2rb = sbt("b2rb", [1, D], BF16)
    offv = sbt("offv", [1, 1], F32)
    offrep = sbt("offrep", [128, 1], F32)
    stats = sbt("stats", [128, 16], F32)
    sown = sbt("sown", [128, 4, NT], F32)
    lng = sbt("lng", [128, D], F32)
    thr = sbt("thr", [1, 2], F32)
    trep = sbt("trep", [128, 1], F32)
    m_tm = sbt("m_tm", [128, 64], BF16)
    rank3 = sbt("rank3", [128, 64], F32)
    prow = sbt("prow", [1, 64], F32)
    prow_rep = sbt("prow_rep", [128, 64], F32)
    trow = sbt("trow", [1, 64], F32)
    scanb = sbt("scanb", [1, 64], F32)
    ptc = sbt("ptc", [128, 2, 256], BF16)
    rhs2 = sbt("rhs2", [128, 2, 64], BF16)
    idxAB = sbt("idxAB", [128, 4], F32)
    idxval = sbt("idxval", [128, 2], F32)
    idxvi = sbt("idxvi", [128, 2], I16)
    idx16 = sbt("idx16", [128, 16], I16)
    sc_tm = sbt("sc_tm", [128, 64], F32)
    surep = sbt("surep", [16, 1], F32)
    su11 = sbt("su11", [1, 1], F32)
    mqk = sbt("mqk", [2, DC, 4], F32)
    mqs = sbt("mqs", [2, 1], F32)
    negC = sbt("negC", [128, 16], F32)
    negC2 = sbt("negC2", [2, DC], F32)
    negCrow = sbt("negCrow", [1, 16], F32)
    gmsk = sbt("gmsk", [128, 2, 512], BF16)
    ish = sbt("ish", [128, 2], F32)
    wgam = sbt("wgam", [1, 4], F32)
    gacc = sbt("gacc", [1, 64], F32)
    wthr = sbt("wthr", [128, 4], F32)
    wabs = sbt("wabs", [128, 64], F32)
    recrow = sbt("recrow", [65, 1024], F32)
    recrep = sbt("recrep", [64, 1024], F32)

    ps = [psf(f"ps{i}", [128, 512]) for i in range(8)]

    dsy = mksem("dsy")
    dgp = mksem("dgp")
    dsc = mksem("dsc")
    vv = mksem("vv")
    ssm = mksem("ssm")
    tsm = mksem("tsm")
    ggm = mksem("ggm")
    hzv = mksem("hzv")

    rec = Rec()
    SY = REng(rec, "sync")
    V = REng(rec, "vector")
    SC = REng(rec, "scalar")
    TE = REng(rec, "tensor")
    GP = REng(rec, "gpsimd")

    def hbar(inst):
        hzv.inc(inst, 1)
        hzv.wait(V)

    stA = ExitStack()
    sba = mk(stA)
    x_fmh = sba("x_fmh", [128, DC, S // 2], BF16)
    xsq = sba("xsq", [128, DC, 512], BF16)
    xg_sb = sba("xg_sb", [128, 2, D], F32)
    hgt = sba("hgt", [128, 2, D], BF16)
    xow = sba("xow", [128, 2, D], F32)
    how = sba("how", [128, 2, D], BF16)
    srow3 = sba("srow3", [2, 2, 512], F32)
    srow1 = sba("srow1", [1, 2, 512], F32)
    sr_a = sba("sr_a", [16, 512], F32)
    sr_b = sba("sr_b", [16, 512], F32)
    sr_c = sba("sr_c", [16, 512], F32)
    sr_t = sba("sr_t", [16, 6, 512], F32)
    sc16 = sba("sc16", [16, 512], F32)
    wtl = sba("wtl", [128, 4, D], F32)
    wqt = sba("wqt", [128, 2, D], BF16)
    wsc = sba("wsc", [128, 2, D], BF16)
    wsc2 = sba("wsc2", [128, 2, D], BF16)
    wscf = sba("wscf", [128, 2, D], F32)
    lnst = sba("lnst", [128, 8, NT], F32)


    for t, src in ((triu, triu_h), (iota256, iota256_h), (qposr, qposr_h),
                   (cm, cm_h), (nwrep, nwrep_h), (nw8, nw8_h), (sw8, sw8_h),
                   (b1c, b1c_h), (offv, offv_h)):
        dsy.inc(SY.dma_start(out=t[:], in_=src), 16)
    LD_CONST = dsy.v
    dgp.inc(GP.dma_start(out=b2rb[:], in_=b2r_h), 16)

    cast_x = {}
    for i_ in range(8):
        dgp.inc(GP.dma_start(out=xh_bf[i_ * 1024:(i_ + 1) * 1024, :],
                             in_=xb[i_ * 1024:(i_ + 1) * 1024, :]), 16)
        cast_x[i_] = dgp.v

    GP.iota(out=rhs2[:, 0, :], pattern=[[1, 64]], base=0, channel_multiplier=0,
            allow_small_or_imprecise_dtypes=True)
    ggm.inc(GP.iota(out=rhs2[:, 1, :], pattern=[[0, 64]], base=0,
                    channel_multiplier=1,
                    allow_small_or_imprecise_dtypes=True), 1)
    IOTA = ggm.v
    GP.wait_ge(dsy.sem, LD_CONST)
    ggm.inc(GP.partition_broadcast(out_ap=offrep[:], in_ap=offv[:],
                                   channels=128), 1)
    OFFREP = ggm.v

    V.memset(ones128b[:], 1.0)
    V.memset(ones128f[:], 1.0)
    V.memset(ones_row[:], 1.0)
    V.memset(onesA[:], 0.0)
    V.memset(wabs[:], 0.0)
    V.memset(onesA[0:64, 0:1], 1.0)
    V.memset(onesA[64:128, 1:2], 1.0)
    V.wait_ge(dsy.sem, LD_CONST)
    hbar(V.tensor_mul(out=u8[:], in0=nw8[:], in1=sw8[:]))
    for dc in range(DC):
        V.memset(u2[:, dc, 0:1], 1.0)
        V.tensor_copy(out=u2[:, dc, 1:2], in_=u8[:, dc:dc + 1])
    vv.inc(V.engine_nop(), 1)
    PREP = vv.v

    TE.wait_ge(vv.sem, PREP)
    i = None
    for dc in range(DC):
        i = TE.matmul(ps[7][0:1, 0:1], u2[:, dc, 1:2], ones128b[:],
                      start=(dc == 0), stop=(dc == DC - 1))
    tsm.inc(i, 1)
    SC.wait_ge(tsm.sem, tsm.v)
    ssm.inc(SC.activation(out=su11[:], in_=ps[7][0:1, 0:1], func=AF.Copy,
                          scale=-1.0), 1)
    SU_EV = ssm.v
    GP.wait_ge(ssm.sem, SU_EV)
    ggm.inc(GP.partition_broadcast(out_ap=surep[:], in_ap=su11[:], channels=16), 1)
    SU_REP = ggm.v

    mm_done = {}
    ev_done = {}
    ev_dma = {}
    xfm_ld = {}
    TE.wait_ge(ssm.sem, SU_EV)
    for ph in range(2):
        SY.wait_ge(dgp.sem, cast_x[4 * ph + 3])
        if ph == 1:
            SY.wait_ge(tsm.sem, mm_done[7])
            SY.wait_ge(ssm.sem, ev_done[7])
        for dc in range(DC):
            dsy.inc(SY.dma_start(out=x_fmh[:, dc, :],
                                 in_=xh_bf[ph * 4096:(ph + 1) * 4096,
                                           dc * 128:(dc + 1) * 128],
                                 transpose=True), 16)
        xfm_ld[ph] = dsy.v
        SC.wait_ge(dsy.sem, xfm_ld[ph])
        TE.wait_ge(dsy.sem, xfm_ld[ph])
        for g8 in range(8):
            gp = ph * 8 + g8
            if gp >= 1:
                SC.wait_ge(tsm.sem, mm_done[gp - 1])
            i = None
            for dc in range(DC):
                i = SC.activation(out=xsq[:, dc, :],
                                  in_=x_fmh[:, dc, g8 * 512:(g8 + 1) * 512],
                                  func=AF.Square)
            ssm.inc(i, 1)
            SQ = ssm.v
            TE.wait_ge(ssm.sem, SQ)
            if gp >= 3:
                TE.wait_ge(ssm.sem, ev_done[gp - 3])
            slot_a = ps[2 * (g8 % 3)]
            slot_b = ps[2 * (g8 % 3) + 1]
            for dc in range(DC):
                s0 = (dc == 0)
                sp = (dc == DC - 1)
                TE.matmul(slot_a[0:2, :], u2[:, dc, :],
                          x_fmh[:, dc, g8 * 512:(g8 + 1) * 512], start=s0, stop=sp)
                i = TE.matmul(slot_b[0:1, :], ones128b[:], xsq[:, dc, :],
                              start=s0, stop=sp)
            tsm.inc(i, 1)
            mm_done[gp] = tsm.v
            SC.wait_ge(tsm.sem, mm_done[gp])
            if gp >= 2:
                SC.wait_ge(dsy.sem, ev_dma[gp - 2])
            SC.copy(out=srow3[0:2, gp % 2, :], in_=slot_a[0:2, :])
            ssm.inc(SC.copy(out=srow1[0:1, gp % 2, :], in_=slot_b[0:1, :]), 1)
            ev_done[gp] = ssm.v
            SY.wait_ge(ssm.sem, ev_done[gp])
            dsy.inc(SY.dma_start(out=srow_hbm[0:2, gp * 512:(gp + 1) * 512],
                                 in_=srow3[0:2, gp % 2, :]), 16)
            dsy.inc(SY.dma_start(out=srow_hbm[2:3, gp * 512:(gp + 1) * 512],
                                 in_=srow1[0:1, gp % 2, :]), 16)
            ev_dma[gp] = dsy.v
    EV_ALL = dsy.v

    SY.wait_ge(dsy.sem, EV_ALL)
    dsy.inc(SY.dma_start(out=sr_b[:], in_=srow_hbm[0].rearrange("(p c) -> p c", p=16)), 16)
    dsy.inc(SY.dma_start(out=sr_a[:], in_=srow_hbm[1].rearrange("(p c) -> p c", p=16)), 16)
    dsy.inc(SY.dma_start(out=sr_c[:], in_=srow_hbm[2].rearrange("(p c) -> p c", p=16)), 16)
    ROWS16 = dsy.v

    V.wait_ge(dsy.sem, ROWS16)
    V.wait_ge(ggm.sem, SU_REP)
    T = sr_t
    hbar(V.tensor_scalar(out=T[:, 0, :], in0=sr_b[:], scalar1=1.0 / D,
                         scalar2=None, op0=ALU.mult))
    hbar(V.tensor_tensor(out=T[:, 1, :], in0=T[:, 0, :], in1=T[:, 0, :],
                         op=ALU.mult))
    hbar(V.scalar_tensor_tensor(out=T[:, 2, :], in0=sr_c[:], scalar=1.0 / D,
                                in1=T[:, 1, :], op0=ALU.mult, op1=ALU.subtract))
    hbar(V.tensor_scalar(out=T[:, 3, :], in0=T[:, 2, :], scalar1=1e-5,
                         scalar2=None, op0=ALU.add))
    vv.inc(V.scalar_tensor_tensor(out=T[:, 4, :], in0=T[:, 3, :], scalar=1e-6,
                                  in1=T[:, 2, :], op0=ALU.mult, op1=ALU.add), 1)
    RM1 = vv.v
    SC.wait_ge(vv.sem, RM1)
    ssm.inc(SC.activation(out=sr_t[:, 5, :], in_=sr_t[:, 4, :], func=AF.Sqrt), 1)
    RM2 = ssm.v
    V.wait_ge(ssm.sem, RM2)
    hbar(V.reciprocal(out=T[:, 4, :], in_=T[:, 5, :]))
    hbar(V.scalar_tensor_tensor(out=T[:, 1, :], in0=T[:, 0, :],
                                scalar=surep[:, 0:1], in1=sr_a[:],
                                op0=ALU.mult, op1=ALU.add))
    vv.inc(V.tensor_mul(out=sc16[:], in0=T[:, 1, :], in1=T[:, 4, :]), 1)
    SC16 = vv.v

    SY.wait_ge(vv.sem, SC16)
    dsy.inc(SY.dma_start(out=s16_hbm.rearrange("(p c) -> p c", p=16), in_=sc16[:]), 16)
    SY.wait_ge(dsy.sem, dsy.v)
    dsy.inc(SY.dma_start(out=sc_tm[:], in_=s16_hbm.rearrange("(p c) -> p c", p=128)), 16)
    SCTM = dsy.v

    GP.wait_ge(dsy.sem, SCTM)
    ggm.inc(GP.kth_largest(out_ap=thr[:], in_ap=sc_tm[:], n_per_lane=64, k=300,
                           quantile=1.0 - ONE_MINUS_Q), 1)
    ggm.inc(GP.partition_broadcast(out_ap=trep[:], in_ap=thr[0:1, 0:1],
                                   channels=128), 1)
    THR = ggm.v
    def emit_ln_tile(x_ap, h_out, keep_idx=None, x2n_mode=False, x2n_out=None):
        vv.inc(V.reduce_sum(out=stats[:, 0:1], in_=x_ap, axis=AX.X), 1)
        SC.wait_ge(vv.sem, vv.v)
        ssm.inc(SC.activation(out=lng[:], in_=x_ap, func=AF.Square,
                              accum_out=stats[:, 1:2]), 1)
        V.wait_ge(ssm.sem, ssm.v)
        hbar(V.tensor_scalar(out=stats[:, 2:3], in0=stats[:, 0:1],
                             scalar1=1.0 / D, scalar2=None, op0=ALU.mult))
        hbar(V.tensor_tensor(out=stats[:, 3:4], in0=stats[:, 2:3],
                             in1=stats[:, 2:3], op=ALU.mult))
        hbar(V.scalar_tensor_tensor(out=stats[:, 4:5], in0=stats[:, 1:2],
                                    scalar=1.0 / D, in1=stats[:, 3:4],
                                    op0=ALU.mult, op1=ALU.subtract))
        hbar(V.tensor_scalar(out=stats[:, 5:6], in0=stats[:, 4:5], scalar1=1e-5,
                             scalar2=None, op0=ALU.add))
        vv.inc(V.reciprocal(out=stats[:, 6:7], in_=stats[:, 5:6]), 1)
        SC.wait_ge(vv.sem, vv.v)
        ssm.inc(SC.activation(out=stats[:, 7:8], in_=stats[:, 6:7], func=AF.Sqrt), 1)
        V.wait_ge(ssm.sem, ssm.v)
        hbar(V.tensor_tensor(out=stats[:, 8:9], in0=stats[:, 7:8],
                             in1=stats[:, 7:8], op=ALU.mult))
        hbar(V.tensor_tensor(out=stats[:, 8:9], in0=stats[:, 8:9],
                             in1=stats[:, 5:6], op=ALU.mult))
        hbar(V.tensor_scalar(out=stats[:, 8:9], in0=stats[:, 8:9], scalar1=-0.5,
                             scalar2=1.5, op0=ALU.mult, op1=ALU.add))
        hbar(V.tensor_tensor(out=stats[:, 8:9], in0=stats[:, 8:9],
                             in1=stats[:, 7:8], op=ALU.mult))
        hbar(V.tensor_tensor(out=stats[:, 9:10], in0=stats[:, 2:3],
                             in1=stats[:, 8:9], op=ALU.mult))
        hbar(V.tensor_scalar(out=stats[:, 9:10], in0=stats[:, 9:10],
                             scalar1=-1.0, scalar2=None, op0=ALU.mult))
        if x2n_mode:
            vv.inc(V.tensor_scalar(out=x2n_out, in0=x_ap, scalar1=stats[:, 8:9],
                                   scalar2=stats[:, 9:10], op0=ALU.mult,
                                   op1=ALU.add), 1)
            return
        V.tensor_scalar(out=lng[:], in0=x_ap, scalar1=stats[:, 8:9],
                        scalar2=stats[:, 9:10], op0=ALU.mult, op1=ALU.add)
        hbar(V.tensor_tensor(out=stats[:, 10:11], in0=stats[:, 4:5],
                             in1=stats[:, 6:7], op=ALU.mult))
        vv.inc(V.tensor_scalar(out=stats[:, 10:11], in0=stats[:, 10:11],
                               scalar1=1e-6, scalar2=None, op0=ALU.add), 1)
        SC.wait_ge(vv.sem, vv.v)
        ssm.inc(SC.activation(out=stats[:, 11:12], in_=stats[:, 10:11],
                              func=AF.Sqrt), 1)
        V.wait_ge(ssm.sem, ssm.v)
        hbar(V.reciprocal(out=stats[:, 12:13], in_=stats[:, 11:12]))
        vv.inc(V.scalar_tensor_tensor(out=h_out, in0=lng[:],
                                      scalar=stats[:, 12:13], in1=nwrep[:],
                                      op0=ALU.mult, op1=ALU.mult), 1)
        if keep_idx is not None:
            V.tensor_copy(out=sown[:, 0, keep_idx:keep_idx + 1], in_=stats[:, 8:9])
            V.tensor_copy(out=sown[:, 1, keep_idx:keep_idx + 1], in_=stats[:, 9:10])
            vv.inc(V.tensor_scalar(out=sown[:, 3, keep_idx:keep_idx + 1],
                                   in0=stats[:, 8:9], scalar1=1.0, scalar2=None,
                                   op0=ALU.add), 1)


    quant_abs = {}
    quant_hl = {}
    quant_gmm = {}
    quant_red = {}
    qld1 = {}
    TE.wait_ge(ssm.sem, SU_EV)
    for mi in range(2):
        for t in range(32):
            ft = mi * 32 + t
            slot = ft % 4
            if ft >= 4:
                SY.wait_ge(ssm.sem, quant_abs[ft - 4])
            src = (w1_h[t * 128:(t + 1) * 128, :] if mi == 0
                   else w2_h[(t // 4) * 128:(t // 4 + 1) * 128,
                             (t % 4) * 1024:(t % 4 + 1) * 1024])
            dsy.inc(SY.dma_start(out=wtl[:, slot, :], in_=src), 16)
            qld1[ft] = dsy.v
            SC.wait_ge(dsy.sem, qld1[ft])
            if ft >= 2:
                SC.wait_ge(vv.sem, quant_hl[ft - 2])
            ssm.inc(SC.activation(out=wscf[:, ft % 2, :], in_=wtl[:, slot, :],
                                  func=AF.Abs), 1)
            quant_abs[ft] = ssm.v
            V.wait_ge(ssm.sem, quant_abs[ft])
            if ft >= 2:
                V.wait_ge(tsm.sem, quant_gmm[ft - 2])
            hbar(V.tensor_copy(out=wsc[:, ft % 2, :], in_=wscf[:, ft % 2, :]))
            vv.inc(V.tensor_sub(out=wsc2[:, ft % 2, :], in0=wscf[:, ft % 2, :],
                                in1=wsc[:, ft % 2, :]), 1)
            quant_hl[ft] = vv.v
            TE.wait_ge(vv.sem, quant_hl[ft])
            pg = ps[6 + (ft % 2)]
            if ft >= 2:
                TE.wait_ge(vv.sem, quant_red[ft - 2])
            TE.matmul(pg[0:1, 0:512], ones128b[:], wsc[:, ft % 2, 0:512],
                      start=True, stop=False)
            TE.matmul(pg[0:1, 0:512], ones128b[:], wsc[:, ft % 2, 512:1024],
                      start=False, stop=False)
            TE.matmul(pg[0:1, 0:512], ones128b[:], wsc2[:, ft % 2, 0:512],
                      start=False, stop=False)
            tsm.inc(TE.matmul(pg[0:1, 0:512], ones128b[:], wsc2[:, ft % 2, 512:1024],
                              start=False, stop=True), 1)
            quant_gmm[ft] = tsm.v
            V.wait_ge(tsm.sem, quant_gmm[ft])
            vv.inc(V.reduce_sum(out=gacc[0:1, ft:ft + 1], in_=pg[0:1, 0:512],
                                axis=AX.X), 1)
            quant_red[ft] = vv.v
    QABS = ssm.v
    QGMM = tsm.v
    V.wait_ge(tsm.sem, QGMM)
    hbar(V.reduce_sum(out=wgam[0:1, 0:1], in_=gacc[0:1, 0:32], axis=AX.X))
    hbar(V.reduce_sum(out=wgam[0:1, 1:2], in_=gacc[0:1, 32:64], axis=AX.X))
    hbar(V.tensor_scalar(out=wgam[0:1, 0:2], in0=wgam[0:1, 0:2],
                         scalar1=0.5 / (FF * D), scalar2=0.5e-5, op0=ALU.mult,
                         op1=ALU.add))
    vv.inc(V.tensor_scalar(out=wgam[0:1, 2:4], in0=wgam[0:1, 0:2], scalar1=-1.0,
                           scalar2=None, op0=ALU.mult), 1)
    QTHR = vv.v

    def q2_src(ft):
        mi2, t2 = divmod(ft, 32)
        return (w1_h[t2 * 128:(t2 + 1) * 128, :] if mi2 == 0
                else w2_h[(t2 // 4) * 128:(t2 // 4 + 1) * 128,
                          (t2 % 4) * 1024:(t2 % 4 + 1) * 1024])

    qld2 = {}
    qspill_v = {}
    qspill_d = {}
    SY.wait_ge(ssm.sem, QABS)
    for ft in range(4):
        dsy.inc(SY.dma_start(out=wtl[:, ft % 4, :], in_=q2_src(ft)), 16)
        qld2[ft] = dsy.v

    own_h_dma = {}
    red_ev = {}
    sq_ev = {}
    for tt in range(NT):
        slot = tt % 2
        if tt >= 2:
            SY.wait_ge(ssm.sem, sq_ev[tt - 2])
            SY.wait_ge(vv.sem, red_ev[tt - 2])
        dsy.inc(SY.dma_start(out=xow[:, slot, :],
                             in_=xown_h[tt * 128:(tt + 1) * 128, :]), 16)
        xld = dsy.v
        V.wait_ge(dsy.sem, xld)
        vv.inc(V.reduce_sum(out=lnst[:, 0, tt:tt + 1], in_=xow[:, slot, :],
                            axis=AX.X), 1)
        red_ev[tt] = vv.v
        SC.wait_ge(dsy.sem, xld)
        ssm.inc(SC.activation(out=lng[:], in_=xow[:, slot, :], func=AF.Square,
                              accum_out=lnst[:, 1, tt:tt + 1]), 1)
        sq_ev[tt] = ssm.v
    OWN_SQ = ssm.v
    V.wait_ge(ssm.sem, OWN_SQ)
    hbar(V.tensor_scalar(out=lnst[:, 2, :], in0=lnst[:, 0, :], scalar1=1.0 / D,
                         scalar2=None, op0=ALU.mult))
    hbar(V.tensor_tensor(out=lnst[:, 3, :], in0=lnst[:, 2, :], in1=lnst[:, 2, :],
                         op=ALU.mult))
    hbar(V.scalar_tensor_tensor(out=lnst[:, 4, :], in0=lnst[:, 1, :],
                                scalar=1.0 / D, in1=lnst[:, 3, :],
                                op0=ALU.mult, op1=ALU.subtract))
    hbar(V.tensor_scalar(out=lnst[:, 5, :], in0=lnst[:, 4, :], scalar1=1e-5,
                         scalar2=None, op0=ALU.add))
    vv.inc(V.reciprocal(out=lnst[:, 6, :], in_=lnst[:, 5, :]), 1)
    SC.wait_ge(vv.sem, vv.v)
    ssm.inc(SC.activation(out=lnst[:, 7, :], in_=lnst[:, 6, :], func=AF.Sqrt), 1)
    V.wait_ge(ssm.sem, ssm.v)
    hbar(V.tensor_tensor(out=sown[:, 0, :], in0=lnst[:, 7, :], in1=lnst[:, 7, :],
                         op=ALU.mult))
    hbar(V.tensor_tensor(out=sown[:, 0, :], in0=sown[:, 0, :], in1=lnst[:, 5, :],
                         op=ALU.mult))
    hbar(V.tensor_scalar(out=sown[:, 0, :], in0=sown[:, 0, :], scalar1=-0.5,
                         scalar2=1.5, op0=ALU.mult, op1=ALU.add))
    hbar(V.tensor_tensor(out=sown[:, 0, :], in0=sown[:, 0, :], in1=lnst[:, 7, :],
                         op=ALU.mult))
    hbar(V.tensor_tensor(out=sown[:, 1, :], in0=lnst[:, 2, :], in1=sown[:, 0, :],
                         op=ALU.mult))
    hbar(V.tensor_scalar(out=sown[:, 1, :], in0=sown[:, 1, :], scalar1=-1.0,
                         scalar2=None, op0=ALU.mult))
    hbar(V.tensor_scalar(out=sown[:, 3, :], in0=sown[:, 0, :], scalar1=1.0,
                         scalar2=None, op0=ALU.add))
    hbar(V.tensor_tensor(out=lnst[:, 3, :], in0=lnst[:, 4, :], in1=lnst[:, 6, :],
                         op=ALU.mult))
    vv.inc(V.tensor_scalar(out=lnst[:, 3, :], in0=lnst[:, 3, :], scalar1=1e-6,
                           scalar2=None, op0=ALU.add), 1)
    SC.wait_ge(vv.sem, vv.v)
    ssm.inc(SC.activation(out=lnst[:, 2, :], in_=lnst[:, 3, :], func=AF.Sqrt), 1)
    V.wait_ge(ssm.sem, ssm.v)
    hbar(V.reciprocal(out=lnst[:, 0, :], in_=lnst[:, 2, :]))

    ln_ev = {}
    for tt in range(NT):
        slot = tt % 2
        if tt >= 2:
            SY.wait_ge(vv.sem, ln_ev[tt - 2])
            SY.wait_ge(dsy.sem, own_h_dma[tt - 2])
        else:
            SY.wait_ge(ssm.sem, sq_ev[NT - 2 + tt])
            SY.wait_ge(vv.sem, red_ev[NT - 2 + tt])
        dsy.inc(SY.dma_start(out=xow[:, slot, :],
                             in_=xown_h[tt * 128:(tt + 1) * 128, :]), 16)
        xld = dsy.v
        V.wait_ge(dsy.sem, xld)
        hbar(V.tensor_scalar(out=lng[:], in0=xow[:, slot, :],
                             scalar1=sown[:, 0, tt:tt + 1],
                             scalar2=sown[:, 1, tt:tt + 1],
                             op0=ALU.mult, op1=ALU.add))
        vv.inc(V.scalar_tensor_tensor(out=how[:, slot, :], in0=lng[:],
                                      scalar=lnst[:, 0, tt:tt + 1],
                                      in1=nwrep[:], op0=ALU.mult,
                                      op1=ALU.mult), 1)
        ln_ev[tt] = vv.v
        SY.wait_ge(vv.sem, ln_ev[tt])
        dsy.inc(SY.dma_start(out=h_hbm[tt * 128:(tt + 1) * 128, :],
                             in_=how[:, slot, :]), 16)
        own_h_dma[tt] = dsy.v

    V.wait_ge(ggm.sem, THR)
    vv.inc(V.tensor_scalar(out=m_tm[:], in0=sc_tm[:], scalar1=trep[:, 0:1],
                           scalar2=None, op0=ALU.is_ge), 1)
    MASK = vv.v
    TE.wait_ge(vv.sem, MASK)
    TE.matmul(ps[0][0:128, 0:64], triu[:], m_tm[:], start=True, stop=True)
    tsm.inc(TE.matmul(ps[1][0:1, 0:64], ones128b[:], m_tm[:], start=True,
                      stop=True), 1)
    CIN = tsm.v
    V.wait_ge(tsm.sem, CIN)
    hbar(V.tensor_copy(out=trow[:], in_=ps[1][0:1, 0:64]))
    hbar(V.tensor_tensor_scan(out=scanb[:], data0=trow[:], data1=trow[:],
                              initial=0.0, op0=ALU.add, op1=ALU.bypass))
    vv.inc(V.tensor_sub(out=prow[:], in0=scanb[:], in1=trow[:]), 1)
    PROW = vv.v
    GP.wait_ge(vv.sem, PROW)
    ggm.inc(GP.partition_broadcast(out_ap=prow_rep[:], in_ap=prow[:],
                                   channels=128), 1)
    PROWREP = ggm.v
    V.wait_ge(ggm.sem, PROWREP)
    hbar(V.tensor_tensor(out=rank3[:], in0=ps[0][0:128, 0:64], in1=prow_rep[:],
                         op=ALU.add))
    hbar(V.tensor_tensor(out=rank3[:], in0=rank3[:], in1=m_tm[:], op=ALU.mult))
    vv.inc(V.tensor_scalar(out=rank3[:], in0=rank3[:], scalar1=-1.0,
                           scalar2=None, op0=ALU.add), 1)
    TE.wait_ge(ggm.sem, IOTA)
    pt_mm = {}
    for c in range(64):
        if c >= 2:
            V.wait_ge(tsm.sem, pt_mm[c - 2])
        vv.inc(V.tensor_scalar(out=ptc[:, c % 2, :], in0=iota256[:],
                               scalar1=rank3[:, c:c + 1], scalar2=None,
                               op0=ALU.is_equal), 1)
        TE.wait_ge(vv.sem, vv.v)
        buf = ptc[:, c % 2, :]
        s0, s1 = (c == 0), (c == 63)
        TE.matmul(ps[2][0:128, 0:2], buf[:, 0:128], rhs2[:, :, c], start=s0, stop=s1)
        tsm.inc(TE.matmul(ps[3][0:128, 0:2], buf[:, 128:256], rhs2[:, :, c],
                          start=s0, stop=s1), 1)
        pt_mm[c] = tsm.v
    PT_MM = tsm.v
    V.wait_ge(tsm.sem, PT_MM)
    V.tensor_copy(out=idxAB[:, 0:2], in_=ps[2][0:128, 0:2])
    hbar(V.tensor_copy(out=idxAB[:, 2:4], in_=ps[3][0:128, 0:2]))
    V.scalar_tensor_tensor(out=idxval[:, 0:1], in0=idxAB[:, 1:2], scalar=64.0,
                           in1=idxAB[:, 0:1], op0=ALU.mult, op1=ALU.add)
    hbar(V.scalar_tensor_tensor(out=idxval[:, 1:2], in0=idxAB[:, 3:4],
                                scalar=64.0, in1=idxAB[:, 2:3], op0=ALU.mult,
                                op1=ALU.add))
    vv.inc(V.tensor_copy(out=idxvi[:], in_=idxval[:]), 1)
    IDX = vv.v
    SY.wait_ge(vv.sem, IDX)
    dsy.inc(SY.dma_start(out=idx_dram.rearrange("(c p) -> p c", p=128),
                         in_=idxvi[:], allow_slow_non_contiguous=True), 16)
    SY.wait_ge(dsy.sem, dsy.v)
    for kk in range(8):
        dsy.inc(SY.dma_start(out=idx16[16 * kk:16 * kk + 16, :],
                             in_=idx_dram.rearrange("(c p) -> p c", p=16),
                             allow_slow_non_contiguous=True), 16)
    IDX16 = dsy.v
    GP.wait_ge(dsy.sem, IDX16)
    dgp.inc(GP.dma_gather(out_ap=xg_sb[:], in_ap=xb, idxs_ap=idx16[:],
                          num_idxs=256, num_idxs_reg=256, elem_size=D), 16)
    GATH = dgp.v
    if DEBUG:
        GP.wait_ge(dgp.sem, GATH)
        dgp.inc(GP.dma_start(out=dbg_xg, in_=xg_sb[:]), 16)
        GATH = dgp.v


    GP.wait_ge(vv.sem, QTHR)
    GP.partition_broadcast(out_ap=wthr[:, 0:1], in_ap=wgam[0:1, 0:1], channels=128)
    GP.partition_broadcast(out_ap=wthr[:, 1:2], in_ap=wgam[0:1, 1:2], channels=128)
    GP.partition_broadcast(out_ap=wthr[:, 2:3], in_ap=wgam[0:1, 2:3], channels=128)
    ggm.inc(GP.partition_broadcast(out_ap=wthr[:, 3:4], in_ap=wgam[0:1, 3:4],
                                   channels=128), 1)
    QREP = ggm.v

    V.wait_ge(dgp.sem, GATH)
    SC.wait_ge(dgp.sem, GATH)
    for t in range(2):
        emit_ln_tile(xg_sb[:, t, :], hgt[:, t, :])
    HG_OK = vv.v
    SY.wait_ge(vv.sem, HG_OK)
    dsy.inc(SY.dma_start(out=hg_hbm.rearrange("(c p) d -> p c d", p=128),
                         in_=hgt[:]), 16)
    HG_DMA = dsy.v


    V.wait_ge(ggm.sem, QREP)
    for ft in range(64):
        mi = ft // 32
        slot = ft % 4
        V.wait_ge(dsy.sem, qld2[ft])
        if ft >= 2:
            V.wait_ge(dsc.sem, qspill_d[ft - 2])
        hbar(V.tensor_scalar(out=wsc[:, 0, :], in0=wtl[:, slot, :],
                             scalar1=wthr[:, 2 + mi:3 + mi], scalar2=None,
                             op0=ALU.is_le))
        vv.inc(V.scalar_tensor_tensor(out=wqt[:, ft % 2, :],
                                      in0=wtl[:, slot, :],
                                      scalar=wthr[:, mi:mi + 1],
                                      in1=wsc[:, 0, :],
                                      op0=ALU.is_ge, op1=ALU.subtract), 1)
        qspill_v[ft] = vv.v
        SC.wait_ge(vv.sem, qspill_v[ft])
        t = ft % 32
        dst = (wq1_hbm[t * 128:(t + 1) * 128, :] if mi == 0
               else wq2_hbm[(t // 4) * 128:(t // 4 + 1) * 128,
                            (t % 4) * 1024:(t % 4 + 1) * 1024])
        dsc.inc(SC.dma_start(out=dst, in_=wqt[:, ft % 2, :]), 16)
        qspill_d[ft] = dsc.v
        if ft + 4 < 64:
            SY.wait_ge(vv.sem, qspill_v[ft])
            dsy.inc(SY.dma_start(out=wtl[:, (ft + 4) % 4, :],
                                 in_=q2_src(ft + 4)), 16)
            qld2[ft + 4] = dsy.v
    QSPILL = dsc.v

    A_V = vv.v
    A_S = ssm.v
    A_T = tsm.v
    A_DY = dsy.v
    A_DG = dgp.v
    stA.close()

    stBC = ExitStack()
    sbb = mk(stBC)
    wqs = sbb("wqs", [128, DC, D], BF16)
    wks = sbb("wks", [128, DC, D], BF16)
    wvs = sbb("wvs", [128, DC, D], BF16)
    wos = sbb("wos", [128, DC, D], BF16)
    h_fmw = sbb("h_fmw", [128, DC, WIN], BF16)
    q_w = sbb("q_w", [128, DC, WIN], BF16)
    k_w = sbb("k_w", [128, DC, WIN], BF16)
    v_w = sbb("v_w", [128, 8, 16, 65], BF16)
    kg_fm = sbb("kg_fm", [128, DC, 256], BF16)
    vg_tm = sbb("vg_tm", [128, 2, 16, 65], BF16)
    hg_fm = sbb("hg_fm", [128, DC, 256], BF16)
    qsqw = sbb("qsqw", [128, 2, WIN], BF16)
    pP = sbb("pP", [128, 4, 512], BF16)
    o_fm = sbb("o_fm", [128, DC, 512], BF16)
    xow2 = sbb("xow2", [128, 2, D], F32)
    r1t = sbb("r1t", [128, 2, D], F32)
    otmp = sbb("otmp", [64, 512], BF16)

    GP.wait_ge(vv.sem, A_V)
    GP.wait_ge(ssm.sem, A_S)
    GP.wait_ge(tsm.sem, A_T)
    GP.wait_ge(dsy.sem, A_DY)
    GP.wait_ge(dsc.sem, QSPILL)
    for w_sb, w_src in ((wqs, wq_h), (wks, wk_h), (wvs, wv_h), (wos, wo_h)):
        for dc in range(DC):
            dgp.inc(GP.dma_start(out=w_sb[:, dc, :],
                                 in_=w_src[dc * 128:(dc + 1) * 128, :]), 16)
    WLOAD = dgp.v

    SY.wait_ge(dsy.sem, HG_DMA)
    SY.wait_ge(dgp.sem, A_DG)
    SY.wait_ge(vv.sem, A_V)
    SY.wait_ge(ssm.sem, A_S)
    SY.wait_ge(dsc.sem, QSPILL)
    for dc in range(DC):
        dsy.inc(SY.dma_start(out=hg_fm[:, dc, :],
                             in_=hg_hbm[:, dc * 128:(dc + 1) * 128],
                             transpose=True), 16)
    HGFM = dsy.v

    bank_free = {bk: None for bk in range(8)}

    def te_bank(bk):
        if bank_free[bk] is not None:
            TE.wait_ge(vv.sem, bank_free[bk])
            bank_free[bk] = None

    TE.wait_ge(dgp.sem, WLOAD)
    TE.wait_ge(dsy.sem, HGFM)
    for oc in range(DC):
        bk = oc % 2
        te_bank(bk)
        i = None
        for dc in range(DC):
            i = TE.matmul(ps[bk][:, 0:256], wks[:, dc, oc * 128:(oc + 1) * 128],
                          hg_fm[:, dc, :], start=(dc == 0), stop=(dc == DC - 1))
        tsm.inc(i, 1)
        V.wait_ge(tsm.sem, tsm.v)
        vv.inc(V.tensor_copy(out=kg_fm[:, oc, :], in_=ps[bk][:, 0:256]), 1)
        bank_free[bk] = vv.v
    V.memset(vg_tm[:, :, :, 64:65], 1.0)
    for t4 in range(2):
        te_bank(2)
        te_bank(3)
        i = None
        for dc in range(DC):
            TE.matmul(ps[2][:, 0:512], hg_fm[:, dc, t4 * 128:(t4 + 1) * 128],
                      wvs[:, dc, 0:512], start=(dc == 0), stop=(dc == DC - 1))
            i = TE.matmul(ps[3][:, 0:512], hg_fm[:, dc, t4 * 128:(t4 + 1) * 128],
                          wvs[:, dc, 512:1024], start=(dc == 0), stop=(dc == DC - 1))
        tsm.inc(i, 1)
        V.wait_ge(tsm.sem, tsm.v)
        V.tensor_copy(out=vg_tm[:, t4, 0:8, 0:64],
                      in_=ps[2][:, 0:512].rearrange("p (h e) -> p h e", h=8))
        vv.inc(V.tensor_copy(out=vg_tm[:, t4, 8:16, 0:64],
                             in_=ps[3][:, 0:512].rearrange("p (h e) -> p h e", h=8)), 1)
        bank_free[2] = vv.v
        bank_free[3] = vv.v
    KVG_EV = vv.v

    SC.wait_ge(vv.sem, KVG_EV)
    for dc in range(DC):
        ssm.inc(SC.activation(out=qsqw[:, dc % 2, 0:256], in_=kg_fm[:, dc, :],
                              func=AF.Square), 1)
        TE.wait_ge(ssm.sem, ssm.v)
        bk = 4 + dc % 2
        te_bank(bk)
        tsm.inc(TE.matmul(ps[bk][0:2, 0:256], onesA[:], qsqw[:, dc % 2, 0:256],
                          start=True, stop=True), 1)
        V.wait_ge(tsm.sem, tsm.v)
        hbar(V.reduce_max(out=mqk[:, dc, 3:4], in_=ps[bk][0:2, 0:256], axis=AX.X))
        vv.inc(V.engine_nop(), 1)
        bank_free[bk] = vv.v
    KGMAX = vv.v

    r1_built = {}
    r1_dma = {}
    prev_o_ev = None
    prev2_o_ev = None
    expc = {}
    lastAV = {}
    SC.wait_ge(ggm.sem, OFFREP)

    for w in range(NW):
        if w >= 1:
            SY.wait_ge(vv.sem, r1_built[(2 * w - 1, 3)])
        SY.wait_ge(dsy.sem, own_h_dma[w * 8 + 7])
        for dc in range(DC):
            dsy.inc(SY.dma_start(out=h_fmw[:, dc, :],
                                 in_=h_hbm[w * WIN:(w + 1) * WIN,
                                           dc * 128:(dc + 1) * 128],
                                 transpose=True), 16)
        HFW = dsy.v

        TE.wait_ge(dsy.sem, HFW)
        qk_ev = {}
        for which, wmat, dest in (("q", wqs, q_w), ("k", wks, k_w)):
            for oc in range(DC):
                bk = oc % 2
                for half in range(2):
                    te_bank(bk)
                    i = None
                    for dc in range(DC):
                        i = TE.matmul(ps[bk][:, 0:512],
                                      wmat[:, dc, oc * 128:(oc + 1) * 128],
                                      h_fmw[:, dc, half * 512:(half + 1) * 512],
                                      start=(dc == 0), stop=(dc == DC - 1))
                    tsm.inc(i, 1)
                    V.wait_ge(tsm.sem, tsm.v)
                    vv.inc(V.tensor_copy(
                        out=dest[:, oc, half * 512:(half + 1) * 512],
                        in_=ps[bk][:, 0:512]), 1)
                    bank_free[bk] = vv.v
                qk_ev[(which, oc)] = vv.v
        V.memset(v_w[:, :, :, 64:65], 1.0)
        for t8 in range(8):
            te_bank(2)
            te_bank(3)
            i = None
            for dc in range(DC):
                TE.matmul(ps[2][:, 0:512], h_fmw[:, dc, t8 * 128:(t8 + 1) * 128],
                          wvs[:, dc, 0:512], start=(dc == 0), stop=(dc == DC - 1))
                i = TE.matmul(ps[3][:, 0:512], h_fmw[:, dc, t8 * 128:(t8 + 1) * 128],
                              wvs[:, dc, 512:1024], start=(dc == 0),
                              stop=(dc == DC - 1))
            tsm.inc(i, 1)
            V.wait_ge(tsm.sem, tsm.v)
            V.tensor_copy(out=v_w[:, t8, 0:8, 0:64],
                          in_=ps[2][:, 0:512].rearrange("p (h e) -> p h e", h=8))
            vv.inc(V.tensor_copy(out=v_w[:, t8, 8:16, 0:64],
                                 in_=ps[3][:, 0:512].rearrange("p (h e) -> p h e", h=8)), 1)
            bank_free[2] = vv.v
            bank_free[3] = vv.v

        for which, src, col in (("q", q_w, 0), ("k", k_w, 1)):
            for dc in range(DC):
                SC.wait_ge(vv.sem, qk_ev[(which, dc)])
                ssm.inc(SC.activation(out=qsqw[:, dc % 2, :], in_=src[:, dc, :],
                                      func=AF.Square), 1)
                TE.wait_ge(ssm.sem, ssm.v)
                bk = 4 + dc % 2
                for gq2 in range(2):
                    te_bank(bk)
                    tsm.inc(TE.matmul(ps[bk][0:2, 0:512], onesA[:],
                                      qsqw[:, dc % 2, gq2 * 512:(gq2 + 1) * 512],
                                      start=True, stop=True), 1)
                    V.wait_ge(tsm.sem, tsm.v)
                    if gq2 == 0:
                        hbar(V.reduce_max(out=mqk[:, dc, col:col + 1],
                                          in_=ps[bk][0:2, 0:512], axis=AX.X))
                    else:
                        hbar(V.reduce_max(out=mqs[:], in_=ps[bk][0:2, 0:512],
                                          axis=AX.X))
                        hbar(V.tensor_tensor(out=mqk[:, dc, col:col + 1],
                                             in0=mqk[:, dc, col:col + 1],
                                             in1=mqs[:], op=ALU.max))
                    vv.inc(V.engine_nop(), 1)
                    bank_free[bk] = vv.v
        V.wait_ge(vv.sem, KGMAX)
        hbar(V.tensor_tensor(out=mqk[:, :, 1], in0=mqk[:, :, 1], in1=mqk[:, :, 3],
                             op=ALU.max))
        hbar(V.tensor_tensor(out=mqk[:, :, 2], in0=mqk[:, :, 0], in1=mqk[:, :, 1],
                             op=ALU.mult))
        vv.inc(V.engine_nop(), 1)
        SC.wait_ge(vv.sem, vv.v)
        ssm.inc(SC.activation(out=mqk[:, :, 2], in_=mqk[:, :, 2], func=AF.Sqrt), 1)
        V.wait_ge(ssm.sem, ssm.v)
        vv.inc(V.tensor_scalar(out=negC2[:], in0=mqk[:, :, 2], scalar1=-SCALE,
                               scalar2=None, op0=ALU.mult), 1)
        NEGC2 = vv.v
        SY.wait_ge(vv.sem, NEGC2)
        dsy.inc(SY.dma_start(out=negc_dram.rearrange("(c p) -> p c", p=2),
                             in_=negC2[:], allow_slow_non_contiguous=True), 16)
        SY.wait_ge(dsy.sem, dsy.v)
        dsy.inc(SY.dma_start(out=negCrow[:], in_=negc_dram.rearrange("(a n) -> a n", a=1)), 16)
        GP.wait_ge(dsy.sem, dsy.v)
        i = None
        for h in range(16):
            i = GP.partition_broadcast(out_ap=negC[:, h:h + 1],
                                       in_ap=negCrow[0:1, h:h + 1],
                                       channels=128)
        ggm.inc(i, 1)
        NEGC = ggm.v
        SC.wait_ge(ggm.sem, NEGC)
        prev_o_ev = vv.v
        prev2_o_ev = vv.v
        TE.wait_ge(vv.sem, vv.v)

        for qg in range(2):
            ob_dma = {}
            rec_dma = [None]
            gi = w * 2 + qg
            q0 = w * WIN + qg * 512
            for half in range(2):
                hbar(V.tensor_scalar(out=ish[:, half:half + 1],
                                     in0=idxval[:, half:half + 1],
                                     scalar1=offrep[:, 0:1], scalar2=float(-q0),
                                     op0=ALU.subtract, op1=ALU.add))
                vv.inc(V.tensor_scalar(out=gmsk[:, half, :], in0=qposr[:],
                                       scalar1=ish[:, half:half + 1],
                                       scalar2=None, op0=ALU.is_ge), 1)
            GM = vv.v

            nloc = 4 * (qg + 1)
            kts = [("loc", kt) for kt in range(nloc)] + [("gl", 0), ("gl", 1)]
            for kc in range(DC):
                hA, hB = 2 * kc, 2 * kc + 1
                TE.wait_ge(ssm.sem, ssm.v)
                if kc == 0 and (gi - 1, 3) in r1_built:
                    TE.wait_ge(vv.sem, r1_built[(gi - 1, 3)])
                if prev2_o_ev is not None:
                    TE.wait_ge(vv.sem, prev2_o_ev)
                for j, (kind, kt) in enumerate(kts):
                    lgA = ps[j % 2]
                    lgB = ps[2 + j % 2]
                    ksl = slice(kt * 128, kt * 128 + 128)
                    if kind == "loc":
                        lhsA = k_w[0:64, kc, ksl]
                        lhsB = k_w[64:128, kc, ksl]
                    else:
                        lhsA = kg_fm[0:64, kc, ksl]
                        lhsB = kg_fm[64:128, kc, ksl]
                    qsl = slice(qg * 512, qg * 512 + 512)
                    if (gi, kc, j - 2) in expc:
                        TE.wait_ge(ssm.sem, expc[(gi, kc, j - 2)])
                    TE.matmul(lgA[:, 0:512], lhsA, q_w[0:64, kc, qsl],
                              start=True, stop=True)
                    tsm.inc(TE.matmul(lgB[:, 0:512], lhsB, q_w[64:128, kc, qsl],
                                      start=True, stop=True,
                                      tile_position=(64, 0)), 1)
                    qk_mm = tsm.v

                    SC.wait_ge(tsm.sem, qk_mm)
                    if (gi, kc, j - 2) in lastAV:
                        SC.wait_ge(tsm.sem, lastAV[(gi, kc, j - 2)])
                    SC.activation(out=pP[:, j % 2, :], in_=lgA[:, 0:512],
                                  func=AF.Exp, bias=negC[:, hA:hA + 1],
                                  scale=SCALE)
                    ssm.inc(SC.activation(out=pP[:, 2 + j % 2, :],
                                          in_=lgB[:, 0:512], func=AF.Exp,
                                          bias=negC[:, hB:hB + 1], scale=SCALE), 1)
                    expc[(gi, kc, j)] = ssm.v

                    masked = (kind == "gl") or (kind == "loc" and kt >= nloc - 4)
                    if masked:
                        mk_ap = gmsk[:, kt, :] if kind == "gl" else cm[:, kt - 4 * qg, :]
                        V.wait_ge(ssm.sem, expc[(gi, kc, j)])
                        V.wait_ge(vv.sem, GM)
                        V.tensor_tensor(out=pP[:, j % 2, :], in0=pP[:, j % 2, :],
                                        in1=mk_ap, op=ALU.mult)
                        vv.inc(V.tensor_tensor(out=pP[:, 2 + j % 2, :],
                                               in0=pP[:, 2 + j % 2, :],
                                               in1=mk_ap, op=ALU.mult), 1)
                        TE.wait_ge(vv.sem, vv.v)
                    else:
                        TE.wait_ge(ssm.sem, expc[(gi, kc, j)])

                    if kind == "loc":
                        vA = v_w[:, kt, hA, 0:65]
                        vB = v_w[:, kt, hB, 0:65]
                    else:
                        vA = vg_tm[:, kt, hA, 0:65]
                        vB = vg_tm[:, kt, hB, 0:65]
                    s0 = (j == 0)
                    obb = 4 + 2 * (kc % 2)
                    TE.matmul(ps[obb][0:65, 0:512], vA, pP[:, j % 2, :],
                              start=s0, stop=False)
                    tsm.inc(TE.matmul(ps[obb + 1][0:65, 0:512], vB,
                                      pP[:, 2 + j % 2, :],
                                      start=s0, stop=False), 1)
                    lastAV[(gi, kc, j)] = tsm.v
                AV_END = tsm.v

                V.wait_ge(tsm.sem, AV_END)
                obb = 4 + 2 * (kc % 2)
                if rec_dma[0] is not None:
                    V.wait_ge(dsy.sem, rec_dma[0])
                V.reciprocal(out=recrow[64:65, 0:512],
                             in_=ps[obb][64:65, 0:512])
                vv.inc(V.reciprocal(out=recrow[64:65, 512:1024],
                                    in_=ps[obb + 1][64:65, 0:512]), 1)
                SY.wait_ge(vv.sem, vv.v)
                SY.wait_ge(ggm.sem, ggm.v)
                dsy.inc(SY.dma_start(out=recrow[0:1, :],
                                     in_=recrow[64:65, :]), 16)
                rec_dma[0] = dsy.v
                GP.wait_ge(dsy.sem, dsy.v)
                GP.wait_ge(vv.sem, vv.v)
                ggm.inc(GP.partition_broadcast(out_ap=recrep[:],
                                               in_ap=recrow[0:1, :],
                                               channels=64), 1)
                V.wait_ge(ggm.sem, ggm.v)
                V.tensor_tensor(out=o_fm[0:64, kc, :],
                                in0=ps[obb][0:64, 0:512],
                                in1=recrep[:, 0:512], op=ALU.mult)
                if kc >= 1:
                    V.wait_ge(dsy.sem, ob_dma[kc - 1])
                vv.inc(V.tensor_tensor(out=otmp[0:64, :],
                                       in0=ps[obb + 1][0:64, 0:512],
                                       in1=recrep[:, 512:1024],
                                       op=ALU.mult), 1)
                SY.wait_ge(vv.sem, vv.v)
                dsy.inc(SY.dma_start(out=o_fm[64:128, kc, :],
                                     in_=otmp[0:64, :]), 16)
                ob_dma[kc] = dsy.v
                prev2_o_ev = prev_o_ev
                prev_o_ev = vv.v
            O_DONE = vv.v
            OB_DMA = dsy.v

            for t4 in range(4):
                tt = gi * 4 + t4
                slot = t4 % 2
                if t4 >= 2:
                    SY.wait_ge(vv.sem, r1_built[(gi, t4 - 2)])
                elif gi > 0:
                    SY.wait_ge(vv.sem, r1_built[(gi - 1, t4 + 2)])
                dsy.inc(SY.dma_start(out=xow2[:, slot, :],
                                     in_=xown_h[tt * 128:(tt + 1) * 128, :]), 16)
                X2LD = dsy.v

                TE.wait_ge(vv.sem, O_DONE)
                TE.wait_ge(dsy.sem, OB_DMA)
                if (gi, t4 - 1) in r1_built:
                    TE.wait_ge(vv.sem, r1_built[(gi, t4 - 1)])
                elif gi > 0 and t4 == 0:
                    TE.wait_ge(vv.sem, r1_built[(gi - 1, 3)])
                if t4 == 0:
                    TE.wait_ge(ssm.sem, ssm.v)
                i = None
                for dh in range(2):
                    for kc2 in range(DC):
                        i = TE.matmul(ps[0 + dh][:, 0:512],
                                      o_fm[:, kc2, t4 * 128:(t4 + 1) * 128],
                                      wos[:, kc2, dh * 512:(dh + 1) * 512],
                                      start=(kc2 == 0), stop=(kc2 == DC - 1))
                tsm.inc(i, 1)
                PRJ = tsm.v
                V.wait_ge(tsm.sem, PRJ)
                V.wait_ge(dsy.sem, X2LD)
                hbar(V.tensor_scalar(out=xow2[:, slot, :], in0=xow2[:, slot, :],
                                     scalar1=sown[:, 3, tt:tt + 1],
                                     scalar2=sown[:, 1, tt:tt + 1],
                                     op0=ALU.mult, op1=ALU.add))
                V.tensor_tensor(out=r1t[:, slot, 0:512], in0=xow2[:, slot, 0:512],
                                in1=ps[0][:, 0:512], op=ALU.add)
                vv.inc(V.tensor_tensor(out=r1t[:, slot, 512:1024],
                                       in0=xow2[:, slot, 512:1024],
                                       in1=ps[1][:, 0:512], op=ALU.add), 1)
                r1_built[(gi, t4)] = vv.v
                SY.wait_ge(vv.sem, vv.v)
                dsy.inc(SY.dma_start(out=r1_hbm[tt * 128:(tt + 1) * 128, :],
                                     in_=r1t[:, slot, :]), 16)
                r1_dma[tt] = dsy.v

    C_V = vv.v
    C_S = ssm.v
    C_T = tsm.v
    C_DY = dsy.v
    stBC.close()

    stD = ExitStack()
    sbd = mk(stD)
    wq1T = sbd("wq1T", [128, DC, FF], BF16)
    wq2T = sbd("wq2T", [128, FC, D], BF16)
    h1g = sbd("h1g", [128, FC, 256], BF16)
    x2nf = sbd("x2nf", [128, DC, 2, 256], BF16)
    x2nb = sbd("x2nb", [128, 2, D], BF16)
    r1ln = sbd("r1ln", [128, 2, D], F32)
    r1g = sbd("r1g", [128, 2, D], F32)

    SY.wait_ge(vv.sem, C_V)
    SY.wait_ge(ssm.sem, C_S)
    SY.wait_ge(tsm.sem, C_T)
    SY.wait_ge(dsc.sem, QSPILL)
    for dc in range(DC):
        dsy.inc(SY.dma_start(out=wq1T[:, dc, :],
                             in_=wq1_hbm[:, dc * 128:(dc + 1) * 128],
                             transpose=True), 16)
    for fc in range(FC):
        dsy.inc(SY.dma_start(out=wq2T[:, fc, :],
                             in_=wq2_hbm[:, fc * 128:(fc + 1) * 128],
                             transpose=True), 16)
    WQT = dsy.v

    x2n_dma = {}
    for tt in range(NT):
        slot = tt % 2
        SY.wait_ge(dsy.sem, r1_dma[tt])
        if tt >= 2:
            SY.wait_ge(dsy.sem, x2n_dma[tt - 2])
        dsy.inc(SY.dma_start(out=r1ln[:, slot, :],
                             in_=r1_hbm[tt * 128:(tt + 1) * 128, :]), 16)
        V.wait_ge(dsy.sem, dsy.v)
        emit_ln_tile(r1ln[:, slot, :], None, x2n_mode=True,
                     x2n_out=x2nb[:, slot, :])
        SY.wait_ge(vv.sem, vv.v)
        dsy.inc(SY.dma_start(out=x2n_hbm[tt * 128:(tt + 1) * 128, :],
                             in_=x2nb[:, slot, :]), 16)
        x2n_dma[tt] = dsy.v

    TE.wait_ge(ssm.sem, C_S)
    TE.wait_ge(vv.sem, C_V)
    TE.wait_ge(dsy.sem, WQT)
    h1_ev = {}
    ffn1_done = {}
    ffn2_ev = {}
    out_dma = {}
    r1g_ld = {}
    x2nf_ld = {}
    for gq in range(8):
        gslot = gq % 2
        SY.wait_ge(dsy.sem, x2n_dma[2 * gq + 1])
        if gq >= 2:
            SY.wait_ge(tsm.sem, ffn1_done[gq - 2])
        for dc in range(DC):
            dsy.inc(SY.dma_start(out=x2nf[:, dc, gslot, :],
                                 in_=x2n_hbm[gq * 256:(gq + 1) * 256,
                                             dc * 128:(dc + 1) * 128],
                                 transpose=True), 16)
        x2nf_ld[gq] = dsy.v
        if gq >= 1:
            SY.wait_ge(dsy.sem, out_dma[2 * gq - 1])
        dsy.inc(SY.dma_start(out=r1g[:].rearrange("p a d -> p (a d)"),
                             in_=r1_hbm[gq * 256:(gq + 1) * 256, :]), 16)
        r1g_ld[gq] = dsy.v

        TE.wait_ge(dsy.sem, x2nf_ld[gq])
        for fc in range(FC):
            bk = fc % 2
            if (gq, fc - 2) in h1_ev:
                TE.wait_ge(ssm.sem, h1_ev[(gq, fc - 2)])
            elif gq > 0 and fc < 2:
                TE.wait_ge(ssm.sem, h1_ev[(gq - 1, FC - 2 + fc)])
            i = None
            for dc in range(DC):
                i = TE.matmul(ps[bk][:, 0:256], wq1T[:, dc, fc * 128:(fc + 1) * 128],
                              x2nf[:, dc, gslot, :], start=(dc == 0),
                              stop=(dc == DC - 1))
            tsm.inc(i, 1)
            SC.wait_ge(tsm.sem, tsm.v)
            if gq >= 1:
                SC.wait_ge(tsm.sem, ffn2_mm.get((gq - 1, 1, 1), 0) or 0)
            ssm.inc(SC.activation(out=h1g[:, fc, :], in_=ps[bk][:, 0:256],
                                  func=AF.Relu, scale=1.0,
                                  bias=b1c[:, fc:fc + 1]), 1)
            h1_ev[(gq, fc)] = ssm.v
        ffn1_done[gq] = tsm.v
        H1 = ssm.v

        if gq == 0:
            ffn2_mm = {}
        TE.wait_ge(ssm.sem, H1)
        for t2 in range(2):
            tt = gq * 2 + t2
            if tt >= 2:
                TE.wait_ge(vv.sem, ffn2_ev[tt - 2])
            for dh in range(2):
                bk = 2 + 2 * t2 + dh
                i = None
                for fc in range(FC):
                    i = TE.matmul(ps[bk][:, 0:512],
                                  h1g[:, fc, t2 * 128:(t2 + 1) * 128],
                                  wq2T[:, fc, dh * 512:(dh + 1) * 512],
                                  start=(fc == 0), stop=False)
                i = TE.matmul(ps[bk][:, 0:512], ones_row[:, 0:128],
                              b2rb[:, dh * 512:(dh + 1) * 512], start=False,
                              stop=True)
                tsm.inc(i, 1)
                ffn2_mm[(gq, t2, dh)] = tsm.v
            FFN2 = tsm.v
            V.wait_ge(tsm.sem, FFN2)
            V.wait_ge(dsy.sem, r1g_ld[gq])
            if tt >= 1:
                V.wait_ge(dsy.sem, out_dma[tt - 1])
            V.tensor_tensor(out=r1g[:, t2, 0:512], in0=r1g[:, t2, 0:512],
                            in1=ps[2 + 2 * t2][:, 0:512], op=ALU.add)
            vv.inc(V.tensor_tensor(out=r1g[:, t2, 512:1024],
                                   in0=r1g[:, t2, 512:1024],
                                   in1=ps[2 + 2 * t2 + 1][:, 0:512], op=ALU.add), 1)
            ffn2_ev[tt] = vv.v
            SY.wait_ge(vv.sem, vv.v)
            dsy.inc(SY.dma_start(out=out_h[tt * 128:(tt + 1) * 128, :],
                                 in_=r1g[:, t2, :]), 16)
            out_dma[tt] = dsy.v

    FINAL = dsy.v
    SY.wait_ge(dsy.sem, FINAL)
    if DEBUG:
        GP.wait_ge(dsy.sem, FINAL)
        for dst, srcd in ((dbg_h, h_hbm), (dbg_srow, srow_hbm), (dbg_s16, s16_hbm),
                          (dbg_idx, idx_dram), (dbg_r1, r1_hbm), (dbg_x2n, x2n_hbm),
                          (dbg_wq1, wq1_hbm), (dbg_wq2, wq2_hbm), (dbg_hg, hg_hbm)):
            dgp.inc(GP.dma_start(out=dst, in_=srcd), 16)
        SY.wait_ge(dsy.sem, FINAL)
        dsy.inc(SY.dma_start(out=dbg_sml[:, 0:16], in_=negC[:]), 16)
        dsy.inc(SY.dma_start(out=dbg_sml[:, 16:18], in_=idxval[:]), 16)
        dsy.inc(SY.dma_start(out=dbg_sml[:, 18:82], in_=sc_tm[:]), 16)
        dsy.inc(SY.dma_start(out=dbg_sml[0:2, 82:90], in_=mqk[:, :, 2], allow_slow_non_contiguous=True), 16)
        dsy.inc(SY.dma_start(out=dbg_sml[0:1, 90:92], in_=thr[:]), 16)
        dsy.inc(SY.dma_start(out=dbg_sml[0:1, 92:93], in_=su11[:]), 16)
        dsy.inc(SY.dma_start(out=dbg_sml[0:1, 93:94], in_=wgam[0:1, 0:1]), 16)
        dsy.inc(SY.dma_start(out=dbg_sml[0:1, 94:95], in_=wgam[0:1, 1:2]), 16)
        dsy.inc(SY.dma_start(out=dbg_i16, in_=idx16[:]), 16)
        dsy.inc(SY.dma_start(out=dbg_wabs, in_=wabs[:]), 16)
        SY.wait_ge(dsy.sem, dsy.v)
        GP.wait_ge(dgp.sem, dgp.v)
    stD.close()

    with nc.Block() as blk:
        def run_engine(name, obj):
            for e, kind, payload in rec.ops:
                if e != name:
                    continue
                if kind == "wait":
                    sem, val = payload
                    obj.wait_ge(sem, val)
                else:
                    meth, args, kwargs, ri = payload
                    inst = getattr(obj, meth)(*args, **kwargs)
                    for sem, n in ri.incs:
                        inst.then_inc(sem, n)

        @blk.sync
        def _(sy):
            run_engine("sync", sy)

        @blk.vector
        def _(v):
            run_engine("vector", v)

        @blk.scalar
        def _(sc):
            run_engine("scalar", sc)

        @blk.tensor
        def _(te):
            run_engine("tensor", te)

        @blk.gpsimd
        def _(g):
            run_engine("gpsimd", g)

    st.close()
    nc.compile()
    return nc


_NC = None


def kernel(L0_attn_x, L0_attn_norm_w, L0_attn_w_q, L0_attn_w_k, L0_attn_w_v,
           L0_attn_w_out, L0_attn_scorer_w, L0_attn_scorer_b, L0_ffn_W1,
           L0_ffn_b1, L0_ffn_W2, L0_ffn_b2):
    global _NC
    if _NC is None:
        _NC = build()
    nc = _NC

    x = np.ascontiguousarray(np.asarray(L0_attn_x, dtype=np.float32))
    nw = np.asarray(L0_attn_norm_w, dtype=np.float32).reshape(D)
    sw = np.asarray(L0_attn_scorer_w, dtype=np.float32).reshape(D)
    wq = np.ascontiguousarray(np.asarray(L0_attn_w_q, dtype=np.float32))
    wk = np.ascontiguousarray(np.asarray(L0_attn_w_k, dtype=np.float32))
    wv = np.ascontiguousarray(np.asarray(L0_attn_w_v, dtype=np.float32))
    wo = np.ascontiguousarray(np.asarray(L0_attn_w_out, dtype=np.float32))
    w1 = np.ascontiguousarray(np.asarray(L0_ffn_W1, dtype=np.float32))
    w2 = np.ascontiguousarray(np.asarray(L0_ffn_W2, dtype=np.float32))
    b1 = np.asarray(L0_ffn_b1, dtype=np.float32).reshape(FF)
    b2 = np.asarray(L0_ffn_b2, dtype=np.float32).reshape(D)

    triu = np.triu(np.ones((128, 128), dtype=np.float32)).astype(ml_dtypes.bfloat16)
    iota256 = np.tile(np.arange(256, dtype=np.float32)[None, :], (128, 1))
    qposr = np.tile(np.arange(512, dtype=np.float32)[None, :], (128, 1))
    cmm = np.zeros((128, 4, 512), dtype=np.float32)
    pp = np.arange(128)[:, None]
    ff = np.arange(512)[None, :]
    for dd in range(4):
        cmm[:, dd, :] = (ff >= pp + 128 * dd).astype(np.float32)
    cmm = cmm.astype(ml_dtypes.bfloat16)
    nwrep = np.tile(nw[None, :], (128, 1))
    nw8 = nw.reshape(DC, 128).T.copy()
    sw8 = sw.reshape(DC, 128).T.copy()
    b1c = b1.reshape(FC, 128).T.copy()
    b2r = b2.reshape(1, D)

    in_maps = []
    for c in range(8):
        b_ = c // 4
        o_ = (c % 4) * SO
        in_maps.append({
            "xb": x[b_], "xown": np.ascontiguousarray(x[b_, o_:o_ + SO]),
            "wq": wq, "wk": wk, "wv": wv, "wo": wo,
            "w1": w1, "w2": w2, "b1c": b1c, "b2r": b2r,
            "nwrep": nwrep, "nw8": nw8, "sw8": sw8,
            "offv": np.array([[float(o_)]], dtype=np.float32),
            "triu": triu, "iota256": iota256, "qposr": qposr, "cm": cmm,
        })
    res = run_bass_kernel_spmd(nc, in_maps, core_ids=list(range(8)))
    full = np.empty((B, S, D), dtype=np.float32)
    for c in range(8):
        b_ = c // 4
        o_ = (c % 4) * SO
        full[b_, o_:o_ + SO] = res.results[c]["out"]
    return full


# version 12
# speedup vs baseline: 1.1721x; 1.0065x over previous
import numpy as np
import ml_dtypes
from contextlib import ExitStack

import concourse.bacc as bacc
import concourse.mybir as mybir
from concourse.bass_utils import run_bass_kernel_spmd

F32 = mybir.dt.float32
BF16 = mybir.dt.bfloat16
I16 = mybir.dt.int16
AF = mybir.ActivationFunctionType
ALU = mybir.AluOpType
AX = mybir.AxisListType

B, S, D = 2, 8192, 1024
H, HD = 16, 64
WIN, G = 1024, 256
SO = 2048
NW = SO // WIN
NT = SO // 128
DC = D // 128
FF = 4096
FC = FF // 128
ONE_MINUS_Q = 255.5 / 8191.0
SCALE = HD ** -0.5


class RInst:
    __slots__ = ("incs",)

    def __init__(self):
        self.incs = []


class Rec:
    def __init__(self):
        self.ops = []

    def op(self, eng, meth, args, kwargs):
        ri = RInst()
        self.ops.append((eng, "op", (meth, args, kwargs, ri)))
        return ri

    def wait(self, eng, sem, val):
        self.ops.append((eng, "wait", (sem, val)))


class REng:
    def __init__(self, rec, name):
        self._rec = rec
        self._name = name

    def wait_ge(self, sem, val):
        self._rec.wait(self._name, sem, val)

    def __getattr__(self, meth):
        if meth.startswith("_"):
            raise AttributeError(meth)

        def call(*args, **kwargs):
            return self._rec.op(self._name, meth, args, kwargs)

        return call


class Ctr:
    def __init__(self, sem):
        self.sem = sem
        self.v = 0

    def inc(self, inst, n=1):
        inst.incs.append((self.sem, n))
        self.v += n
        return self.v

    def wait(self, eng, v=None):
        eng.wait_ge(self.sem, self.v if v is None else v)


DEBUG = False


def build():
    nc = bacc.Bacc("TRN2", target_bir_lowering=False)

    xb = nc.dram_tensor("xb", [S, D], F32, kind="ExternalInput").ap()
    xown_h = nc.dram_tensor("xown", [SO, D], F32, kind="ExternalInput").ap()
    wq_h = nc.dram_tensor("wq", [D, D], F32, kind="ExternalInput").ap()
    wk_h = nc.dram_tensor("wk", [D, D], F32, kind="ExternalInput").ap()
    wv_h = nc.dram_tensor("wv", [D, D], F32, kind="ExternalInput").ap()
    wo_h = nc.dram_tensor("wo", [D, D], F32, kind="ExternalInput").ap()
    w1_h = nc.dram_tensor("w1", [FF, D], F32, kind="ExternalInput").ap()
    w2_h = nc.dram_tensor("w2", [D, FF], F32, kind="ExternalInput").ap()
    b1c_h = nc.dram_tensor("b1c", [128, FC], F32, kind="ExternalInput").ap()
    b2r_h = nc.dram_tensor("b2r", [1, D], F32, kind="ExternalInput").ap()
    nwrep_h = nc.dram_tensor("nwrep", [128, D], F32, kind="ExternalInput").ap()
    nw8_h = nc.dram_tensor("nw8", [128, DC], F32, kind="ExternalInput").ap()
    sw8_h = nc.dram_tensor("sw8", [128, DC], F32, kind="ExternalInput").ap()
    offv_h = nc.dram_tensor("offv", [1, 1], F32, kind="ExternalInput").ap()
    triu_h = nc.dram_tensor("triu", [128, 128], BF16, kind="ExternalInput").ap()
    iota256_h = nc.dram_tensor("iota256", [128, 256], F32, kind="ExternalInput").ap()
    qposr_h = nc.dram_tensor("qposr", [128, 512], F32, kind="ExternalInput").ap()
    cm_h = nc.dram_tensor("cm", [128, 4, 512], BF16, kind="ExternalInput").ap()

    out_h = nc.dram_tensor("out", [SO, D], F32, kind="ExternalOutput").ap()
    if DEBUG:
        dbg_h = nc.dram_tensor("dbg_h", [SO, D], BF16, kind="ExternalOutput").ap()
        dbg_srow = nc.dram_tensor("dbg_srow", [3, S], F32, kind="ExternalOutput").ap()
        dbg_s16 = nc.dram_tensor("dbg_s16", [S], F32, kind="ExternalOutput").ap()
        dbg_idx = nc.dram_tensor("dbg_idx", [256], I16, kind="ExternalOutput").ap()
        dbg_r1 = nc.dram_tensor("dbg_r1", [SO, D], F32, kind="ExternalOutput").ap()
        dbg_x2n = nc.dram_tensor("dbg_x2n", [SO, D], BF16, kind="ExternalOutput").ap()
        dbg_wq1 = nc.dram_tensor("dbg_wq1", [FF, D], BF16, kind="ExternalOutput").ap()
        dbg_wq2 = nc.dram_tensor("dbg_wq2", [D, FF], BF16, kind="ExternalOutput").ap()
        dbg_hg = nc.dram_tensor("dbg_hg", [G, D], BF16, kind="ExternalOutput").ap()
        dbg_sml = nc.dram_tensor("dbg_sml", [128, 96], F32, kind="ExternalOutput").ap()
        dbg_xg = nc.dram_tensor("dbg_xg", [128, 2, D], F32, kind="ExternalOutput").ap()
        dbg_i16 = nc.dram_tensor("dbg_i16", [128, 16], I16, kind="ExternalOutput").ap()
        dbg_wabs = nc.dram_tensor("dbg_wabs", [128, 64], F32, kind="ExternalOutput").ap()

    xh_bf = nc.dram_tensor("xh_bf", [S, D], BF16, kind="Internal").ap()
    h_hbm = nc.dram_tensor("h_hbm", [SO, D], BF16, kind="Internal").ap()
    hg_hbm = nc.dram_tensor("hg_hbm", [G, D], BF16, kind="Internal").ap()
    x2n_hbm = nc.dram_tensor("x2n_hbm", [SO, D], BF16, kind="Internal").ap()
    r1_hbm = nc.dram_tensor("r1_hbm", [SO, D], F32, kind="Internal").ap()
    wq1_hbm = nc.dram_tensor("wq1_hbm", [FF, D], BF16, kind="Internal").ap()
    wq2_hbm = nc.dram_tensor("wq2_hbm", [D, FF], BF16, kind="Internal").ap()
    srow_hbm = nc.dram_tensor("srow_hbm", [3, S], F32, kind="Internal").ap()
    s16_hbm = nc.dram_tensor("s16_hbm", [S], F32, kind="Internal").ap()
    idx_dram = nc.dram_tensor("idx_dram", [256], I16, kind="Internal").ap()
    negc_dram = nc.dram_tensor("negc_dram", [16], F32, kind="Internal").ap()

    st = ExitStack()

    def mk(stack):
        return (lambda n, s, d: stack.enter_context(nc.sbuf_tensor(n + "_s", s, d)))

    sbt = mk(st)
    psf = lambda n, s: st.enter_context(nc.psum_tensor(n + "_p", s, F32))
    mksem = lambda n: Ctr(st.enter_context(nc.semaphore(n)))

    triu = sbt("triu", [128, 128], BF16)
    iota256 = sbt("iota256", [128, 256], F32)
    qposr = sbt("qposr", [128, 512], F32)
    cm = sbt("cm", [128, 4, 512], BF16)
    nwrep = sbt("nwrep", [128, D], F32)
    nw8 = sbt("nw8", [128, DC], F32)
    sw8 = sbt("sw8", [128, DC], F32)
    u8 = sbt("u8", [128, DC], F32)
    u2 = sbt("u2", [128, DC, 2], BF16)
    ones128b = sbt("ones128b", [128, 1], BF16)
    ones128f = sbt("ones128f", [128, 1], F32)
    ones_row = sbt("ones_row", [1, 128], BF16)
    onesA = sbt("onesA", [128, 2], BF16)
    b1c = sbt("b1c", [128, FC], F32)
    b2rb = sbt("b2rb", [1, D], BF16)
    offv = sbt("offv", [1, 1], F32)
    offrep = sbt("offrep", [128, 1], F32)
    stats = sbt("stats", [128, 16], F32)
    sown = sbt("sown", [128, 4, NT], F32)
    lng = sbt("lng", [128, D], F32)
    thr = sbt("thr", [1, 2], F32)
    trep = sbt("trep", [128, 1], F32)
    m_tm = sbt("m_tm", [128, 64], BF16)
    rank3 = sbt("rank3", [128, 64], F32)
    prow = sbt("prow", [1, 64], F32)
    prow_rep = sbt("prow_rep", [128, 64], F32)
    trow = sbt("trow", [1, 64], F32)
    scanb = sbt("scanb", [1, 64], F32)
    ptc = sbt("ptc", [128, 2, 256], BF16)
    rhs2 = sbt("rhs2", [128, 2, 64], BF16)
    idxAB = sbt("idxAB", [128, 4], F32)
    idxval = sbt("idxval", [128, 2], F32)
    idxvi = sbt("idxvi", [128, 2], I16)
    idx16 = sbt("idx16", [128, 16], I16)
    sc_tm = sbt("sc_tm", [128, 64], F32)
    surep = sbt("surep", [16, 1], F32)
    su11 = sbt("su11", [1, 1], F32)
    mqk = sbt("mqk", [2, DC, 4], F32)
    mqs = sbt("mqs", [2, 1], F32)
    negC = sbt("negC", [128, 16], F32)
    negC2 = sbt("negC2", [2, DC], F32)
    negCrow = sbt("negCrow", [1, 16], F32)
    gmsk = sbt("gmsk", [128, 2, 512], BF16)
    ish = sbt("ish", [128, 2], F32)
    wgam = sbt("wgam", [1, 4], F32)
    gacc = sbt("gacc", [1, 64], F32)
    wthr = sbt("wthr", [128, 4], F32)
    wabs = sbt("wabs", [128, 64], F32)
    recrow = sbt("recrow", [65, 1024], F32)
    recrep = sbt("recrep", [64, 1024], F32)

    ps = [psf(f"ps{i}", [128, 512]) for i in range(8)]

    dsy = mksem("dsy")
    dgp = mksem("dgp")
    dsc = mksem("dsc")
    vv = mksem("vv")
    ssm = mksem("ssm")
    tsm = mksem("tsm")
    ggm = mksem("ggm")
    hzv = mksem("hzv")

    rec = Rec()
    SY = REng(rec, "sync")
    V = REng(rec, "vector")
    SC = REng(rec, "scalar")
    TE = REng(rec, "tensor")
    GP = REng(rec, "gpsimd")

    def hbar(inst):
        hzv.inc(inst, 1)
        hzv.wait(V)

    stA = ExitStack()
    sba = mk(stA)
    x_fmh = sba("x_fmh", [128, DC, S // 2], BF16)
    xsq = sba("xsq", [128, DC, 512], BF16)
    xg_sb = sba("xg_sb", [128, 2, D], F32)
    hgt = sba("hgt", [128, 2, D], BF16)
    xow = sba("xow", [128, 2, D], F32)
    how = sba("how", [128, 2, D], BF16)
    srow3 = sba("srow3", [2, 2, 512], F32)
    srow1 = sba("srow1", [1, 2, 512], F32)
    sr_a = sba("sr_a", [16, 512], F32)
    sr_b = sba("sr_b", [16, 512], F32)
    sr_c = sba("sr_c", [16, 512], F32)
    sr_t = sba("sr_t", [16, 6, 512], F32)
    sc16 = sba("sc16", [16, 512], F32)
    wtl = sba("wtl", [128, 4, D], F32)
    wqt = sba("wqt", [128, 2, D], BF16)
    wsc = sba("wsc", [128, 2, D], BF16)
    wsc2 = sba("wsc2", [128, 2, D], BF16)
    wscf = sba("wscf", [128, 2, D], F32)
    lnst = sba("lnst", [128, 8, NT], F32)


    for t, src in ((triu, triu_h), (iota256, iota256_h), (qposr, qposr_h),
                   (cm, cm_h), (nwrep, nwrep_h), (nw8, nw8_h), (sw8, sw8_h),
                   (b1c, b1c_h), (offv, offv_h)):
        dsy.inc(SY.dma_start(out=t[:], in_=src), 16)
    LD_CONST = dsy.v
    dgp.inc(GP.dma_start(out=b2rb[:], in_=b2r_h), 16)

    cast_x = {}
    for i_ in range(8):
        dgp.inc(GP.dma_start(out=xh_bf[i_ * 1024:(i_ + 1) * 1024, :],
                             in_=xb[i_ * 1024:(i_ + 1) * 1024, :]), 16)
        cast_x[i_] = dgp.v

    GP.iota(out=rhs2[:, 0, :], pattern=[[1, 64]], base=0, channel_multiplier=0,
            allow_small_or_imprecise_dtypes=True)
    ggm.inc(GP.iota(out=rhs2[:, 1, :], pattern=[[0, 64]], base=0,
                    channel_multiplier=1,
                    allow_small_or_imprecise_dtypes=True), 1)
    IOTA = ggm.v
    GP.wait_ge(dsy.sem, LD_CONST)
    ggm.inc(GP.partition_broadcast(out_ap=offrep[:], in_ap=offv[:],
                                   channels=128), 1)
    OFFREP = ggm.v

    V.memset(ones128b[:], 1.0)
    V.memset(ones128f[:], 1.0)
    V.memset(ones_row[:], 1.0)
    V.memset(onesA[:], 0.0)
    V.memset(wabs[:], 0.0)
    V.memset(onesA[0:64, 0:1], 1.0)
    V.memset(onesA[64:128, 1:2], 1.0)
    V.wait_ge(dsy.sem, LD_CONST)
    hbar(V.tensor_mul(out=u8[:], in0=nw8[:], in1=sw8[:]))
    for dc in range(DC):
        V.memset(u2[:, dc, 0:1], 1.0)
        V.tensor_copy(out=u2[:, dc, 1:2], in_=u8[:, dc:dc + 1])
    vv.inc(V.engine_nop(), 1)
    PREP = vv.v

    TE.wait_ge(vv.sem, PREP)
    i = None
    for dc in range(DC):
        i = TE.matmul(ps[7][0:1, 0:1], u2[:, dc, 1:2], ones128b[:],
                      start=(dc == 0), stop=(dc == DC - 1))
    tsm.inc(i, 1)
    SC.wait_ge(tsm.sem, tsm.v)
    ssm.inc(SC.activation(out=su11[:], in_=ps[7][0:1, 0:1], func=AF.Copy,
                          scale=-1.0), 1)
    SU_EV = ssm.v
    GP.wait_ge(ssm.sem, SU_EV)
    ggm.inc(GP.partition_broadcast(out_ap=surep[:], in_ap=su11[:], channels=16), 1)
    SU_REP = ggm.v

    mm_done = {}
    ev_done = {}
    ev_dma = {}
    xfm_ld = {}
    TE.wait_ge(ssm.sem, SU_EV)
    for ph in range(2):
        SY.wait_ge(dgp.sem, cast_x[4 * ph + 3])
        if ph == 1:
            SY.wait_ge(tsm.sem, mm_done[7])
            SY.wait_ge(ssm.sem, ev_done[7])
        for dc in range(DC):
            dsy.inc(SY.dma_start(out=x_fmh[:, dc, :],
                                 in_=xh_bf[ph * 4096:(ph + 1) * 4096,
                                           dc * 128:(dc + 1) * 128],
                                 transpose=True), 16)
        xfm_ld[ph] = dsy.v
        SC.wait_ge(dsy.sem, xfm_ld[ph])
        TE.wait_ge(dsy.sem, xfm_ld[ph])
        for g8 in range(8):
            gp = ph * 8 + g8
            if gp >= 1:
                SC.wait_ge(tsm.sem, mm_done[gp - 1])
            i = None
            for dc in range(DC):
                i = SC.activation(out=xsq[:, dc, :],
                                  in_=x_fmh[:, dc, g8 * 512:(g8 + 1) * 512],
                                  func=AF.Square)
            ssm.inc(i, 1)
            SQ = ssm.v
            TE.wait_ge(ssm.sem, SQ)
            if gp >= 3:
                TE.wait_ge(ssm.sem, ev_done[gp - 3])
            slot_a = ps[2 * (g8 % 3)]
            slot_b = ps[2 * (g8 % 3) + 1]
            for dc in range(DC):
                s0 = (dc == 0)
                sp = (dc == DC - 1)
                TE.matmul(slot_a[0:2, :], u2[:, dc, :],
                          x_fmh[:, dc, g8 * 512:(g8 + 1) * 512], start=s0, stop=sp)
                i = TE.matmul(slot_b[0:1, :], ones128b[:], xsq[:, dc, :],
                              start=s0, stop=sp)
            tsm.inc(i, 1)
            mm_done[gp] = tsm.v
            SC.wait_ge(tsm.sem, mm_done[gp])
            if gp >= 2:
                SC.wait_ge(dsy.sem, ev_dma[gp - 2])
            SC.copy(out=srow3[0:2, gp % 2, :], in_=slot_a[0:2, :])
            ssm.inc(SC.copy(out=srow1[0:1, gp % 2, :], in_=slot_b[0:1, :]), 1)
            ev_done[gp] = ssm.v
            SY.wait_ge(ssm.sem, ev_done[gp])
            dsy.inc(SY.dma_start(out=srow_hbm[0:2, gp * 512:(gp + 1) * 512],
                                 in_=srow3[0:2, gp % 2, :]), 16)
            dsy.inc(SY.dma_start(out=srow_hbm[2:3, gp * 512:(gp + 1) * 512],
                                 in_=srow1[0:1, gp % 2, :]), 16)
            ev_dma[gp] = dsy.v
    EV_ALL = dsy.v

    SY.wait_ge(dsy.sem, EV_ALL)
    dsy.inc(SY.dma_start(out=sr_b[:], in_=srow_hbm[0].rearrange("(p c) -> p c", p=16)), 16)
    dsy.inc(SY.dma_start(out=sr_a[:], in_=srow_hbm[1].rearrange("(p c) -> p c", p=16)), 16)
    dsy.inc(SY.dma_start(out=sr_c[:], in_=srow_hbm[2].rearrange("(p c) -> p c", p=16)), 16)
    ROWS16 = dsy.v

    V.wait_ge(dsy.sem, ROWS16)
    V.wait_ge(ggm.sem, SU_REP)
    T = sr_t
    hbar(V.tensor_scalar(out=T[:, 0, :], in0=sr_b[:], scalar1=1.0 / D,
                         scalar2=None, op0=ALU.mult))
    hbar(V.tensor_tensor(out=T[:, 1, :], in0=T[:, 0, :], in1=T[:, 0, :],
                         op=ALU.mult))
    hbar(V.scalar_tensor_tensor(out=T[:, 2, :], in0=sr_c[:], scalar=1.0 / D,
                                in1=T[:, 1, :], op0=ALU.mult, op1=ALU.subtract))
    hbar(V.tensor_scalar(out=T[:, 3, :], in0=T[:, 2, :], scalar1=1e-5,
                         scalar2=None, op0=ALU.add))
    vv.inc(V.scalar_tensor_tensor(out=T[:, 4, :], in0=T[:, 3, :], scalar=1e-6,
                                  in1=T[:, 2, :], op0=ALU.mult, op1=ALU.add), 1)
    RM1 = vv.v
    SC.wait_ge(vv.sem, RM1)
    ssm.inc(SC.activation(out=sr_t[:, 5, :], in_=sr_t[:, 4, :], func=AF.Sqrt), 1)
    RM2 = ssm.v
    V.wait_ge(ssm.sem, RM2)
    hbar(V.reciprocal(out=T[:, 4, :], in_=T[:, 5, :]))
    hbar(V.scalar_tensor_tensor(out=T[:, 1, :], in0=T[:, 0, :],
                                scalar=surep[:, 0:1], in1=sr_a[:],
                                op0=ALU.mult, op1=ALU.add))
    vv.inc(V.tensor_mul(out=sc16[:], in0=T[:, 1, :], in1=T[:, 4, :]), 1)
    SC16 = vv.v

    SY.wait_ge(vv.sem, SC16)
    dsy.inc(SY.dma_start(out=s16_hbm.rearrange("(p c) -> p c", p=16), in_=sc16[:]), 16)
    SY.wait_ge(dsy.sem, dsy.v)
    dsy.inc(SY.dma_start(out=sc_tm[:], in_=s16_hbm.rearrange("(p c) -> p c", p=128)), 16)
    SCTM = dsy.v

    GP.wait_ge(dsy.sem, SCTM)
    ggm.inc(GP.kth_largest(out_ap=thr[:], in_ap=sc_tm[:], n_per_lane=64, k=300,
                           quantile=1.0 - ONE_MINUS_Q), 1)
    ggm.inc(GP.partition_broadcast(out_ap=trep[:], in_ap=thr[0:1, 0:1],
                                   channels=128), 1)
    THR = ggm.v
    def emit_ln_tile(x_ap, h_out, keep_idx=None, x2n_mode=False, x2n_out=None):
        vv.inc(V.reduce_sum(out=stats[:, 0:1], in_=x_ap, axis=AX.X), 1)
        SC.wait_ge(vv.sem, vv.v)
        ssm.inc(SC.activation(out=lng[:], in_=x_ap, func=AF.Square,
                              accum_out=stats[:, 1:2]), 1)
        V.wait_ge(ssm.sem, ssm.v)
        hbar(V.tensor_scalar(out=stats[:, 2:3], in0=stats[:, 0:1],
                             scalar1=1.0 / D, scalar2=None, op0=ALU.mult))
        hbar(V.tensor_tensor(out=stats[:, 3:4], in0=stats[:, 2:3],
                             in1=stats[:, 2:3], op=ALU.mult))
        hbar(V.scalar_tensor_tensor(out=stats[:, 4:5], in0=stats[:, 1:2],
                                    scalar=1.0 / D, in1=stats[:, 3:4],
                                    op0=ALU.mult, op1=ALU.subtract))
        hbar(V.tensor_scalar(out=stats[:, 5:6], in0=stats[:, 4:5], scalar1=1e-5,
                             scalar2=None, op0=ALU.add))
        vv.inc(V.reciprocal(out=stats[:, 6:7], in_=stats[:, 5:6]), 1)
        SC.wait_ge(vv.sem, vv.v)
        ssm.inc(SC.activation(out=stats[:, 7:8], in_=stats[:, 6:7], func=AF.Sqrt), 1)
        V.wait_ge(ssm.sem, ssm.v)
        hbar(V.tensor_tensor(out=stats[:, 8:9], in0=stats[:, 7:8],
                             in1=stats[:, 7:8], op=ALU.mult))
        hbar(V.tensor_tensor(out=stats[:, 8:9], in0=stats[:, 8:9],
                             in1=stats[:, 5:6], op=ALU.mult))
        hbar(V.tensor_scalar(out=stats[:, 8:9], in0=stats[:, 8:9], scalar1=-0.5,
                             scalar2=1.5, op0=ALU.mult, op1=ALU.add))
        hbar(V.tensor_tensor(out=stats[:, 8:9], in0=stats[:, 8:9],
                             in1=stats[:, 7:8], op=ALU.mult))
        hbar(V.tensor_tensor(out=stats[:, 9:10], in0=stats[:, 2:3],
                             in1=stats[:, 8:9], op=ALU.mult))
        hbar(V.tensor_scalar(out=stats[:, 9:10], in0=stats[:, 9:10],
                             scalar1=-1.0, scalar2=None, op0=ALU.mult))
        if x2n_mode:
            vv.inc(V.tensor_scalar(out=x2n_out, in0=x_ap, scalar1=stats[:, 8:9],
                                   scalar2=stats[:, 9:10], op0=ALU.mult,
                                   op1=ALU.add), 1)
            return
        V.tensor_scalar(out=lng[:], in0=x_ap, scalar1=stats[:, 8:9],
                        scalar2=stats[:, 9:10], op0=ALU.mult, op1=ALU.add)
        hbar(V.tensor_tensor(out=stats[:, 10:11], in0=stats[:, 4:5],
                             in1=stats[:, 6:7], op=ALU.mult))
        vv.inc(V.tensor_scalar(out=stats[:, 10:11], in0=stats[:, 10:11],
                               scalar1=1e-6, scalar2=None, op0=ALU.add), 1)
        SC.wait_ge(vv.sem, vv.v)
        ssm.inc(SC.activation(out=stats[:, 11:12], in_=stats[:, 10:11],
                              func=AF.Sqrt), 1)
        V.wait_ge(ssm.sem, ssm.v)
        hbar(V.reciprocal(out=stats[:, 12:13], in_=stats[:, 11:12]))
        vv.inc(V.scalar_tensor_tensor(out=h_out, in0=lng[:],
                                      scalar=stats[:, 12:13], in1=nwrep[:],
                                      op0=ALU.mult, op1=ALU.mult), 1)
        if keep_idx is not None:
            V.tensor_copy(out=sown[:, 0, keep_idx:keep_idx + 1], in_=stats[:, 8:9])
            V.tensor_copy(out=sown[:, 1, keep_idx:keep_idx + 1], in_=stats[:, 9:10])
            vv.inc(V.tensor_scalar(out=sown[:, 3, keep_idx:keep_idx + 1],
                                   in0=stats[:, 8:9], scalar1=1.0, scalar2=None,
                                   op0=ALU.add), 1)


    quant_abs = {}
    quant_hl = {}
    quant_gmm = {}
    quant_red = {}
    qld1 = {}
    TE.wait_ge(ssm.sem, SU_EV)
    for mi in range(2):
        for t in range(32):
            ft = mi * 32 + t
            slot = ft % 4
            if ft >= 4:
                SY.wait_ge(ssm.sem, quant_abs[ft - 4])
            src = (w1_h[t * 128:(t + 1) * 128, :] if mi == 0
                   else w2_h[(t // 4) * 128:(t // 4 + 1) * 128,
                             (t % 4) * 1024:(t % 4 + 1) * 1024])
            dsy.inc(SY.dma_start(out=wtl[:, slot, :], in_=src), 16)
            qld1[ft] = dsy.v
            SC.wait_ge(dsy.sem, qld1[ft])
            if ft >= 2:
                SC.wait_ge(vv.sem, quant_hl[ft - 2])
            ssm.inc(SC.activation(out=wscf[:, ft % 2, :], in_=wtl[:, slot, :],
                                  func=AF.Abs), 1)
            quant_abs[ft] = ssm.v
            V.wait_ge(ssm.sem, quant_abs[ft])
            if ft >= 2:
                V.wait_ge(tsm.sem, quant_gmm[ft - 2])
            hbar(V.tensor_copy(out=wsc[:, ft % 2, :], in_=wscf[:, ft % 2, :]))
            vv.inc(V.tensor_sub(out=wsc2[:, ft % 2, :], in0=wscf[:, ft % 2, :],
                                in1=wsc[:, ft % 2, :]), 1)
            quant_hl[ft] = vv.v
            TE.wait_ge(vv.sem, quant_hl[ft])
            pg = ps[6 + (ft % 2)]
            if ft >= 2:
                TE.wait_ge(vv.sem, quant_red[ft - 2])
            TE.matmul(pg[0:1, 0:512], ones128b[:], wsc[:, ft % 2, 0:512],
                      start=True, stop=False)
            TE.matmul(pg[0:1, 0:512], ones128b[:], wsc[:, ft % 2, 512:1024],
                      start=False, stop=False)
            TE.matmul(pg[0:1, 0:512], ones128b[:], wsc2[:, ft % 2, 0:512],
                      start=False, stop=False)
            tsm.inc(TE.matmul(pg[0:1, 0:512], ones128b[:], wsc2[:, ft % 2, 512:1024],
                              start=False, stop=True), 1)
            quant_gmm[ft] = tsm.v
            V.wait_ge(tsm.sem, quant_gmm[ft])
            vv.inc(V.reduce_sum(out=gacc[0:1, ft:ft + 1], in_=pg[0:1, 0:512],
                                axis=AX.X), 1)
            quant_red[ft] = vv.v
    QABS = ssm.v
    QGMM = tsm.v
    V.wait_ge(tsm.sem, QGMM)
    hbar(V.reduce_sum(out=wgam[0:1, 0:1], in_=gacc[0:1, 0:32], axis=AX.X))
    hbar(V.reduce_sum(out=wgam[0:1, 1:2], in_=gacc[0:1, 32:64], axis=AX.X))
    hbar(V.tensor_scalar(out=wgam[0:1, 0:2], in0=wgam[0:1, 0:2],
                         scalar1=0.5 / (FF * D), scalar2=0.5e-5, op0=ALU.mult,
                         op1=ALU.add))
    vv.inc(V.tensor_scalar(out=wgam[0:1, 2:4], in0=wgam[0:1, 0:2], scalar1=-1.0,
                           scalar2=None, op0=ALU.mult), 1)
    QTHR = vv.v

    def q2_src(ft):
        mi2, t2 = divmod(ft, 32)
        return (w1_h[t2 * 128:(t2 + 1) * 128, :] if mi2 == 0
                else w2_h[(t2 // 4) * 128:(t2 // 4 + 1) * 128,
                          (t2 % 4) * 1024:(t2 % 4 + 1) * 1024])

    qld2 = {}
    qspill_v = {}
    qspill_d = {}
    SY.wait_ge(ssm.sem, QABS)
    for ft in range(4):
        dsy.inc(SY.dma_start(out=wtl[:, ft % 4, :], in_=q2_src(ft)), 16)
        qld2[ft] = dsy.v

    own_h_dma = {}
    red_ev = {}
    sq_ev = {}
    for tt in range(NT):
        slot = tt % 2
        if tt >= 2:
            SY.wait_ge(ssm.sem, sq_ev[tt - 2])
            SY.wait_ge(vv.sem, red_ev[tt - 2])
        dsy.inc(SY.dma_start(out=xow[:, slot, :],
                             in_=xown_h[tt * 128:(tt + 1) * 128, :]), 16)
        xld = dsy.v
        V.wait_ge(dsy.sem, xld)
        vv.inc(V.reduce_sum(out=lnst[:, 0, tt:tt + 1], in_=xow[:, slot, :],
                            axis=AX.X), 1)
        red_ev[tt] = vv.v
        SC.wait_ge(dsy.sem, xld)
        ssm.inc(SC.activation(out=lng[:], in_=xow[:, slot, :], func=AF.Square,
                              accum_out=lnst[:, 1, tt:tt + 1]), 1)
        sq_ev[tt] = ssm.v
    OWN_SQ = ssm.v
    V.wait_ge(ssm.sem, OWN_SQ)
    hbar(V.tensor_scalar(out=lnst[:, 2, :], in0=lnst[:, 0, :], scalar1=1.0 / D,
                         scalar2=None, op0=ALU.mult))
    hbar(V.tensor_tensor(out=lnst[:, 3, :], in0=lnst[:, 2, :], in1=lnst[:, 2, :],
                         op=ALU.mult))
    hbar(V.scalar_tensor_tensor(out=lnst[:, 4, :], in0=lnst[:, 1, :],
                                scalar=1.0 / D, in1=lnst[:, 3, :],
                                op0=ALU.mult, op1=ALU.subtract))
    hbar(V.tensor_scalar(out=lnst[:, 5, :], in0=lnst[:, 4, :], scalar1=1e-5,
                         scalar2=None, op0=ALU.add))
    vv.inc(V.reciprocal(out=lnst[:, 6, :], in_=lnst[:, 5, :]), 1)
    SC.wait_ge(vv.sem, vv.v)
    ssm.inc(SC.activation(out=lnst[:, 7, :], in_=lnst[:, 6, :], func=AF.Sqrt), 1)
    V.wait_ge(ssm.sem, ssm.v)
    hbar(V.tensor_tensor(out=sown[:, 0, :], in0=lnst[:, 7, :], in1=lnst[:, 7, :],
                         op=ALU.mult))
    hbar(V.tensor_tensor(out=sown[:, 0, :], in0=sown[:, 0, :], in1=lnst[:, 5, :],
                         op=ALU.mult))
    hbar(V.tensor_scalar(out=sown[:, 0, :], in0=sown[:, 0, :], scalar1=-0.5,
                         scalar2=1.5, op0=ALU.mult, op1=ALU.add))
    hbar(V.tensor_tensor(out=sown[:, 0, :], in0=sown[:, 0, :], in1=lnst[:, 7, :],
                         op=ALU.mult))
    hbar(V.tensor_tensor(out=sown[:, 1, :], in0=lnst[:, 2, :], in1=sown[:, 0, :],
                         op=ALU.mult))
    hbar(V.tensor_scalar(out=sown[:, 1, :], in0=sown[:, 1, :], scalar1=-1.0,
                         scalar2=None, op0=ALU.mult))
    hbar(V.tensor_scalar(out=sown[:, 3, :], in0=sown[:, 0, :], scalar1=1.0,
                         scalar2=None, op0=ALU.add))
    hbar(V.tensor_tensor(out=lnst[:, 3, :], in0=lnst[:, 4, :], in1=lnst[:, 6, :],
                         op=ALU.mult))
    vv.inc(V.tensor_scalar(out=lnst[:, 3, :], in0=lnst[:, 3, :], scalar1=1e-6,
                           scalar2=None, op0=ALU.add), 1)
    SC.wait_ge(vv.sem, vv.v)
    ssm.inc(SC.activation(out=lnst[:, 2, :], in_=lnst[:, 3, :], func=AF.Sqrt), 1)
    V.wait_ge(ssm.sem, ssm.v)
    hbar(V.reciprocal(out=lnst[:, 0, :], in_=lnst[:, 2, :]))


    V.wait_ge(ggm.sem, THR)
    vv.inc(V.tensor_scalar(out=m_tm[:], in0=sc_tm[:], scalar1=trep[:, 0:1],
                           scalar2=None, op0=ALU.is_ge), 1)
    MASK = vv.v
    TE.wait_ge(vv.sem, MASK)
    TE.matmul(ps[0][0:128, 0:64], triu[:], m_tm[:], start=True, stop=True)
    tsm.inc(TE.matmul(ps[1][0:1, 0:64], ones128b[:], m_tm[:], start=True,
                      stop=True), 1)
    CIN = tsm.v
    V.wait_ge(tsm.sem, CIN)
    hbar(V.tensor_copy(out=trow[:], in_=ps[1][0:1, 0:64]))
    hbar(V.tensor_tensor_scan(out=scanb[:], data0=trow[:], data1=trow[:],
                              initial=0.0, op0=ALU.add, op1=ALU.bypass))
    vv.inc(V.tensor_sub(out=prow[:], in0=scanb[:], in1=trow[:]), 1)
    PROW = vv.v
    GP.wait_ge(vv.sem, PROW)
    ggm.inc(GP.partition_broadcast(out_ap=prow_rep[:], in_ap=prow[:],
                                   channels=128), 1)
    PROWREP = ggm.v
    V.wait_ge(ggm.sem, PROWREP)
    hbar(V.tensor_tensor(out=rank3[:], in0=ps[0][0:128, 0:64], in1=prow_rep[:],
                         op=ALU.add))
    hbar(V.tensor_tensor(out=rank3[:], in0=rank3[:], in1=m_tm[:], op=ALU.mult))
    vv.inc(V.tensor_scalar(out=rank3[:], in0=rank3[:], scalar1=-1.0,
                           scalar2=None, op0=ALU.add), 1)
    TE.wait_ge(ggm.sem, IOTA)
    pt_mm = {}
    for c in range(64):
        if c >= 2:
            V.wait_ge(tsm.sem, pt_mm[c - 2])
        vv.inc(V.tensor_scalar(out=ptc[:, c % 2, :], in0=iota256[:],
                               scalar1=rank3[:, c:c + 1], scalar2=None,
                               op0=ALU.is_equal), 1)
        TE.wait_ge(vv.sem, vv.v)
        buf = ptc[:, c % 2, :]
        s0, s1 = (c == 0), (c == 63)
        TE.matmul(ps[2][0:128, 0:2], buf[:, 0:128], rhs2[:, :, c], start=s0, stop=s1)
        tsm.inc(TE.matmul(ps[3][0:128, 0:2], buf[:, 128:256], rhs2[:, :, c],
                          start=s0, stop=s1), 1)
        pt_mm[c] = tsm.v
    PT_MM = tsm.v
    V.wait_ge(tsm.sem, PT_MM)
    V.tensor_copy(out=idxAB[:, 0:2], in_=ps[2][0:128, 0:2])
    hbar(V.tensor_copy(out=idxAB[:, 2:4], in_=ps[3][0:128, 0:2]))
    V.scalar_tensor_tensor(out=idxval[:, 0:1], in0=idxAB[:, 1:2], scalar=64.0,
                           in1=idxAB[:, 0:1], op0=ALU.mult, op1=ALU.add)
    hbar(V.scalar_tensor_tensor(out=idxval[:, 1:2], in0=idxAB[:, 3:4],
                                scalar=64.0, in1=idxAB[:, 2:3], op0=ALU.mult,
                                op1=ALU.add))
    vv.inc(V.tensor_copy(out=idxvi[:], in_=idxval[:]), 1)
    IDX = vv.v
    SY.wait_ge(vv.sem, IDX)
    dsy.inc(SY.dma_start(out=idx_dram.rearrange("(c p) -> p c", p=128),
                         in_=idxvi[:], allow_slow_non_contiguous=True), 16)
    SY.wait_ge(dsy.sem, dsy.v)
    for kk in range(8):
        dsy.inc(SY.dma_start(out=idx16[16 * kk:16 * kk + 16, :],
                             in_=idx_dram.rearrange("(c p) -> p c", p=16),
                             allow_slow_non_contiguous=True), 16)
    IDX16 = dsy.v
    GP.wait_ge(dsy.sem, IDX16)
    dgp.inc(GP.dma_gather(out_ap=xg_sb[:], in_ap=xb, idxs_ap=idx16[:],
                          num_idxs=256, num_idxs_reg=256, elem_size=D), 16)
    GATH = dgp.v
    if DEBUG:
        GP.wait_ge(dgp.sem, GATH)
        dgp.inc(GP.dma_start(out=dbg_xg, in_=xg_sb[:]), 16)
        GATH = dgp.v


    ln_ev = {}
    for tt in range(NT):
        slot = tt % 2
        if tt >= 2:
            SY.wait_ge(vv.sem, ln_ev[tt - 2])
            SY.wait_ge(dsy.sem, own_h_dma[tt - 2])
        else:
            SY.wait_ge(ssm.sem, sq_ev[NT - 2 + tt])
            SY.wait_ge(vv.sem, red_ev[NT - 2 + tt])
        dsy.inc(SY.dma_start(out=xow[:, slot, :],
                             in_=xown_h[tt * 128:(tt + 1) * 128, :]), 16)
        xld = dsy.v
        V.wait_ge(dsy.sem, xld)
        hbar(V.tensor_scalar(out=lng[:], in0=xow[:, slot, :],
                             scalar1=sown[:, 0, tt:tt + 1],
                             scalar2=sown[:, 1, tt:tt + 1],
                             op0=ALU.mult, op1=ALU.add))
        vv.inc(V.scalar_tensor_tensor(out=how[:, slot, :], in0=lng[:],
                                      scalar=lnst[:, 0, tt:tt + 1],
                                      in1=nwrep[:], op0=ALU.mult,
                                      op1=ALU.mult), 1)
        ln_ev[tt] = vv.v
        SY.wait_ge(vv.sem, ln_ev[tt])
        dsy.inc(SY.dma_start(out=h_hbm[tt * 128:(tt + 1) * 128, :],
                             in_=how[:, slot, :]), 16)
        own_h_dma[tt] = dsy.v

    GP.wait_ge(vv.sem, QTHR)
    GP.partition_broadcast(out_ap=wthr[:, 0:1], in_ap=wgam[0:1, 0:1], channels=128)
    GP.partition_broadcast(out_ap=wthr[:, 1:2], in_ap=wgam[0:1, 1:2], channels=128)
    GP.partition_broadcast(out_ap=wthr[:, 2:3], in_ap=wgam[0:1, 2:3], channels=128)
    ggm.inc(GP.partition_broadcast(out_ap=wthr[:, 3:4], in_ap=wgam[0:1, 3:4],
                                   channels=128), 1)
    QREP = ggm.v

    V.wait_ge(dgp.sem, GATH)
    SC.wait_ge(dgp.sem, GATH)
    for t in range(2):
        emit_ln_tile(xg_sb[:, t, :], hgt[:, t, :])
    HG_OK = vv.v
    SY.wait_ge(vv.sem, HG_OK)
    dsy.inc(SY.dma_start(out=hg_hbm.rearrange("(c p) d -> p c d", p=128),
                         in_=hgt[:]), 16)
    HG_DMA = dsy.v


    V.wait_ge(ggm.sem, QREP)
    for ft in range(64):
        mi = ft // 32
        slot = ft % 4
        V.wait_ge(dsy.sem, qld2[ft])
        if ft >= 2:
            V.wait_ge(dsc.sem, qspill_d[ft - 2])
        hbar(V.tensor_scalar(out=wsc[:, 0, :], in0=wtl[:, slot, :],
                             scalar1=wthr[:, 2 + mi:3 + mi], scalar2=None,
                             op0=ALU.is_le))
        vv.inc(V.scalar_tensor_tensor(out=wqt[:, ft % 2, :],
                                      in0=wtl[:, slot, :],
                                      scalar=wthr[:, mi:mi + 1],
                                      in1=wsc[:, 0, :],
                                      op0=ALU.is_ge, op1=ALU.subtract), 1)
        qspill_v[ft] = vv.v
        SC.wait_ge(vv.sem, qspill_v[ft])
        t = ft % 32
        dst = (wq1_hbm[t * 128:(t + 1) * 128, :] if mi == 0
               else wq2_hbm[(t // 4) * 128:(t // 4 + 1) * 128,
                            (t % 4) * 1024:(t % 4 + 1) * 1024])
        dsc.inc(SC.dma_start(out=dst, in_=wqt[:, ft % 2, :]), 16)
        qspill_d[ft] = dsc.v
        if ft + 4 < 64:
            SY.wait_ge(vv.sem, qspill_v[ft])
            dsy.inc(SY.dma_start(out=wtl[:, (ft + 4) % 4, :],
                                 in_=q2_src(ft + 4)), 16)
            qld2[ft + 4] = dsy.v
    QSPILL = dsc.v

    A_V = vv.v
    A_S = ssm.v
    A_T = tsm.v
    A_DY = dsy.v
    A_DG = dgp.v
    stA.close()

    stBC = ExitStack()
    sbb = mk(stBC)
    wqs = sbb("wqs", [128, DC, D], BF16)
    wks = sbb("wks", [128, DC, D], BF16)
    wvs = sbb("wvs", [128, DC, D], BF16)
    wos = sbb("wos", [128, DC, D], BF16)
    h_fmw = sbb("h_fmw", [128, DC, WIN], BF16)
    q_w = sbb("q_w", [128, DC, WIN], BF16)
    k_w = sbb("k_w", [128, DC, WIN], BF16)
    v_w = sbb("v_w", [128, 8, 16, 65], BF16)
    kg_fm = sbb("kg_fm", [128, DC, 256], BF16)
    vg_tm = sbb("vg_tm", [128, 2, 16, 65], BF16)
    hg_fm = sbb("hg_fm", [128, DC, 256], BF16)
    qsqw = sbb("qsqw", [128, 2, WIN], BF16)
    pP = sbb("pP", [128, 4, 512], BF16)
    o_fm = sbb("o_fm", [128, DC, 512], BF16)
    xow2 = sbb("xow2", [128, 2, D], F32)
    r1t = sbb("r1t", [128, 2, D], F32)
    otmp = sbb("otmp", [64, 512], BF16)

    GP.wait_ge(vv.sem, A_V)
    GP.wait_ge(ssm.sem, A_S)
    GP.wait_ge(tsm.sem, A_T)
    GP.wait_ge(dsy.sem, A_DY)
    GP.wait_ge(dsc.sem, QSPILL)
    for w_sb, w_src in ((wqs, wq_h), (wks, wk_h), (wvs, wv_h), (wos, wo_h)):
        for dc in range(DC):
            dgp.inc(GP.dma_start(out=w_sb[:, dc, :],
                                 in_=w_src[dc * 128:(dc + 1) * 128, :]), 16)
    WLOAD = dgp.v

    SY.wait_ge(dsy.sem, HG_DMA)
    SY.wait_ge(dgp.sem, A_DG)
    SY.wait_ge(vv.sem, A_V)
    SY.wait_ge(ssm.sem, A_S)
    SY.wait_ge(dsc.sem, QSPILL)
    for dc in range(DC):
        dsy.inc(SY.dma_start(out=hg_fm[:, dc, :],
                             in_=hg_hbm[:, dc * 128:(dc + 1) * 128],
                             transpose=True), 16)
    HGFM = dsy.v

    bank_free = {bk: None for bk in range(8)}

    def te_bank(bk):
        if bank_free[bk] is not None:
            TE.wait_ge(vv.sem, bank_free[bk])
            bank_free[bk] = None

    TE.wait_ge(dgp.sem, WLOAD)
    TE.wait_ge(dsy.sem, HGFM)
    for oc in range(DC):
        bk = oc % 2
        te_bank(bk)
        i = None
        for dc in range(DC):
            i = TE.matmul(ps[bk][:, 0:256], wks[:, dc, oc * 128:(oc + 1) * 128],
                          hg_fm[:, dc, :], start=(dc == 0), stop=(dc == DC - 1))
        tsm.inc(i, 1)
        V.wait_ge(tsm.sem, tsm.v)
        vv.inc(V.tensor_copy(out=kg_fm[:, oc, :], in_=ps[bk][:, 0:256]), 1)
        bank_free[bk] = vv.v
    V.memset(vg_tm[:, :, :, 64:65], 1.0)
    for t4 in range(2):
        te_bank(2)
        te_bank(3)
        i = None
        for dc in range(DC):
            TE.matmul(ps[2][:, 0:512], hg_fm[:, dc, t4 * 128:(t4 + 1) * 128],
                      wvs[:, dc, 0:512], start=(dc == 0), stop=(dc == DC - 1))
            i = TE.matmul(ps[3][:, 0:512], hg_fm[:, dc, t4 * 128:(t4 + 1) * 128],
                          wvs[:, dc, 512:1024], start=(dc == 0), stop=(dc == DC - 1))
        tsm.inc(i, 1)
        V.wait_ge(tsm.sem, tsm.v)
        V.tensor_copy(out=vg_tm[:, t4, 0:8, 0:64],
                      in_=ps[2][:, 0:512].rearrange("p (h e) -> p h e", h=8))
        vv.inc(V.tensor_copy(out=vg_tm[:, t4, 8:16, 0:64],
                             in_=ps[3][:, 0:512].rearrange("p (h e) -> p h e", h=8)), 1)
        bank_free[2] = vv.v
        bank_free[3] = vv.v
    KVG_EV = vv.v

    SC.wait_ge(vv.sem, KVG_EV)
    for dc in range(DC):
        ssm.inc(SC.activation(out=qsqw[:, dc % 2, 0:256], in_=kg_fm[:, dc, :],
                              func=AF.Square), 1)
        TE.wait_ge(ssm.sem, ssm.v)
        bk = 4 + dc % 2
        te_bank(bk)
        tsm.inc(TE.matmul(ps[bk][0:2, 0:256], onesA[:], qsqw[:, dc % 2, 0:256],
                          start=True, stop=True), 1)
        V.wait_ge(tsm.sem, tsm.v)
        hbar(V.reduce_max(out=mqk[:, dc, 3:4], in_=ps[bk][0:2, 0:256], axis=AX.X))
        vv.inc(V.engine_nop(), 1)
        bank_free[bk] = vv.v
    KGMAX = vv.v

    r1_built = {}
    r1_dma = {}
    prev_o_ev = None
    prev2_o_ev = None
    expc = {}
    lastAV = {}
    SC.wait_ge(ggm.sem, OFFREP)

    for w in range(NW):
        if w >= 1:
            SY.wait_ge(vv.sem, r1_built[(2 * w - 1, 3)])
        SY.wait_ge(dsy.sem, own_h_dma[w * 8 + 7])
        for dc in range(DC):
            dsy.inc(SY.dma_start(out=h_fmw[:, dc, :],
                                 in_=h_hbm[w * WIN:(w + 1) * WIN,
                                           dc * 128:(dc + 1) * 128],
                                 transpose=True), 16)
        HFW = dsy.v

        TE.wait_ge(dsy.sem, HFW)
        qk_ev = {}
        for which, wmat, dest in (("q", wqs, q_w), ("k", wks, k_w)):
            for oc in range(DC):
                bk = oc % 2
                for half in range(2):
                    te_bank(bk)
                    i = None
                    for dc in range(DC):
                        i = TE.matmul(ps[bk][:, 0:512],
                                      wmat[:, dc, oc * 128:(oc + 1) * 128],
                                      h_fmw[:, dc, half * 512:(half + 1) * 512],
                                      start=(dc == 0), stop=(dc == DC - 1))
                    tsm.inc(i, 1)
                    V.wait_ge(tsm.sem, tsm.v)
                    vv.inc(V.tensor_copy(
                        out=dest[:, oc, half * 512:(half + 1) * 512],
                        in_=ps[bk][:, 0:512]), 1)
                    bank_free[bk] = vv.v
                qk_ev[(which, oc)] = vv.v
        V.memset(v_w[:, :, :, 64:65], 1.0)
        for t8 in range(8):
            te_bank(2)
            te_bank(3)
            i = None
            for dc in range(DC):
                TE.matmul(ps[2][:, 0:512], h_fmw[:, dc, t8 * 128:(t8 + 1) * 128],
                          wvs[:, dc, 0:512], start=(dc == 0), stop=(dc == DC - 1))
                i = TE.matmul(ps[3][:, 0:512], h_fmw[:, dc, t8 * 128:(t8 + 1) * 128],
                              wvs[:, dc, 512:1024], start=(dc == 0),
                              stop=(dc == DC - 1))
            tsm.inc(i, 1)
            V.wait_ge(tsm.sem, tsm.v)
            V.tensor_copy(out=v_w[:, t8, 0:8, 0:64],
                          in_=ps[2][:, 0:512].rearrange("p (h e) -> p h e", h=8))
            vv.inc(V.tensor_copy(out=v_w[:, t8, 8:16, 0:64],
                                 in_=ps[3][:, 0:512].rearrange("p (h e) -> p h e", h=8)), 1)
            bank_free[2] = vv.v
            bank_free[3] = vv.v

        for which, src, col in (("q", q_w, 0), ("k", k_w, 1)):
            for dc in range(DC):
                SC.wait_ge(vv.sem, qk_ev[(which, dc)])
                ssm.inc(SC.activation(out=qsqw[:, dc % 2, :], in_=src[:, dc, :],
                                      func=AF.Square), 1)
                TE.wait_ge(ssm.sem, ssm.v)
                bk = 4 + dc % 2
                for gq2 in range(2):
                    te_bank(bk)
                    tsm.inc(TE.matmul(ps[bk][0:2, 0:512], onesA[:],
                                      qsqw[:, dc % 2, gq2 * 512:(gq2 + 1) * 512],
                                      start=True, stop=True), 1)
                    V.wait_ge(tsm.sem, tsm.v)
                    if gq2 == 0:
                        hbar(V.reduce_max(out=mqk[:, dc, col:col + 1],
                                          in_=ps[bk][0:2, 0:512], axis=AX.X))
                    else:
                        hbar(V.reduce_max(out=mqs[:], in_=ps[bk][0:2, 0:512],
                                          axis=AX.X))
                        hbar(V.tensor_tensor(out=mqk[:, dc, col:col + 1],
                                             in0=mqk[:, dc, col:col + 1],
                                             in1=mqs[:], op=ALU.max))
                    vv.inc(V.engine_nop(), 1)
                    bank_free[bk] = vv.v
        V.wait_ge(vv.sem, KGMAX)
        hbar(V.tensor_tensor(out=mqk[:, :, 1], in0=mqk[:, :, 1], in1=mqk[:, :, 3],
                             op=ALU.max))
        hbar(V.tensor_tensor(out=mqk[:, :, 2], in0=mqk[:, :, 0], in1=mqk[:, :, 1],
                             op=ALU.mult))
        vv.inc(V.engine_nop(), 1)
        SC.wait_ge(vv.sem, vv.v)
        ssm.inc(SC.activation(out=mqk[:, :, 2], in_=mqk[:, :, 2], func=AF.Sqrt), 1)
        V.wait_ge(ssm.sem, ssm.v)
        vv.inc(V.tensor_scalar(out=negC2[:], in0=mqk[:, :, 2], scalar1=-SCALE,
                               scalar2=None, op0=ALU.mult), 1)
        NEGC2 = vv.v
        SY.wait_ge(vv.sem, NEGC2)
        dsy.inc(SY.dma_start(out=negc_dram.rearrange("(c p) -> p c", p=2),
                             in_=negC2[:], allow_slow_non_contiguous=True), 16)
        SY.wait_ge(dsy.sem, dsy.v)
        dsy.inc(SY.dma_start(out=negCrow[:], in_=negc_dram.rearrange("(a n) -> a n", a=1)), 16)
        GP.wait_ge(dsy.sem, dsy.v)
        i = None
        for h in range(16):
            i = GP.partition_broadcast(out_ap=negC[:, h:h + 1],
                                       in_ap=negCrow[0:1, h:h + 1],
                                       channels=128)
        ggm.inc(i, 1)
        NEGC = ggm.v
        SC.wait_ge(ggm.sem, NEGC)
        prev_o_ev = vv.v
        prev2_o_ev = vv.v
        TE.wait_ge(vv.sem, vv.v)

        for qg in range(2):
            ob_dma = {}
            rec_dma = [None]
            gi = w * 2 + qg
            q0 = w * WIN + qg * 512
            for half in range(2):
                hbar(V.tensor_scalar(out=ish[:, half:half + 1],
                                     in0=idxval[:, half:half + 1],
                                     scalar1=offrep[:, 0:1], scalar2=float(-q0),
                                     op0=ALU.subtract, op1=ALU.add))
                vv.inc(V.tensor_scalar(out=gmsk[:, half, :], in0=qposr[:],
                                       scalar1=ish[:, half:half + 1],
                                       scalar2=None, op0=ALU.is_ge), 1)
            GM = vv.v

            nloc = 4 * (qg + 1)
            kts = [("loc", kt) for kt in range(nloc)] + [("gl", 0), ("gl", 1)]
            for kc in range(DC):
                hA, hB = 2 * kc, 2 * kc + 1
                TE.wait_ge(ssm.sem, ssm.v)
                if kc == 0 and (gi - 1, 3) in r1_built:
                    TE.wait_ge(vv.sem, r1_built[(gi - 1, 3)])
                if prev2_o_ev is not None:
                    TE.wait_ge(vv.sem, prev2_o_ev)
                for j, (kind, kt) in enumerate(kts):
                    lgA = ps[j % 2]
                    lgB = ps[2 + j % 2]
                    ksl = slice(kt * 128, kt * 128 + 128)
                    if kind == "loc":
                        lhsA = k_w[0:64, kc, ksl]
                        lhsB = k_w[64:128, kc, ksl]
                    else:
                        lhsA = kg_fm[0:64, kc, ksl]
                        lhsB = kg_fm[64:128, kc, ksl]
                    qsl = slice(qg * 512, qg * 512 + 512)
                    if (gi, kc, j - 2) in expc:
                        TE.wait_ge(ssm.sem, expc[(gi, kc, j - 2)])
                    TE.matmul(lgA[:, 0:512], lhsA, q_w[0:64, kc, qsl],
                              start=True, stop=True)
                    tsm.inc(TE.matmul(lgB[:, 0:512], lhsB, q_w[64:128, kc, qsl],
                                      start=True, stop=True,
                                      tile_position=(64, 0)), 1)
                    qk_mm = tsm.v

                    SC.wait_ge(tsm.sem, qk_mm)
                    if (gi, kc, j - 2) in lastAV:
                        SC.wait_ge(tsm.sem, lastAV[(gi, kc, j - 2)])
                    SC.activation(out=pP[:, j % 2, :], in_=lgA[:, 0:512],
                                  func=AF.Exp, bias=negC[:, hA:hA + 1],
                                  scale=SCALE)
                    ssm.inc(SC.activation(out=pP[:, 2 + j % 2, :],
                                          in_=lgB[:, 0:512], func=AF.Exp,
                                          bias=negC[:, hB:hB + 1], scale=SCALE), 1)
                    expc[(gi, kc, j)] = ssm.v

                    masked = (kind == "gl") or (kind == "loc" and kt >= nloc - 4)
                    if masked:
                        mk_ap = gmsk[:, kt, :] if kind == "gl" else cm[:, kt - 4 * qg, :]
                        V.wait_ge(ssm.sem, expc[(gi, kc, j)])
                        V.wait_ge(vv.sem, GM)
                        V.tensor_tensor(out=pP[:, j % 2, :], in0=pP[:, j % 2, :],
                                        in1=mk_ap, op=ALU.mult)
                        vv.inc(V.tensor_tensor(out=pP[:, 2 + j % 2, :],
                                               in0=pP[:, 2 + j % 2, :],
                                               in1=mk_ap, op=ALU.mult), 1)
                        TE.wait_ge(vv.sem, vv.v)
                    else:
                        TE.wait_ge(ssm.sem, expc[(gi, kc, j)])

                    if kind == "loc":
                        vA = v_w[:, kt, hA, 0:65]
                        vB = v_w[:, kt, hB, 0:65]
                    else:
                        vA = vg_tm[:, kt, hA, 0:65]
                        vB = vg_tm[:, kt, hB, 0:65]
                    s0 = (j == 0)
                    obb = 4 + 2 * (kc % 2)
                    TE.matmul(ps[obb][0:65, 0:512], vA, pP[:, j % 2, :],
                              start=s0, stop=False)
                    tsm.inc(TE.matmul(ps[obb + 1][0:65, 0:512], vB,
                                      pP[:, 2 + j % 2, :],
                                      start=s0, stop=False), 1)
                    lastAV[(gi, kc, j)] = tsm.v
                AV_END = tsm.v

                V.wait_ge(tsm.sem, AV_END)
                obb = 4 + 2 * (kc % 2)
                if rec_dma[0] is not None:
                    V.wait_ge(dsy.sem, rec_dma[0])
                V.reciprocal(out=recrow[64:65, 0:512],
                             in_=ps[obb][64:65, 0:512])
                vv.inc(V.reciprocal(out=recrow[64:65, 512:1024],
                                    in_=ps[obb + 1][64:65, 0:512]), 1)
                SY.wait_ge(vv.sem, vv.v)
                SY.wait_ge(ggm.sem, ggm.v)
                dsy.inc(SY.dma_start(out=recrow[0:1, :],
                                     in_=recrow[64:65, :]), 16)
                rec_dma[0] = dsy.v
                GP.wait_ge(dsy.sem, dsy.v)
                GP.wait_ge(vv.sem, vv.v)
                ggm.inc(GP.partition_broadcast(out_ap=recrep[:],
                                               in_ap=recrow[0:1, :],
                                               channels=64), 1)
                V.wait_ge(ggm.sem, ggm.v)
                V.tensor_tensor(out=o_fm[0:64, kc, :],
                                in0=ps[obb][0:64, 0:512],
                                in1=recrep[:, 0:512], op=ALU.mult)
                if kc >= 1:
                    V.wait_ge(dsy.sem, ob_dma[kc - 1])
                vv.inc(V.tensor_tensor(out=otmp[0:64, :],
                                       in0=ps[obb + 1][0:64, 0:512],
                                       in1=recrep[:, 512:1024],
                                       op=ALU.mult), 1)
                SY.wait_ge(vv.sem, vv.v)
                dsy.inc(SY.dma_start(out=o_fm[64:128, kc, :],
                                     in_=otmp[0:64, :]), 16)
                ob_dma[kc] = dsy.v
                prev2_o_ev = prev_o_ev
                prev_o_ev = vv.v
            O_DONE = vv.v
            OB_DMA = dsy.v

            for t4 in range(4):
                tt = gi * 4 + t4
                slot = t4 % 2
                if t4 >= 2:
                    SY.wait_ge(vv.sem, r1_built[(gi, t4 - 2)])
                elif gi > 0:
                    SY.wait_ge(vv.sem, r1_built[(gi - 1, t4 + 2)])
                dsy.inc(SY.dma_start(out=xow2[:, slot, :],
                                     in_=xown_h[tt * 128:(tt + 1) * 128, :]), 16)
                X2LD = dsy.v

                TE.wait_ge(vv.sem, O_DONE)
                TE.wait_ge(dsy.sem, OB_DMA)
                if (gi, t4 - 1) in r1_built:
                    TE.wait_ge(vv.sem, r1_built[(gi, t4 - 1)])
                elif gi > 0 and t4 == 0:
                    TE.wait_ge(vv.sem, r1_built[(gi - 1, 3)])
                if t4 == 0:
                    TE.wait_ge(ssm.sem, ssm.v)
                i = None
                for dh in range(2):
                    for kc2 in range(DC):
                        i = TE.matmul(ps[0 + dh][:, 0:512],
                                      o_fm[:, kc2, t4 * 128:(t4 + 1) * 128],
                                      wos[:, kc2, dh * 512:(dh + 1) * 512],
                                      start=(kc2 == 0), stop=(kc2 == DC - 1))
                tsm.inc(i, 1)
                PRJ = tsm.v
                V.wait_ge(tsm.sem, PRJ)
                V.wait_ge(dsy.sem, X2LD)
                hbar(V.tensor_scalar(out=xow2[:, slot, :], in0=xow2[:, slot, :],
                                     scalar1=sown[:, 3, tt:tt + 1],
                                     scalar2=sown[:, 1, tt:tt + 1],
                                     op0=ALU.mult, op1=ALU.add))
                V.tensor_tensor(out=r1t[:, slot, 0:512], in0=xow2[:, slot, 0:512],
                                in1=ps[0][:, 0:512], op=ALU.add)
                vv.inc(V.tensor_tensor(out=r1t[:, slot, 512:1024],
                                       in0=xow2[:, slot, 512:1024],
                                       in1=ps[1][:, 0:512], op=ALU.add), 1)
                r1_built[(gi, t4)] = vv.v
                SY.wait_ge(vv.sem, vv.v)
                dsy.inc(SY.dma_start(out=r1_hbm[tt * 128:(tt + 1) * 128, :],
                                     in_=r1t[:, slot, :]), 16)
                r1_dma[tt] = dsy.v

    C_V = vv.v
    C_S = ssm.v
    C_T = tsm.v
    C_DY = dsy.v
    stBC.close()

    stD = ExitStack()
    sbd = mk(stD)
    wq1T = sbd("wq1T", [128, DC, FF], BF16)
    wq2T = sbd("wq2T", [128, FC, D], BF16)
    h1g = sbd("h1g", [128, FC, 256], BF16)
    x2nf = sbd("x2nf", [128, DC, 2, 256], BF16)
    x2nb = sbd("x2nb", [128, 2, D], BF16)
    r1ln = sbd("r1ln", [128, 2, D], F32)
    r1g = sbd("r1g", [128, 2, D], F32)

    SY.wait_ge(vv.sem, C_V)
    SY.wait_ge(ssm.sem, C_S)
    SY.wait_ge(tsm.sem, C_T)
    SY.wait_ge(dsc.sem, QSPILL)
    for dc in range(DC):
        dsy.inc(SY.dma_start(out=wq1T[:, dc, :],
                             in_=wq1_hbm[:, dc * 128:(dc + 1) * 128],
                             transpose=True), 16)
    for fc in range(FC):
        dsy.inc(SY.dma_start(out=wq2T[:, fc, :],
                             in_=wq2_hbm[:, fc * 128:(fc + 1) * 128],
                             transpose=True), 16)
    WQT = dsy.v

    x2n_dma = {}
    for tt in range(NT):
        slot = tt % 2
        SY.wait_ge(dsy.sem, r1_dma[tt])
        if tt >= 2:
            SY.wait_ge(dsy.sem, x2n_dma[tt - 2])
        dsy.inc(SY.dma_start(out=r1ln[:, slot, :],
                             in_=r1_hbm[tt * 128:(tt + 1) * 128, :]), 16)
        V.wait_ge(dsy.sem, dsy.v)
        emit_ln_tile(r1ln[:, slot, :], None, x2n_mode=True,
                     x2n_out=x2nb[:, slot, :])
        SY.wait_ge(vv.sem, vv.v)
        dsy.inc(SY.dma_start(out=x2n_hbm[tt * 128:(tt + 1) * 128, :],
                             in_=x2nb[:, slot, :]), 16)
        x2n_dma[tt] = dsy.v

    TE.wait_ge(ssm.sem, C_S)
    TE.wait_ge(vv.sem, C_V)
    TE.wait_ge(dsy.sem, WQT)
    h1_ev = {}
    ffn1_done = {}
    ffn2_ev = {}
    out_dma = {}
    r1g_ld = {}
    x2nf_ld = {}
    for gq in range(8):
        gslot = gq % 2
        SY.wait_ge(dsy.sem, x2n_dma[2 * gq + 1])
        if gq >= 2:
            SY.wait_ge(tsm.sem, ffn1_done[gq - 2])
        for dc in range(DC):
            dsy.inc(SY.dma_start(out=x2nf[:, dc, gslot, :],
                                 in_=x2n_hbm[gq * 256:(gq + 1) * 256,
                                             dc * 128:(dc + 1) * 128],
                                 transpose=True), 16)
        x2nf_ld[gq] = dsy.v
        if gq >= 1:
            SY.wait_ge(dsy.sem, out_dma[2 * gq - 1])
        dsy.inc(SY.dma_start(out=r1g[:].rearrange("p a d -> p (a d)"),
                             in_=r1_hbm[gq * 256:(gq + 1) * 256, :]), 16)
        r1g_ld[gq] = dsy.v

        TE.wait_ge(dsy.sem, x2nf_ld[gq])
        for fc in range(FC):
            bk = fc % 2
            if (gq, fc - 2) in h1_ev:
                TE.wait_ge(ssm.sem, h1_ev[(gq, fc - 2)])
            elif gq > 0 and fc < 2:
                TE.wait_ge(ssm.sem, h1_ev[(gq - 1, FC - 2 + fc)])
            i = None
            for dc in range(DC):
                i = TE.matmul(ps[bk][:, 0:256], wq1T[:, dc, fc * 128:(fc + 1) * 128],
                              x2nf[:, dc, gslot, :], start=(dc == 0),
                              stop=(dc == DC - 1))
            tsm.inc(i, 1)
            SC.wait_ge(tsm.sem, tsm.v)
            if gq >= 1:
                SC.wait_ge(tsm.sem, ffn2_mm.get((gq - 1, 1, 1), 0) or 0)
            ssm.inc(SC.activation(out=h1g[:, fc, :], in_=ps[bk][:, 0:256],
                                  func=AF.Relu, scale=1.0,
                                  bias=b1c[:, fc:fc + 1]), 1)
            h1_ev[(gq, fc)] = ssm.v
        ffn1_done[gq] = tsm.v
        H1 = ssm.v

        if gq == 0:
            ffn2_mm = {}
        TE.wait_ge(ssm.sem, H1)
        for t2 in range(2):
            tt = gq * 2 + t2
            if tt >= 2:
                TE.wait_ge(vv.sem, ffn2_ev[tt - 2])
            for dh in range(2):
                bk = 2 + 2 * t2 + dh
                i = None
                for fc in range(FC):
                    i = TE.matmul(ps[bk][:, 0:512],
                                  h1g[:, fc, t2 * 128:(t2 + 1) * 128],
                                  wq2T[:, fc, dh * 512:(dh + 1) * 512],
                                  start=(fc == 0), stop=False)
                i = TE.matmul(ps[bk][:, 0:512], ones_row[:, 0:128],
                              b2rb[:, dh * 512:(dh + 1) * 512], start=False,
                              stop=True)
                tsm.inc(i, 1)
                ffn2_mm[(gq, t2, dh)] = tsm.v
            FFN2 = tsm.v
            V.wait_ge(tsm.sem, FFN2)
            V.wait_ge(dsy.sem, r1g_ld[gq])
            if tt >= 1:
                V.wait_ge(dsy.sem, out_dma[tt - 1])
            V.tensor_tensor(out=r1g[:, t2, 0:512], in0=r1g[:, t2, 0:512],
                            in1=ps[2 + 2 * t2][:, 0:512], op=ALU.add)
            vv.inc(V.tensor_tensor(out=r1g[:, t2, 512:1024],
                                   in0=r1g[:, t2, 512:1024],
                                   in1=ps[2 + 2 * t2 + 1][:, 0:512], op=ALU.add), 1)
            ffn2_ev[tt] = vv.v
            SY.wait_ge(vv.sem, vv.v)
            dsy.inc(SY.dma_start(out=out_h[tt * 128:(tt + 1) * 128, :],
                                 in_=r1g[:, t2, :]), 16)
            out_dma[tt] = dsy.v

    FINAL = dsy.v
    SY.wait_ge(dsy.sem, FINAL)
    if DEBUG:
        GP.wait_ge(dsy.sem, FINAL)
        for dst, srcd in ((dbg_h, h_hbm), (dbg_srow, srow_hbm), (dbg_s16, s16_hbm),
                          (dbg_idx, idx_dram), (dbg_r1, r1_hbm), (dbg_x2n, x2n_hbm),
                          (dbg_wq1, wq1_hbm), (dbg_wq2, wq2_hbm), (dbg_hg, hg_hbm)):
            dgp.inc(GP.dma_start(out=dst, in_=srcd), 16)
        SY.wait_ge(dsy.sem, FINAL)
        dsy.inc(SY.dma_start(out=dbg_sml[:, 0:16], in_=negC[:]), 16)
        dsy.inc(SY.dma_start(out=dbg_sml[:, 16:18], in_=idxval[:]), 16)
        dsy.inc(SY.dma_start(out=dbg_sml[:, 18:82], in_=sc_tm[:]), 16)
        dsy.inc(SY.dma_start(out=dbg_sml[0:2, 82:90], in_=mqk[:, :, 2], allow_slow_non_contiguous=True), 16)
        dsy.inc(SY.dma_start(out=dbg_sml[0:1, 90:92], in_=thr[:]), 16)
        dsy.inc(SY.dma_start(out=dbg_sml[0:1, 92:93], in_=su11[:]), 16)
        dsy.inc(SY.dma_start(out=dbg_sml[0:1, 93:94], in_=wgam[0:1, 0:1]), 16)
        dsy.inc(SY.dma_start(out=dbg_sml[0:1, 94:95], in_=wgam[0:1, 1:2]), 16)
        dsy.inc(SY.dma_start(out=dbg_i16, in_=idx16[:]), 16)
        dsy.inc(SY.dma_start(out=dbg_wabs, in_=wabs[:]), 16)
        SY.wait_ge(dsy.sem, dsy.v)
        GP.wait_ge(dgp.sem, dgp.v)
    stD.close()

    with nc.Block() as blk:
        def run_engine(name, obj):
            for e, kind, payload in rec.ops:
                if e != name:
                    continue
                if kind == "wait":
                    sem, val = payload
                    obj.wait_ge(sem, val)
                else:
                    meth, args, kwargs, ri = payload
                    inst = getattr(obj, meth)(*args, **kwargs)
                    for sem, n in ri.incs:
                        inst.then_inc(sem, n)

        @blk.sync
        def _(sy):
            run_engine("sync", sy)

        @blk.vector
        def _(v):
            run_engine("vector", v)

        @blk.scalar
        def _(sc):
            run_engine("scalar", sc)

        @blk.tensor
        def _(te):
            run_engine("tensor", te)

        @blk.gpsimd
        def _(g):
            run_engine("gpsimd", g)

    st.close()
    nc.compile()
    return nc


_NC = None


def kernel(L0_attn_x, L0_attn_norm_w, L0_attn_w_q, L0_attn_w_k, L0_attn_w_v,
           L0_attn_w_out, L0_attn_scorer_w, L0_attn_scorer_b, L0_ffn_W1,
           L0_ffn_b1, L0_ffn_W2, L0_ffn_b2):
    global _NC
    if _NC is None:
        _NC = build()
    nc = _NC

    x = np.ascontiguousarray(np.asarray(L0_attn_x, dtype=np.float32))
    nw = np.asarray(L0_attn_norm_w, dtype=np.float32).reshape(D)
    sw = np.asarray(L0_attn_scorer_w, dtype=np.float32).reshape(D)
    wq = np.ascontiguousarray(np.asarray(L0_attn_w_q, dtype=np.float32))
    wk = np.ascontiguousarray(np.asarray(L0_attn_w_k, dtype=np.float32))
    wv = np.ascontiguousarray(np.asarray(L0_attn_w_v, dtype=np.float32))
    wo = np.ascontiguousarray(np.asarray(L0_attn_w_out, dtype=np.float32))
    w1 = np.ascontiguousarray(np.asarray(L0_ffn_W1, dtype=np.float32))
    w2 = np.ascontiguousarray(np.asarray(L0_ffn_W2, dtype=np.float32))
    b1 = np.asarray(L0_ffn_b1, dtype=np.float32).reshape(FF)
    b2 = np.asarray(L0_ffn_b2, dtype=np.float32).reshape(D)

    triu = np.triu(np.ones((128, 128), dtype=np.float32)).astype(ml_dtypes.bfloat16)
    iota256 = np.tile(np.arange(256, dtype=np.float32)[None, :], (128, 1))
    qposr = np.tile(np.arange(512, dtype=np.float32)[None, :], (128, 1))
    cmm = np.zeros((128, 4, 512), dtype=np.float32)
    pp = np.arange(128)[:, None]
    ff = np.arange(512)[None, :]
    for dd in range(4):
        cmm[:, dd, :] = (ff >= pp + 128 * dd).astype(np.float32)
    cmm = cmm.astype(ml_dtypes.bfloat16)
    nwrep = np.tile(nw[None, :], (128, 1))
    nw8 = nw.reshape(DC, 128).T.copy()
    sw8 = sw.reshape(DC, 128).T.copy()
    b1c = b1.reshape(FC, 128).T.copy()
    b2r = b2.reshape(1, D)

    in_maps = []
    for c in range(8):
        b_ = c // 4
        o_ = (c % 4) * SO
        in_maps.append({
            "xb": x[b_], "xown": np.ascontiguousarray(x[b_, o_:o_ + SO]),
            "wq": wq, "wk": wk, "wv": wv, "wo": wo,
            "w1": w1, "w2": w2, "b1c": b1c, "b2r": b2r,
            "nwrep": nwrep, "nw8": nw8, "sw8": sw8,
            "offv": np.array([[float(o_)]], dtype=np.float32),
            "triu": triu, "iota256": iota256, "qposr": qposr, "cm": cmm,
        })
    res = run_bass_kernel_spmd(nc, in_maps, core_ids=list(range(8)))
    full = np.empty((B, S, D), dtype=np.float32)
    for c in range(8):
        b_ = c // 4
        o_ = (c % 4) * SO
        full[b_, o_:o_ + SO] = res.results[c]["out"]
    return full
